# version 14
# baseline (speedup 1.0000x reference)
import numpy as np
import ml_dtypes
from contextlib import ExitStack

import concourse.bass as bass
import concourse.mybir as mybir
from concourse.bass_utils import run_bass_kernel_spmd
from bass_rust import IndirectOffsetOnAxis

F32 = mybir.dt.float32
BF16 = mybir.dt.bfloat16
U32 = mybir.dt.uint32
AF = mybir.ActivationFunctionType
ALU = mybir.AluOpType
AX = mybir.AxisListType

B = 4
N = 2048
DM = 1024
H = 16
PAIRS = 8
HD = 64
MLM = 64
SEG = 32
ROWS = B * N
NCH = ROWS // 128
E = 8
FF = 4096
CAP = 4096
NSC = CAP // 256
NFC = FF // 128
EPS = 1e-5

ENGS = ("sync", "tensor", "vector", "scalar", "gpsimd")


class Sem:
    def __init__(self, stack, nc, name):
        self.h = stack.enter_context(nc.semaphore(name))
        self.v = 0

    def bump(self, k=1):
        self.v += k
        return self.v


class MK:
    def __init__(self, nc, stack, debug_stage=None):
        self.nc = nc
        self.stack = stack
        self.debug_stage = debug_stage
        self.q = None

    def new_block(self):
        self.q = {k: [] for k in ENGS}

    def emit(self, eng, fn):
        self.q[eng].append(fn)

    def run_block(self):
        q = self.q
        with self.nc.Block() as blk:
            for eng in ENGS:
                fns = q[eng]
                if not fns:
                    continue
                def body(e, fns=fns):
                    for fn in fns:
                        fn(e)
                getattr(blk, eng)(body)
        self.q = None

    _uid = 0

    def uname(self, name):
        MK._uid += 1
        return f"{name}_u{MK._uid}"

    def sb(self, name, shape, dt):
        return self.stack.enter_context(self.nc.sbuf_tensor(name, shape, dt))

    def ps(self, name, shape, dt):
        return self.stack.enter_context(self.nc.psum_tensor(name, shape, dt))

    def sem(self, name):
        return Sem(self.stack, self.nc, name)


def build_program(debug_stage=None):
    nc = bass.Bass("TRN2", target_bir_lowering=False, detect_race_conditions=False)

    x_d = nc.dram_tensor("x", [ROWS, DM], F32, kind="ExternalInput").ap()
    wqkv_d = nc.dram_tensor("wqkv", [DM, 3 * DM], F32, kind="ExternalInput").ap()
    wproj_d = nc.dram_tensor("wproj", [DM, DM], F32, kind="ExternalInput").ap()
    cent_d = nc.dram_tensor("cent", [E, DM], F32, kind="ExternalInput").ap()
    lng_d = nc.dram_tensor("lng", [E, DM], F32, kind="ExternalInput").ap()
    lnb_d = nc.dram_tensor("lnb", [E, DM], F32, kind="ExternalInput").ap()
    w1_d = nc.dram_tensor("w1", [E, FF, DM], F32, kind="ExternalInput").ap()
    b1_d = nc.dram_tensor("b1", [E, FF], F32, kind="ExternalInput").ap()
    w2_d = nc.dram_tensor("w2", [E, DM, FF], F32, kind="ExternalInput").ap()
    b2_d = nc.dram_tensor("b2", [E, DM], F32, kind="ExternalInput").ap()
    out_d = nc.dram_tensor("out", [ROWS, DM], F32, kind="ExternalOutput").ap()

    dbg_d = None
    if debug_stage in ("p2abd",):
        dbg_d = nc.dram_tensor("dbg", [128, 4, 2048], BF16, kind="ExternalOutput").ap()
    x1_kind = "ExternalOutput" if debug_stage == "attn" else "Internal"
    x1s = nc.dram_tensor("x1s", [ROWS, DM], F32, kind=x1_kind).ap()
    znbkt = nc.dram_tensor("znbkt", [E * CAP, DM], BF16, kind="Internal").ap()
    rmtab = nc.dram_tensor("rmtab", [E * CAP + 128, 4], F32, kind="Internal").ap()
    if debug_stage == "dump":
        d_zn = nc.dram_tensor("d_zn", [E * CAP, DM], BF16, kind="ExternalOutput").ap()
        d_o = nc.dram_tensor("d_o", [128, 2, NCH], U32, kind="ExternalOutput").ap()
        d_rk = nc.dram_tensor("d_rk", [128, 4, NCH], F32, kind="ExternalOutput").ap()
    if debug_stage == "epi":
        d_oq = nc.dram_tensor("d_oq", [128, 2, DM], F32, kind="ExternalOutput").ap()
        d_fq = nc.dram_tensor("d_fq", [128, 2, DM], F32, kind="ExternalOutput").ap()
        d_xg = nc.dram_tensor("d_xg", [128, 2, DM], F32, kind="ExternalOutput").ap()
        d_ts = nc.dram_tensor("d_ts", [128, 2, 2, DM], BF16, kind="ExternalOutput").ap()
        d_rm2 = nc.dram_tensor("d_rm2", [128, 2, 2, 4], F32, kind="ExternalOutput").ap()
        d_al2 = nc.dram_tensor("d_al2", [128, 4], F32, kind="ExternalOutput").ap()
        d_ix = nc.dram_tensor("d_ix", [128, 4], U32, kind="ExternalOutput").ap()
        d_mv = nc.dram_tensor("d_mv", [128, 2, 2, 2], F32, kind="ExternalOutput").ap()
        d_zl = nc.dram_tensor("d_zl", [128, 3, 2, DM], BF16, kind="ExternalOutput").ap()
    w1b = nc.dram_tensor("w1b", [E, FF, DM], BF16, kind="Internal").ap()
    w2b = nc.dram_tensor("w2b", [E, DM, FF], BF16, kind="Internal").ap()

    with ExitStack() as stack:
        k = MK(nc, stack, debug_stage)
        sb, ps, sem = k.sb, k.ps, k.sem

        id128 = sb("id128", [128, 128], BF16)
        id64s = sb("id64s", [128, 64], BF16)
        id128f = sb("id128f", [128, 128], F32)
        ltu = sb("ltu", [128, 128], F32)
        ones128f = sb("ones128f", [128, 1], F32)
        iota8 = sb("iota8", [128, 8], F32)
        ones128 = sb("ones128", [128, 1], BF16)
        ones1r = sb("ones1r", [1, 128], BF16)
        ones1f = sb("ones1f", [1, 128], F32)
        sel0 = sb("sel0", [1, 128], F32)
        sel1 = sb("sel1", [1, 128], F32)
        cIs = sb("cIs", [128, 3, 64], F32)
        CT = sb("CT", [128, 8, E], F32)
        GT = sb("GT", [128, 8, E], F32)
        BT = sb("BT", [128, 8, E], F32)
        B1T = sb("B1T", [128, NFC, E], F32)
        dbg_rk = sb("dbg_rk", [128, 4, NCH], F32)
        o_all = sb("o_all", [128, NCH], U32)
        rmsent = sb("rmsent", [128, 4], F32)
        riota = sb("riota", [128, NCH], F32)
        Rrow = sb("Rrow", [1, E], F32)
        nse_f = sb("nse_f", [1, E], F32)
        nse_i = sb("nse_i", [1, E], mybir.dt.int32)
        epsc = sb("epsc", [128, 1], F32)
        zot = sb("zot", [128, DM], BF16)

        pA = ps("pA", [128, 512], F32)
        pB = ps("pB", [128, 512], F32)
        pC = ps("pC", [128, 512], F32)
        pD = ps("pD", [128, 512], F32)
        pE_ = ps("pE", [128, 512], F32)
        pF_ = ps("pF", [128, 512], F32)
        pTrb = ps("pTrb", [128, 8, 128], BF16)
        pTr2 = ps("pTr2", [128, 8, 128], BF16)

        s_x = sem("s_x")
        s_st = sem("s_st")
        s_gq = sem("s_gq")
        s_wp = sem("s_wp")
        s_ln = sem("s_ln")
        s_sq = sem("s_sq")
        s_tp = sem("s_tp")
        s_ev = sem("s_ev")
        s_ev2 = sem("s_ev2")
        s_mm = sem("s_mm")
        s_nt = sem("s_nt")
        s_nv = sem("s_nv")
        s_gp = sem("s_gp")
        s_rt = sem("s_rt")
        s_w = sem("s_w")
        s_zn = sem("s_zn")
        s_trg = sem("s_trg")
        s_znT = sem("s_znT")
        s_w1 = sem("s_w1")
        s_he = sem("s_he")
        s_w2 = sem("s_w2")
        s_te = sem("s_te")
        s_sc = sem("s_sc")

        wprep_units = [("w1", e) for e in range(E)] + [("w2", e) for e in range(E)]

        def emit_wprep(units):
            def body(g, units=units):
                for kind, e in units:
                    if kind == "w1":
                        g.dma_start(w1b[e, :, :], w1_d[e, :, :]).then_inc(s_wp.h, 16)
                    else:
                        g.dma_start(w2b[e, :, :], w2_d[e, :, :]).then_inc(s_wp.h, 16)
            s_wp.bump(16 * len(units))
            return body

        p0stk = ExitStack()
        rowbuf = p0stk.enter_context(nc.sbuf_tensor("rowbuf", [1, 4 * DM], F32))
        k.new_block()

        def p0_g(g):
            def ident_into(t, val=1.0, halfsize=None):
                g.memset(t, 0.0)
                if halfsize is None:
                    sq = t.shape[-1]
                    g.affine_select(out=t, in_=t, compare_op=ALU.not_equal, fill=val,
                                    base=0, pattern=[[-1, sq]], channel_multiplier=1)
            ident_into(id128[:, :])
            ident_into(id128f[:, :])
            g.memset(id64s[:, :], 0.0)
            for hh in range(2):
                g.affine_select(out=id64s[hh * 64:(hh + 1) * 64, :],
                                in_=id64s[hh * 64:(hh + 1) * 64, :],
                                compare_op=ALU.not_equal, fill=1.0, base=0,
                                pattern=[[-1, 64]], channel_multiplier=1)
            g.memset(ltu[:, :], 1.0)
            g.affine_select(out=ltu[:, :], in_=ltu[:, :], compare_op=ALU.is_ge,
                            fill=0.0, base=-1, pattern=[[1, 128]], channel_multiplier=-1)
            g.iota(iota8[:, :], [[1, 8]], base=0, channel_multiplier=0,
                   allow_small_or_imprecise_dtypes=True)
            g.memset(rmsent[:, :], 1.0e9)
            g.iota(riota[:, :], [[128, NCH]], base=0, channel_multiplier=1,
                   allow_small_or_imprecise_dtypes=True)
            g.memset(ones128[:, :], 1.0)
            g.memset(ones128f[:, :], 1.0)
            g.memset(ones1r[:, :], 1.0)
            g.memset(ones1f[:, :], 1.0)
            g.memset(sel0[:, :], 1.0)
            g.affine_select(out=sel0[:, :], in_=sel0[:, :], compare_op=ALU.is_ge,
                            fill=0.0, base=63, pattern=[[-1, 128]], channel_multiplier=0)
            g.memset(sel1[:, :], 1.0)
            g.affine_select(out=sel1[:, :], in_=sel1[:, :], compare_op=ALU.is_ge,
                            fill=0.0, base=-64, pattern=[[1, 128]], channel_multiplier=0)
            g.memset(cIs[:, :, :], 0.0)
            for j, c in enumerate([7.0, 15.0, 13.0]):
                for hh in range(2):
                    g.affine_select(out=cIs[hh * 64:(hh + 1) * 64, j, :],
                                    in_=cIs[hh * 64:(hh + 1) * 64, j, :],
                                    compare_op=ALU.not_equal, fill=c, base=0,
                                    pattern=[[-1, 64]], channel_multiplier=1)
            g.memset(Rrow[:, :], 0.0)
            g.memset(epsc[:, :], EPS)
            g.memset(zot[:, :], 0.0)
            g.nop().then_inc(s_gp.h, 1)
        P0G = s_gp.bump()
        k.emit("gpsimd", p0_g)

        p0_plan = []
        ev_ms = []
        for e in range(E):
            p0_plan.append(("4rows", e))
        for seg in range(1, 4):
            for e in range(E):
                p0_plan.append(("b1seg", e, seg))

        def p0_sync(sync):
            for it, item in enumerate(p0_plan):
                if it >= 1:
                    sync.wait_ge(s_ev.h, it)
                if item[0] == "4rows":
                    e = item[1]
                    sync.dma_start(rowbuf[0:1, 0:DM], cent_d[e:e + 1, :]).then_inc(s_x.h, 16)
                    sync.dma_start(rowbuf[0:1, DM:2 * DM], lng_d[e:e + 1, :]).then_inc(s_x.h, 16)
                    sync.dma_start(rowbuf[0:1, 2 * DM:3 * DM], lnb_d[e:e + 1, :]).then_inc(s_x.h, 16)
                    sync.dma_start(rowbuf[0:1, 3 * DM:4 * DM], b1_d[e:e + 1, 0:1024]).then_inc(s_x.h, 16)
                else:
                    e, seg = item[1], item[2]
                    sync.dma_start(rowbuf[0:1, 0:DM],
                                   b1_d[e:e + 1, seg * 1024:(seg + 1) * 1024]).then_inc(s_x.h, 16)
        nload = 0
        p0_loadcnt = []
        for item in p0_plan:
            nload += 4 if item[0] == "4rows" else 1
            p0_loadcnt.append(nload)
        s_x.bump(16 * nload)

        def p0_t(t):
            for it, item in enumerate(p0_plan):
                t.wait_ge(s_x.h, 16 * p0_loadcnt[it])
                if it >= 2:
                    t.wait_ge(s_ev.h, it - 1)
                pX = pA if it % 2 == 0 else pB
                nr = 4 if item[0] == "4rows" else 1
                for c in range(8):
                    for r in range(nr):
                        mm = t.matmul(pX[:, 4 * c + r:4 * c + r + 1],
                                      rowbuf[0:1, r * DM + c * 128:r * DM + (c + 1) * 128],
                                      ones1f[0:1, 0:1], start=True, stop=True)
                mm.then_inc(s_mm.h, 1)
        p0_mm0 = s_mm.v
        s_mm.bump(len(p0_plan))

        def p0_v(v):
            for it, item in enumerate(p0_plan):
                v.wait_ge(s_mm.h, p0_mm0 + it + 1)
                pX = pA if it % 2 == 0 else pB
                if item[0] == "4rows":
                    e = item[1]
                    for c in range(8):
                        v.tensor_copy(CT[:, c, e:e + 1], pX[:, 4 * c:4 * c + 1])
                        v.tensor_copy(GT[:, c, e:e + 1], pX[:, 4 * c + 1:4 * c + 2])
                        v.tensor_copy(BT[:, c, e:e + 1], pX[:, 4 * c + 2:4 * c + 3])
                        op = v.tensor_copy(B1T[:, c, e:e + 1], pX[:, 4 * c + 3:4 * c + 4])
                    op.then_inc(s_ev.h, 1)
                else:
                    e, seg = item[1], item[2]
                    for c in range(8):
                        op = v.tensor_copy(B1T[:, seg * 8 + c, e:e + 1],
                                           pX[:, 4 * c:4 * c + 1])
                    op.then_inc(s_ev.h, 1)
        s_ev.bump(len(p0_plan))

        k.emit("sync", p0_sync)
        k.emit("tensor", p0_t)
        k.emit("vector", p0_v)
        k.run_block()
        p0stk.close()

        if debug_stage == "p0":
            return nc

        wprep_pos = 0
        CTbf = None

        nbatch = 1 if debug_stage in ("p2ab", "p2abd", "p2c", "attn1") else B
        for b in range(nbatch):
            with ExitStack() as bstk:
                bsb = lambda n, s, d: bstk.enter_context(nc.sbuf_tensor(k.uname(n), s, d))
                QKT = bsb("QKT", [128, 16, N], BF16)
                Vrm = bsb("Vrm", [128, 16, DM], BF16)
                xnlT = bsb("xnlT", [128, 8, MLM], BF16)
                QlKlT = bsb("QlKlT", [128, 16, MLM], BF16)

                with ExitStack() as astk:
                    asb = lambda n, s, d: astk.enter_context(nc.sbuf_tensor(k.uname(n), s, d))
                    xnT = asb("xnT", [128, 8, N], BF16)
                    wqkvb = asb("wqkvb", [128, 8, 3 * DM], BF16)
                    xc = asb("xc", [128, 2, DM], F32)
                    xnrm = asb("xnrm", [128, 2, DM], BF16)
                    stat = asb("stat", [128, 16], F32)
                    mv2 = asb("mv2", [128, 2], F32)
                    rstd = asb("rstd", [128, 1], F32)
                    sjunk = asb("sjunk", [128, 4], F32)
                    lred = asb("lred", [128, MLM], F32)

                    k.new_block()
                    xrow0 = b * N

                    gq0 = s_gq.v
                    def a_gq(g, gq0=gq0):
                        for d in range(8):
                            g.dma_start(wqkvb[:, d, :],
                                        wqkv_d[d * 128:(d + 1) * 128, :]).then_inc(s_gq.h, 16)
                    s_gq.bump(16 * 8)
                    k.emit("gpsimd", a_gq)

                    ln_stats = {}
                    ln_apply = {}
                    tp_ms = {}
                    evV = {}
                    evS = {}
                    for i in range(16):
                        x_at = s_x.v + 16

                        def c_sync(sync, i=i, xrow0=xrow0):
                            if i >= 2:
                                sync.wait_ge(s_ln.h, ln_apply[i - 2])
                            sync.dma_start(
                                xc[:, i % 2, :],
                                x_d[xrow0 + 128 * i:xrow0 + 128 * (i + 1), :]).then_inc(s_x.h, 16)
                        s_x.bump(16)
                        k.emit("sync", c_sync)

                        e1_, e2_ = s_ln.bump(), s_ln.bump()
                        def c_stats(v, i=i, x_at=x_at, e1_=e1_, e2_=e2_):
                            v.wait_ge(s_x.h, x_at)
                            v.bn_stats(stat[:, 0:6], xc[:, i % 2, 0:512]).then_inc(s_ln.h, 1)
                            v.wait_ge(s_ln.h, e1_)
                            v.bn_stats(stat[:, 6:12], xc[:, i % 2, 512:1024]).then_inc(s_ln.h, 1)
                            v.wait_ge(s_ln.h, e2_)
                            v.bn_aggr(mv2[:, :], stat[:, 0:12]).then_inc(s_ln.h, 1)
                        ln_stats[i] = s_ln.bump()
                        k.emit("vector", c_stats)

                        def c_sqrt(sc, i=i):
                            sc.wait_ge(s_ln.h, ln_stats[i])
                            sc.activation(sjunk[:, 0:1], mv2[:, 1:2], AF.Sqrt,
                                          bias=epsc[:, 0:1], scale=1.0).then_inc(s_sq.h, 1)
                        sq_i = s_sq.bump()
                        k.emit("scalar", c_sqrt)

                        er_ = s_ln.bump()
                        def c_apply(v, i=i, sq_i=sq_i, er_=er_):
                            v.wait_ge(s_sq.h, sq_i)
                            if i >= 2:
                                v.wait_ge(s_tp.h, tp_ms[i - 2])
                            v.reciprocal(rstd[:, :], sjunk[:, 0:1]).then_inc(s_ln.h, 1)
                            v.wait_ge(s_ln.h, er_)
                            v.tensor_scalar(xnrm[:, i % 2, :], xc[:, i % 2, :],
                                            mv2[:, 0:1], rstd[:, :],
                                            op0=ALU.subtract, op1=ALU.mult
                                            ).then_inc(s_ln.h, 1)
                        ln_apply[i] = s_ln.bump()
                        k.emit("vector", c_apply)

                        def c_tr(t, i=i):
                            t.wait_ge(s_ln.h, ln_apply[i])
                            if i >= 2:
                                t.wait_ge(s_ev.h, evV[i - 2])
                            pT = pTrb if i % 2 == 0 else pTr2
                            for d in range(8):
                                mm = t.transpose(pT[:, d, :],
                                                 xnrm[:, i % 2, d * 128:(d + 1) * 128],
                                                 id128[:, :])
                            mm.then_inc(s_tp.h, 1)
                        tp_ms[i] = s_tp.bump()
                        k.emit("tensor", c_tr)

                        def c_evV(v, i=i):
                            v.wait_ge(s_tp.h, tp_ms[i])
                            pT = pTrb if i % 2 == 0 else pTr2
                            for d in range(8):
                                mm = v.tensor_copy(xnT[:, d, i * 128:(i + 1) * 128],
                                                   pT[:, d, :])
                            mm.then_inc(s_ev.h, 1)
                        evV[i] = s_ev.bump()
                        k.emit("vector", c_evV)
                        evS[i] = s_ev2.v

                    if b == 0:
                        def zfill(sync):
                            for cz in range(E * CAP // 128):
                                sync.dma_start(
                                    znbkt[128 * cz:128 * (cz + 1), :],
                                    zot[:, :]).then_inc(s_st.h, 16)

                        globals()
                        k.emit("sync", zfill)
                        ZFILL_AT = s_st.bump(16 * (E * CAP // 128))
                    else:
                        nrmf = (E * CAP + 128) // 128
                        lo_f = ((b - 1) * nrmf + 2) // 3
                        hi_f = (b * nrmf + 2) // 3
                        def rmfill(sync, lo_f=lo_f, hi_f=hi_f):
                            for cz in range(lo_f, hi_f):
                                sync.dma_start(
                                    rmtab[128 * cz:128 * (cz + 1), :],
                                    rmsent[:, :]).then_inc(s_st.h, 16)
                        k.emit("sync", rmfill)
                        s_st.bump(16 * (hi_f - lo_f))

                    lm_pieces = []
                    for d in range(8):
                        el_ = None
                        def c_lm(v, d=d, el_=None):
                            if d == 0:
                                v.wait_ge(s_ev.h, evV[15])
                            v.tensor_reduce(lred[:, :],
                                            xnT[:, d, :].rearrange("p (m s) -> p m s", m=MLM),
                                            axis=AX.X, op=ALU.add).then_inc(s_ln.h, 1)
                            v.wait_ge(s_ln.h, el_)
                            v.tensor_scalar(xnlT[:, d, :], lred[:, :], 1.0 / SEG, None,
                                            op0=ALU.mult).then_inc(s_ln.h, 1)
                        lm_pieces.append(c_lm)
                    lm_ms = []

                    qk_mm = {}
                    qk_ev = {}
                    wq_at = s_gq.v

                    for nh in range(4):
                        for cc in range(16):
                            gidx = nh * 16 + cc

                            def qk_t(t, nh=nh, cc=cc, gidx=gidx, wq_at=wq_at):
                                if gidx == 0:
                                    t.wait_ge(s_gq.h, wq_at)
                                if cc == 0:
                                    for j in range(4):
                                        t.wait_ge(s_ev.h, evV[4 * nh + j])
                                if gidx >= 2:
                                    t.wait_ge(s_ev.h, qk_ev[gidx - 2])
                                pQ = pA if gidx % 2 == 0 else pB
                                for d in range(8):
                                    mm = t.matmul(pQ[:, :],
                                                  wqkvb[:, d, cc * 128:(cc + 1) * 128],
                                                  xnT[:, d, nh * 512:(nh + 1) * 512],
                                                  start=(d == 0), stop=(d == 7))
                                mm.then_inc(s_mm.h, 1)
                            qk_mm[gidx] = s_mm.bump()
                            k.emit("tensor", qk_t)

                            def qk_e(v, nh=nh, cc=cc, gidx=gidx):
                                v.wait_ge(s_mm.h, qk_mm[gidx])
                                pQ = pA if gidx % 2 == 0 else pB
                                scale = 0.125 if cc < 8 else 1.0
                                v.tensor_scalar(QKT[:, cc, nh * 512:(nh + 1) * 512],
                                                pQ[:, :], scale, None,
                                                op0=ALU.mult).then_inc(s_ev.h, 1)
                            qk_ev[gidx] = s_ev.bump()
                            k.emit("vector", qk_e)
                            if 8 <= gidx < 16:
                                el_ = s_ln.bump()
                                fn = lm_pieces[gidx - 8]
                                k.emit("vector",
                                       lambda v, fn=fn, el_=el_: fn(v, el_=el_))
                                lm_ms.append(s_ln.bump())

                    v_mm = {}
                    v_ev = {}
                    for i in range(16):
                        for dh in range(2):
                            vc = i * 2 + dh

                            def v_t(t, i=i, dh=dh, vc=vc):
                                if vc >= 2:
                                    t.wait_ge(s_ev2.h, v_ev[vc - 2])
                                pV = pC if vc % 2 == 0 else pD
                                for d in range(8):
                                    mm = t.matmul(pV[:, :], xnT[:, d, i * 128:(i + 1) * 128],
                                                  wqkvb[:, d, 2048 + dh * 512:2048 + (dh + 1) * 512],
                                                  start=(d == 0), stop=(d == 7))
                                mm.then_inc(s_mm.h, 1)
                            v_mm[vc] = s_mm.bump()
                            k.emit("tensor", v_t)

                            def v_e(sc, i=i, dh=dh, vc=vc):
                                sc.wait_ge(s_mm.h, v_mm[vc])
                                pV = pC if vc % 2 == 0 else pD
                                sc.copy(Vrm[:, i, dh * 512:(dh + 1) * 512],
                                        pV[:, :]).then_inc(s_ev2.h, 1)
                            v_ev[vc] = s_ev2.bump()
                            k.emit("scalar", v_e)

                    ql_mm = {}
                    ql_ev = {}
                    for cc in range(16):
                        def ql_t(t, cc=cc):
                            if cc == 0:
                                t.wait_ge(s_ln.h, lm_ms[-1])
                            if cc >= 2:
                                t.wait_ge(s_ev.h, ql_ev[cc - 2])
                            pQ = pE_ if cc % 2 == 0 else pF_
                            for d in range(8):
                                mm = t.matmul(pQ[:, 0:MLM],
                                              wqkvb[:, d, cc * 128:(cc + 1) * 128],
                                              xnlT[:, d, :], start=(d == 0), stop=(d == 7))
                            mm.then_inc(s_mm.h, 1)
                        ql_mm[cc] = s_mm.bump()
                        k.emit("tensor", ql_t)

                        def ql_e(v, cc=cc):
                            v.wait_ge(s_mm.h, ql_mm[cc])
                            pQ = pE_ if cc % 2 == 0 else pF_
                            scale = 0.125 if cc < 8 else 1.0
                            v.tensor_scalar(QlKlT[:, cc, :], pQ[:, 0:MLM], scale, None,
                                            op0=ALU.mult).then_inc(s_ev.h, 1)
                        ql_ev[cc] = s_ev.bump()
                        k.emit("vector", ql_e)

                    nun = 4
                    k.emit("gpsimd", emit_wprep(wprep_units[wprep_pos:wprep_pos + nun]))
                    wprep_pos += nun

                    k.run_block()

                if debug_stage == "p2ab":
                    return nc
                if debug_stage == "p2abd":
                    k.new_block()
                    def dump(sync):
                        sync.dma_start(dbg_d[:, 0, :], xnT[:, 0, :]).then_inc(s_st.h, 16)
                        sync.dma_start(dbg_d[:, 1, :], QKT[:, 0, :]).then_inc(s_st.h, 16)
                        sync.dma_start(dbg_d[:, 2, :], QKT[:, 8, :]).then_inc(s_st.h, 16)
                        sync.dma_start(dbg_d[:, 3, 0:1024], Vrm[:, 0, :]).then_inc(s_st.h, 16)
                        sync.dma_start(dbg_d[:, 3, 1024:1088], QlKlT[:, 0, :]).then_inc(s_st.h, 16)
                        sync.dma_start(dbg_d[:, 3, 1088:1152], xnlT[:, 0, :]).then_inc(s_st.h, 16)
                        sync.wait_ge(s_st.h, s_st_target)
                    s_st_target = s_st.bump(16 * 6)
                    k.emit("sync", dump)
                    k.run_block()
                    return nc

                with ExitStack() as pstk:
                    psb = lambda n, s, d: pstk.enter_context(nc.sbuf_tensor(k.uname(n), s, d))
                    SVT = psb("SVT", [128, 8, N], BF16)
                    E1T = psb("E1T", [128, 2, N], BF16)
                    E3T = psb("E3T", [128, 2, 16, 128], BF16)
                    e1f = psb("e1f", [128, 2, 512], F32)
                    e1n = psb("e1n", [128, 2, 512], BF16)
                    k2e = psb("k2e", [128, 64], F32)
                    k2s = psb("k2s", [128, 1], F32)
                    k2n = psb("k2n", [128, 64], BF16)
                    k2T_st = psb("k2T_st", [128, 64], BF16)
                    k2T_bd = psb("k2T_bd", [128, 128], BF16)
                    Kl_bd = psb("Kl_bd", [128, 128], BF16)
                    Ql_bd = psb("Ql_bd", [128, 128], BF16)
                    V_st = psb("V_st", [128, 64], BF16)
                    V_bd = psb("V_bd", [128, 128], BF16)
                    VT_bd = psb("VT_bd", [128, 128], BF16)
                    KVT_bd = psb("KVT_bd", [128, 128], BF16)
                    Z_bd = psb("Z_bd", [128, 128], BF16)
                    Zst = psb("Zst", [128, 64], BF16)
                    T_st = psb("T_st", [128, 64], BF16)
                    AVn = psb("AVn", [128, 64], BF16)
                    rdr = psb("rdr", [1, 2], F32)
                    scol = psb("scol", [128, 1], F32)
                    rds = psb("rds", [128, 1], F32)
                    s1r = psb("s1r", [128, 8], F32)
                    rs3 = psb("rs3", [128, 1], F32)
                    nmax = psb("nmax", [128, 1], F32)

                    k.new_block()

                    def pz(g):
                        for tile in (k2T_bd, Kl_bd, Ql_bd, V_bd, VT_bd, KVT_bd, Z_bd):
                            g.memset(tile[:, :], 0.0)
                        g.nop().then_inc(s_gp.h, 1)
                    bd_zero = s_gp.bump()
                    k.emit("gpsimd", pz)

                    prev_pf_free = None
                    prev_sv = {}
                    prev_av = {}
                    prev_tr2_ok = None
                    prev_kg_last = None
                    prev_k3e = None
                    prev_k1T = None
                    prev_svev = None
                    avn_prev = {}

                    sub = debug_stage if isinstance(debug_stage, str) else ""
                    npairs = 1 if sub.startswith("p2c") and sub != "p2c" else PAIRS
                    for pr in range(npairs):
                        par = pr % 2
                        def c0(g, pr=pr, gate=prev_kg_last):
                            if pr == 0:
                                g.wait_ge(s_gp.h, bd_zero)
                            if gate is not None:
                                g.wait_ge(s_mm.h, gate)
                            g.tensor_copy(Kl_bd[0:64, 0:64], QlKlT[0:64, 8 + pr, :])
                            g.tensor_copy(Kl_bd[64:128, 64:128], QlKlT[64:128, 8 + pr, :])
                            g.tensor_copy(Ql_bd[0:64, 0:64], QlKlT[0:64, pr, :])
                            g.tensor_copy(Ql_bd[64:128, 64:128],
                                          QlKlT[64:128, pr, :]).then_inc(s_gp.h, 1)
                        klbd_ok = s_gp.bump()
                        k.emit("gpsimd", c0)
                        if sub == "p2c_0":
                            break

                        def c1(t, pr=pr, gate=prev_pf_free):
                            if gate is not None:
                                t.wait_ge(s_nv.h, gate)
                            for hh in range(2):
                                mm = t.matmul(pF_[hh * 64:(hh + 1) * 64, 0:64],
                                              QlKlT[hh * 64:(hh + 1) * 64, pr, :],
                                              QlKlT[hh * 64:(hh + 1) * 64, 8 + pr, :],
                                              start=True, stop=True)
                            mm.then_inc(s_mm.h, 1)
                        l2_ms = s_mm.bump()
                        k.emit("tensor", c1)
                        if sub == "p2c_1":
                            break

                        def c2(v, l2_ms=l2_ms):
                            v.wait_ge(s_mm.h, l2_ms)
                            v.reduce_max(nmax[:, :], pF_[:, 0:64], axis=AX.X,
                                         negate=True).then_inc(s_nv.h, 1)
                        nmax_ms = s_nv.bump()
                        k.emit("vector", c2)
                        if sub == "p2c_2":
                            break

                        def c3(sc, nmax_ms=nmax_ms):
                            sc.wait_ge(s_nv.h, nmax_ms)
                            sc.activation(k2e[:, :], pF_[:, 0:64], AF.Exp,
                                          bias=nmax[:, :], scale=1.0,
                                          accum_out=k2s[:, :]).then_inc(s_ev2.h, 1)
                        k2e_ms = s_ev2.bump()
                        k.emit("scalar", c3)
                        if sub == "p2c_3":
                            break

                        ec4 = s_nv.bump()
                        def c4(v, k2e_ms=k2e_ms, gate=prev_kg_last, ec4=ec4):
                            v.wait_ge(s_ev2.h, k2e_ms)
                            if gate is not None:
                                v.wait_ge(s_mm.h, gate)
                            v.reciprocal(rds[:, :], k2s[:, :]).then_inc(s_nv.h, 1)
                            v.wait_ge(s_nv.h, ec4)
                            v.tensor_scalar(k2n[:, :], k2e[:, :], rds[:, :], None,
                                            op0=ALU.mult).then_inc(s_nv.h, 1)
                        k2n_ms = s_nv.bump()
                        k.emit("vector", c4)
                        if sub == "p2c_4":
                            break

                        def c5(t, k2n_ms=k2n_ms, gate=prev_tr2_ok):
                            t.wait_ge(s_nv.h, k2n_ms)
                            if gate is not None:
                                t.wait_ge(s_nv.h, gate)
                            t.transpose(pTr2[0:64, 0, 0:64], k2n[0:64, :], id64s[0:64, :])
                            mm = t.transpose(pTr2[64:128, 0, 64:128], k2n[64:128, :],
                                             id64s[64:128, :])
                            mm.then_inc(s_nt.h, 1)
                        k2t_ps = s_nt.bump()
                        k.emit("tensor", c5)
                        if sub == "p2c_5":
                            break

                        def c6(v, k2t_ps=k2t_ps):
                            v.wait_ge(s_nt.h, k2t_ps)
                            v.tensor_copy(k2T_st[0:64, :], pTr2[0:64, 0, 0:64])
                            v.tensor_copy(k2T_st[64:128, :], pTr2[64:128, 0, 64:128])
                            v.tensor_copy(k2T_bd[0:64, 0:64], pTr2[0:64, 0, 0:64])
                            v.tensor_copy(k2T_bd[64:128, 64:128],
                                          pTr2[64:128, 0, 64:128]).then_inc(s_nv.h, 1)
                        k2t_ok = s_nv.bump()
                        k.emit("vector", c6)
                        if sub == "p2c_6":
                            break

                        def c7a(v, k2t_ok=k2t_ok):
                            v.wait_ge(s_nv.h, k2t_ok)
                            v.reduce_sum(scol[:, :], k2T_st[:, :], axis=AX.X
                                         ).then_inc(s_nv.h, 1)
                        scol_ms = s_nv.bump()
                        k.emit("vector", c7a)

                        def c7b(t, scol_ms=scol_ms):
                            t.wait_ge(s_nv.h, scol_ms)
                            t.transpose(pF_[0:1, 256:384], scol[:, :],
                                        id128f[:, :]).then_inc(s_mm.h, 1)
                        den_ms = s_mm.bump()
                        k.emit("tensor", c7b)
                        if sub == "p2c_7":
                            break

                        ec8 = s_nv.bump()
                        def c8(v, den_ms=den_ms, ec8=ec8):
                            v.wait_ge(s_mm.h, den_ms)
                            v.reduce_max(rdr[0:1, 0:1], pF_[0:1, 256:320], axis=AX.X)
                            v.reduce_max(rdr[0:1, 1:2], pF_[0:1, 320:384],
                                         axis=AX.X).then_inc(s_nv.h, 1)
                            v.wait_ge(s_nv.h, ec8)
                            v.reciprocal(rdr[0:1, :], rdr[0:1, :]).then_inc(s_nv.h, 1)
                        rd_ms = s_nv.bump()
                        k.emit("vector", c8)
                        if sub == "p2c_8":
                            break

                        def c9(t, rd_ms=rd_ms):
                            t.wait_ge(s_nv.h, rd_ms)
                            t.matmul(pF_[:, 192:193], sel0[0:1, :], rdr[0:1, 0:1],
                                     start=True, stop=False)
                            t.matmul(pF_[:, 192:193], sel1[0:1, :], rdr[0:1, 1:2],
                                     start=False, stop=True).then_inc(s_mm.h, 1)
                        rdb_ms = s_mm.bump()
                        k.emit("tensor", c9)
                        if sub == "p2c_9":
                            break

                        ec10 = s_nv.bump()
                        ec10b = s_nv.bump()
                        def c10(v, rdb_ms=rdb_ms, k2t_ok=k2t_ok, ec10=ec10, ec10b=ec10b):
                            v.wait_ge(s_mm.h, rdb_ms)
                            v.wait_ge(s_nv.h, k2t_ok)
                            v.tensor_copy(rds[:, :], pF_[:, 192:193]).then_inc(s_nv.h, 1)
                            v.wait_ge(s_nv.h, ec10)
                            v.tensor_scalar(V_st[:, :], k2T_st[:, :], rds[:, :], None,
                                            op0=ALU.mult).then_inc(s_nv.h, 1)
                            v.wait_ge(s_nv.h, ec10b)
                            v.tensor_copy(V_bd[0:64, 0:64], V_st[0:64, :])
                            v.tensor_copy(V_bd[64:128, 64:128],
                                          V_st[64:128, :]).then_inc(s_nv.h, 1)
                        v_ok = s_nv.bump()
                        k.emit("vector", c10)
                        if sub == "p2cS":
                            break

                        newton = []
                        vok = {0: v_ok}
                        for it in range(6):
                            slot = 1 + (it % 2)

                            def n_b(t, it=it, slot=slot, vok=vok):
                                t.wait_ge(s_nv.h, vok[it])
                                t.transpose(pTr2[0:64, slot, 0:64], V_st[0:64, :],
                                            id64s[0:64, :])
                                t.transpose(pTr2[64:128, slot, 64:128], V_st[64:128, :],
                                            id64s[64:128, :])
                                t.matmul(pF_[:, 256:320], k2T_bd[:, :], V_st[:, :],
                                         start=True, stop=True)
                                t.matmul(pF_[:, 320:384], V_bd[:, :], k2T_st[:, :],
                                         start=True, stop=True).then_inc(s_nt.h, 1)
                            kv_ms = s_nt.bump()

                            def n_c(v, it=it, slot=slot, kv_ms=kv_ms):
                                v.wait_ge(s_nt.h, kv_ms)
                                v.tensor_tensor(T_st[:, :], cIs[:, 0, :], pF_[:, 256:320],
                                                op=ALU.subtract)
                                v.tensor_copy(KVT_bd[0:64, 0:64], pF_[0:64, 320:384])
                                v.tensor_copy(KVT_bd[64:128, 64:128], pF_[64:128, 320:384])
                                v.tensor_copy(VT_bd[0:64, 0:64], pTr2[0:64, slot, 0:64])
                                v.tensor_copy(VT_bd[64:128, 64:128],
                                              pTr2[64:128, slot, 64:128]).then_inc(s_nv.h, 1)
                            t1_ms = s_nv.bump()

                            def n_d(t, t1_ms=t1_ms):
                                t.wait_ge(s_nv.h, t1_ms)
                                t.matmul(pF_[:, 384:448], KVT_bd[:, :], T_st[:, :],
                                         start=True, stop=True).then_inc(s_nt.h, 1)
                            u1_ms = s_nt.bump()

                            def n_e(v, u1_ms=u1_ms):
                                v.wait_ge(s_nt.h, u1_ms)
                                v.tensor_tensor(T_st[:, :], cIs[:, 1, :], pF_[:, 384:448],
                                                op=ALU.subtract).then_inc(s_nv.h, 1)
                            t2_ms = s_nv.bump()

                            def n_f(t, t2_ms=t2_ms):
                                t.wait_ge(s_nv.h, t2_ms)
                                t.matmul(pF_[:, 448:512], KVT_bd[:, :], T_st[:, :],
                                         start=True, stop=True).then_inc(s_nt.h, 1)
                            u2_ms = s_nt.bump()

                            def n_g(v, u2_ms=u2_ms):
                                v.wait_ge(s_nt.h, u2_ms)
                                v.tensor_tensor(T_st[:, :], cIs[:, 2, :], pF_[:, 448:512],
                                                op=ALU.subtract).then_inc(s_nv.h, 1)
                            t3_ms = s_nv.bump()

                            def n_h(t, t3_ms=t3_ms):
                                t.wait_ge(s_nv.h, t3_ms)
                                t.matmul(pF_[:, 256:320], VT_bd[:, :], T_st[:, :],
                                         start=True, stop=True).then_inc(s_nt.h, 1)
                            vn_ms = s_nt.bump()

                            eni = s_nv.bump() if it < 5 else None
                            def n_i(v, vn_ms=vn_ms, it=it, eni=eni):
                                v.wait_ge(s_nt.h, vn_ms)
                                if it < 5:
                                    v.tensor_scalar(V_st[:, :], pF_[:, 256:320], 0.25, None,
                                                    op0=ALU.mult).then_inc(s_nv.h, 1)
                                    v.wait_ge(s_nv.h, eni)
                                    v.tensor_copy(V_bd[0:64, 0:64], V_st[0:64, :])
                                    v.tensor_copy(V_bd[64:128, 64:128],
                                                  V_st[64:128, :]).then_inc(s_nv.h, 1)
                                else:
                                    v.tensor_scalar(V_st[:, :], pF_[:, 256:320], 0.25,
                                                    None, op0=ALU.mult).then_inc(s_nv.h, 1)
                            vok[it + 1] = s_nv.bump()

                            newton += [("tensor", n_b), ("vector", n_c),
                                       ("tensor", n_d), ("vector", n_e), ("tensor", n_f),
                                       ("vector", n_g), ("tensor", n_h), ("vector", n_i)]

                        kpath = []
                        k1g = {}
                        k1e_ms = {}
                        k1n_ms = {}
                        k1t_ms = {}
                        k1T_ms = {}
                        k3g = {}
                        k3e_ms = {}
                        for g8 in range(8):
                            isk1 = g8 < 4
                            gg = g8 % 4

                            def kg_t(t, pr=pr, g8=g8, isk1=isk1, gg=gg,
                                     klbd_ok=klbd_ok, pk3e=prev_k3e,
                                     k1e_ms=k1e_ms, k3e_ms=k3e_ms):
                                if g8 == 0:
                                    t.wait_ge(s_gp.h, klbd_ok)
                                if g8 >= 2:
                                    t.wait_ge(s_ev2.h,
                                              (k1e_ms if g8 - 2 < 4 else k3e_ms)[
                                                  (g8 - 2) % 4])
                                elif pk3e is not None:
                                    t.wait_ge(s_ev2.h, pk3e[2 + g8])
                                pX = pA if g8 % 2 == 0 else pB
                                cc = (pr if isk1 else 8 + pr)
                                bd = Kl_bd if isk1 else Ql_bd
                                for j in range(4):
                                    nch = 4 * gg + j
                                    mm = t.matmul(pX[:, j * 128:(j + 1) * 128],
                                                  QKT[:, cc, nch * 128:(nch + 1) * 128],
                                                  bd[:, :], start=True, stop=True)
                                mm.then_inc(s_mm.h, 1)
                            (k1g if isk1 else k3g)[gg] = s_mm.bump()
                            kpath.append(("tensor", kg_t))

                            if isk1:
                                def k1_s(sc, pr=pr, gg=gg, par=par, k1g=k1g, k1n_ms=k1n_ms):
                                    sc.wait_ge(s_mm.h, k1g[gg])
                                    if gg >= 2:
                                        sc.wait_ge(s_ev.h, k1n_ms[gg - 2])
                                    pX = pA if gg % 2 == 0 else pB
                                    sc.activation(e1f[:, gg % 2, :], pX[:, :],
                                                  AF.Exp).then_inc(s_ev2.h, 1)
                                k1e_ms[gg] = s_ev2.bump()
                                kpath.append(("scalar", k1_s))

                                ek1, ek2 = s_ev.bump(), s_ev.bump()
                                def k1_v(v, gg=gg, par=par, k1e_ms=k1e_ms,
                                         k1t_ms=k1t_ms, ek1=ek1, ek2=ek2):
                                    v.wait_ge(s_ev2.h, k1e_ms[gg])
                                    if gg >= 2:
                                        v.wait_ge(s_tp.h, k1t_ms[gg - 2])
                                    v.tensor_reduce(s1r[:, :],
                                                    e1f[:, gg % 2, :].rearrange(
                                                        "p (a m) -> p a m", a=8),
                                                    axis=AX.X, op=ALU.add).then_inc(s_ev.h, 1)
                                    v.wait_ge(s_ev.h, ek1)
                                    v.reciprocal(s1r[:, :], s1r[:, :]).then_inc(s_ev.h, 1)
                                    v.wait_ge(s_ev.h, ek2)
                                    for j in range(8):
                                        op = v.tensor_scalar(
                                            e1n[:, gg % 2, j * 64:(j + 1) * 64],
                                            e1f[:, gg % 2, j * 64:(j + 1) * 64],
                                            s1r[:, j:j + 1], None, op0=ALU.mult)
                                    op.then_inc(s_ev.h, 1)
                                k1n_ms[gg] = s_ev.bump()
                                kpath.append(("vector", k1_v))

                                def k1_tr(t, gg=gg, par=par, pk1T=prev_k1T, k1n_ms=k1n_ms, k1T_ms=k1T_ms):
                                    t.wait_ge(s_ev.h, k1n_ms[gg])
                                    if gg >= 1:
                                        t.wait_ge(s_ev2.h, k1T_ms[gg - 1])
                                    elif pk1T is not None:
                                        t.wait_ge(s_ev2.h, pk1T[3])
                                    for j in range(4):
                                        mm = t.transpose(pTrb[:, 4 * (gg % 2) + j, :],
                                                         e1n[:, gg % 2, j * 128:(j + 1) * 128],
                                                         id128[:, :])
                                    mm.then_inc(s_tp.h, 1)
                                k1t_ms[gg] = s_tp.bump()
                                kpath.append(("tensor", k1_tr))

                                def k1_e(sc, gg=gg, par=par, pr=pr, k1t_ms=k1t_ms):
                                    sc.wait_ge(s_tp.h, k1t_ms[gg])
                                    if gg == 0 and pr >= 2 and prev_sv.get(pr - 2):
                                        sc.wait_ge(s_mm.h, prev_sv[pr - 2])
                                    for j in range(4):
                                        nch = 4 * gg + j
                                        mm = sc.copy(E1T[:, par, nch * 128:(nch + 1) * 128],
                                                     pTrb[:, 4 * (gg % 2) + j, :])
                                    mm.then_inc(s_ev2.h, 1)
                                k1T_ms[gg] = s_ev2.bump()
                                kpath.append(("scalar", k1_e))
                            else:
                                def k3_s(sc, gg=gg, par=par, pr=pr, k3g=k3g):
                                    sc.wait_ge(s_mm.h, k3g[gg])
                                    if gg == 0 and pr >= 2 and prev_av.get(pr - 2):
                                        sc.wait_ge(s_mm.h, prev_av[pr - 2])
                                    pX = pA if gg % 2 == 0 else pB
                                    sc.activation(
                                        E3T[:, par, 4 * gg:4 * gg + 4, :].rearrange(
                                            "p a m -> p (a m)"),
                                        pX[:, :], AF.Exp).then_inc(s_ev2.h, 1)
                                k3e_ms[gg] = s_ev2.bump()
                                kpath.append(("scalar", k3_s))

                        def s3av_t(t, pr=pr, par=par, psvev=prev_svev, k3e_ms=k3e_ms):
                            t.wait_ge(s_ev2.h, k3e_ms[3])
                            if pr >= 1:
                                t.wait_ge(s_nv.h, avn_prev[pr - 1])
                            if psvev is not None:
                                t.wait_ge(s_ev.h, psvev[3])
                            for nch in range(16):
                                t.matmul(pC[:, 0:1], E3T[:, par, nch, :], ones128[:, :],
                                         start=(nch == 0), stop=(nch == 15))
                                mm = t.matmul(pD[:, 0:128], E3T[:, par, nch, :],
                                              Vrm[:, nch, pr * 128:(pr + 1) * 128],
                                              start=(nch == 0), stop=(nch == 15))
                            mm.then_inc(s_mm.h, 1)
                        av_ms = s_mm.bump()
                        prev_av[pr] = av_ms
                        kpath.append(("tensor", s3av_t))

                        eav = s_nv.bump()
                        def av_v(v, av_ms=av_ms, eav=eav):
                            v.wait_ge(s_mm.h, av_ms)
                            v.reciprocal(rs3[:, :], pC[:, 0:1]).then_inc(s_nv.h, 1)
                            v.wait_ge(s_nv.h, eav)
                            v.tensor_scalar(AVn[0:64, :], pD[0:64, 0:64],
                                            rs3[0:64, :], None, op0=ALU.mult)
                            v.tensor_scalar(AVn[64:128, :], pD[64:128, 64:128],
                                            rs3[64:128, :], None,
                                            op0=ALU.mult).then_inc(s_nv.h, 1)
                        avn_ms = s_nv.bump()
                        avn_prev[pr] = avn_ms
                        kpath.append(("vector", av_v))

                        if sub == "p2cN":
                            for eng, fn in newton:
                                k.emit(eng, fn)
                            break
                        if sub == "p2cK":
                            for eng, fn in newton + kpath:
                                k.emit(eng, fn)
                            break
                        merged = []
                        ni, ki = 0, 0
                        while ni < len(newton) or ki < len(kpath):
                            for _ in range(3):
                                if ni < len(newton):
                                    merged.append(newton[ni]); ni += 1
                            if ki < len(kpath):
                                merged.append(kpath[ki]); ki += 1
                        for eng, fn in merged:
                            k.emit(eng, fn)

                        def z_t(t, vok6=vok[6], avn_ms=avn_ms):
                            t.wait_ge(s_nv.h, vok6)
                            t.transpose(pTr2[0:64, 3, 0:64], V_st[0:64, :], id64s[0:64, :])
                            mm = t.transpose(pTr2[64:128, 3, 64:128], V_st[64:128, :],
                                             id64s[64:128, :])
                            mm.then_inc(s_nt.h, 1)
                        v6t_ps = s_nt.bump()
                        k.emit("tensor", z_t)

                        def z_v(v, v6t_ps=v6t_ps):
                            v.wait_ge(s_nt.h, v6t_ps)
                            v.tensor_copy(VT_bd[0:64, 0:64], pTr2[0:64, 3, 0:64])
                            v.tensor_copy(VT_bd[64:128, 64:128],
                                          pTr2[64:128, 3, 64:128]).then_inc(s_nv.h, 1)
                        v6t_ok = s_nv.bump()
                        k.emit("vector", z_v)

                        def z_mm(t, v6t_ok=v6t_ok, avn_ms=avn_ms):
                            t.wait_ge(s_nv.h, v6t_ok)
                            t.wait_ge(s_nv.h, avn_ms)
                            t.matmul(pF_[:, 384:448], VT_bd[:, :], AVn[:, :],
                                     start=True, stop=True).then_inc(s_nt.h, 1)
                        zmm_ms = s_nt.bump()
                        k.emit("tensor", z_mm)

                        def z_ev(v, zmm_ms=zmm_ms):
                            v.wait_ge(s_nt.h, zmm_ms)
                            v.tensor_copy(Zst[:, :], pF_[:, 384:448]).then_inc(s_nv.h, 1)
                        zst_ms = s_nv.bump()
                        k.emit("vector", z_ev)

                        def z_bd(g, zst_ms=zst_ms):
                            g.wait_ge(s_nv.h, zst_ms)
                            g.tensor_copy(Z_bd[0:64, 0:64], Zst[0:64, :])
                            g.tensor_copy(Z_bd[64:128, 64:128],
                                          Zst[64:128, :]).then_inc(s_gp.h, 1)
                        zbd_ms = s_gp.bump()
                        k.emit("gpsimd", z_bd)
                        prev_pf_free = zst_ms
                        prev_tr2_ok = vok[6]

                        sv_ev_ms = {}
                        for nh2 in range(4):
                            def sv_t(t, pr=pr, par=par, nh2=nh2, zbd_ms=zbd_ms, k1T_ms=k1T_ms, sv_ev_ms=sv_ev_ms):
                                if nh2 == 0:
                                    t.wait_ge(s_gp.h, zbd_ms)
                                    t.wait_ge(s_ev2.h, k1T_ms[3])
                                if nh2 >= 2:
                                    t.wait_ge(s_ev.h, sv_ev_ms[nh2 - 2])
                                pX = pE_ if nh2 % 2 == 0 else pC
                                mm = t.matmul(pX[:, :], Z_bd[:, :],
                                              E1T[:, par, nh2 * 512:(nh2 + 1) * 512],
                                              start=True, stop=True)
                                mm.then_inc(s_mm.h, 1)
                            svmm = s_mm.bump()
                            k.emit("tensor", sv_t)

                            def sv_e(v, pr=pr, nh2=nh2, svmm=svmm):
                                v.wait_ge(s_mm.h, svmm)
                                pX = pE_ if nh2 % 2 == 0 else pC
                                v.tensor_copy(SVT[:, pr, nh2 * 512:(nh2 + 1) * 512],
                                              pX[:, :]).then_inc(s_ev.h, 1)
                            sv_ev_ms[nh2] = s_ev.bump()
                            k.emit("vector", sv_e)
                        prev_sv[pr] = svmm
                        prev_kg_last = k3g[3]
                        prev_k3e = dict(k3e_ms)
                        prev_k1T = dict(k1T_ms)
                        prev_svev = dict(sv_ev_ms)

                    sv_done_v = s_ev.v
                    k.run_block()

                    if sub.startswith("p2c"):
                        return nc

                    with ExitStack() as estk:
                        esb = lambda n, s, d: estk.enter_context(nc.sbuf_tensor(k.uname(n), s, d))
                        wprojb = esb("wprojb", [128, 8, DM], BF16)
                        xc2 = esb("xc2", [128, 2, DM], F32)
                        x1t = esb("x1t", [128, 2, DM], F32)
                        stat2 = esb("stat2", [128, 16], F32)
                        mv3 = esb("mv3", [128, 2], F32)
                        rstd2 = esb("rstd2", [128, 1], F32)
                        sjunk2 = esb("sjunk2", [128, 4], F32)

                        k.new_block()
                        gq0 = s_gq.v

                        def e_gq(g):
                            for d in range(8):
                                g.dma_start(wprojb[:, d, :],
                                            wproj_d[d * 128:(d + 1) * 128, :]).then_inc(s_gq.h, 16)
                        s_gq.bump(16 * 8)
                        k.emit("gpsimd", e_gq)

                        pj_mm = {}
                        pj_ap = {}
                        ST_X1 = {}
                        for i in range(16):
                            def e_sync(sync, i=i, b=b):
                                if i >= 2:
                                    sync.wait_ge(s_ln.h, pj_ap[i - 2])
                                sync.dma_start(
                                    xc2[:, i % 2, :],
                                    x_d[b * N + 128 * i:b * N + 128 * (i + 1), :]
                                ).then_inc(s_x.h, 16)
                                if i >= 1:
                                    sync.wait_ge(s_ln.h, pj_ap[i - 1])
                                    sync.dma_start(
                                        x1s[b * N + 128 * (i - 1):b * N + 128 * i, :],
                                        x1t[:, (i - 1) % 2, :]).then_inc(s_st.h, 16)
                            x_at = s_x.bump(16)
                            if i >= 1:
                                ST_X1[i - 1] = s_st.bump(16)
                            k.emit("sync", e_sync)

                            def e_t(t, i=i, gq0=gq0):
                                if i == 0:
                                    t.wait_ge(s_gq.h, gq0 + 128)
                                for dh in range(2):
                                    gI = 2 * i + dh
                                    if gI >= 2:
                                        t.wait_ge(s_ln.h, pj_ap.get(i - 1, 0))
                                    pX = pA if gI % 2 == 0 else pB
                                    for pr2 in range(8):
                                        mm = t.matmul(pX[:, :],
                                                      SVT[:, pr2, i * 128:(i + 1) * 128],
                                                      wprojb[:, pr2, dh * 512:(dh + 1) * 512],
                                                      start=(pr2 == 0), stop=(pr2 == 7))
                                    mm.then_inc(s_mm.h, 1)
                            pj_mm[i] = (s_mm.bump(), s_mm.bump())
                            k.emit("tensor", e_t)

                            ee1, ee2 = s_ln.bump(), s_ln.bump()
                            def e_v1(v, i=i, x_at=x_at, ee1=ee1, ee2=ee2):
                                v.wait_ge(s_mm.h, pj_mm[i][1])
                                v.bn_stats(stat2[:, 0:6], pA[:, :]).then_inc(s_ln.h, 1)
                                v.wait_ge(s_ln.h, ee1)
                                v.bn_stats(stat2[:, 6:12], pB[:, :]).then_inc(s_ln.h, 1)
                                v.wait_ge(s_ln.h, ee2)
                                v.bn_aggr(mv3[:, :], stat2[:, 0:12]).then_inc(s_ln.h, 1)
                            pst = s_ln.bump()
                            k.emit("vector", e_v1)

                            def e_sq(sc, i=i, pst=pst):
                                sc.wait_ge(s_ln.h, pst)
                                sc.activation(sjunk2[:, 0:1], mv3[:, 1:2], AF.Sqrt,
                                              bias=epsc[:, 0:1], scale=1.0).then_inc(s_sq.h, 1)
                            sqm = s_sq.bump()
                            k.emit("scalar", e_sq)

                            ee3 = s_ln.bump()
                            def e_v2(v, i=i, sqm=sqm, x_at=x_at, ee3=ee3):
                                v.wait_ge(s_sq.h, sqm)
                                v.wait_ge(s_x.h, x_at)
                                if i >= 2:
                                    v.wait_ge(s_st.h, ST_X1[i - 2])
                                v.reciprocal(rstd2[:, :], sjunk2[:, 0:1]).then_inc(s_ln.h, 1)
                                v.wait_ge(s_ln.h, ee3)
                                for dh in range(2):
                                    pX = pA if dh == 0 else pB
                                    v.tensor_scalar(x1t[:, i % 2, dh * 512:(dh + 1) * 512],
                                                    pX[:, :], mv3[:, 0:1], rstd2[:, :],
                                                    op0=ALU.subtract, op1=ALU.mult)
                                op = v.tensor_tensor(x1t[:, i % 2, :], x1t[:, i % 2, :],
                                                     xc2[:, i % 2, :], op=ALU.add)
                                op.then_inc(s_ln.h, 1)
                            pj_ap[i] = s_ln.bump()
                            k.emit("vector", e_v2)

                        def e_fin(sync, b=b):
                            sync.wait_ge(s_ln.h, pj_ap[15])
                            sync.dma_start(x1s[b * N + 128 * 15:b * N + 128 * 16, :],
                                           x1t[:, 15 % 2, :]).then_inc(s_st.h, 16)
                        s_st.bump(16)
                        k.emit("sync", e_fin)
                        k.run_block()

        if debug_stage in ("attn", "attn1"):
            k.new_block()
            st_target = s_st.v
            def fin(sync):
                sync.wait_ge(s_st.h, st_target)
            k.emit("sync", fin)
            k.run_block()
            return nc

        W1T = sb("W1T", [128, 8, FF], BF16)
        W2T = sb("W2T", [128, NFC, DM], BF16)
        with ExitStack() as rstk:
            rsb = lambda n, s, d: rstk.enter_context(nc.sbuf_tensor(k.uname(n), s, d))
            xc1 = rsb("xc1", [128, 2, DM], F32)
            zf = rsb("zf", [128, 2, DM], F32)
            rmch = rsb("rmch", [128, 2, 4], F32)
            znbt = rsb("znbt", [128, 2, DM], BF16)
            zT32 = rsb("zT32", [128, 2, 8, 128], F32)
            stat3 = rsb("stat3", [128, 16], F32)
            mvB = rsb("mvB", [128, 2, 2], F32)
            rstdB = rsb("rstdB", [128, 1], F32)
            sjk = rsb("sjk", [128, 4], F32)
            junkS = rsb("junkS", [128, DM], F32)
            scr = rsb("scr", [128, 8], F32)
            affs = rsb("affs", [128, 8], F32)
            max8 = rsb("max8", [128, 8], F32)
            idx8 = rsb("idx8", [128, 8], U32)
            esf = rsb("esf", [128, 1], F32)
            ohf = rsb("ohf", [128, 8], F32)
            ohsel = rsb("ohsel", [128, 8], F32)
            rkred = rsb("rkred", [128, 1], F32)
            of32 = rsb("of32", [128, 1], F32)
            ovr = rsb("ovr", [128, 1], F32)
            cnt8 = rsb("cnt8", [1, 8], F32)

            DBG = debug_stage == "dump"
            k.new_block()
            P2_ST_DONE = s_st.v
            _breg = {}

            def r_breg(g):
                _breg["r"] = g.alloc_register(name="sc_bound")
                g.reg_mov(_breg["r"], E * CAP - 1)
                _breg["rows"] = g.alloc_register(name="sc_bound2")
                g.reg_mov(_breg["rows"], ROWS - 1)
            k.emit("gpsimd", r_breg)
            if wprep_pos < len(wprep_units):
                k.emit("gpsimd", emit_wprep(wprep_units[wprep_pos:]))
                wprep_pos = len(wprep_units)

            W0_BASE = s_w.v
            WP_ALL = s_wp.v
            s_w.bump(16 * 40)
            wpre_dmas = ([("w1", d) for d in range(8)] +
                         [("w2", f) for f in range(NFC)])

            def emit_wpre_piece(lo, hi, first):
                def wp(sync, lo=lo, hi=hi, first=first):
                    if first:
                        sync.wait_ge(s_st.h, P2_ST_DONE)
                        sync.wait_ge(s_wp.h, WP_ALL)
                    for kind, i in wpre_dmas[lo:hi]:
                        if kind == "w1":
                            sync.dma_start(W1T[:, i, :],
                                           w1b[0, :, i * 128:(i + 1) * 128],
                                           transpose=True).then_inc(s_w.h, 16)
                        else:
                            sync.dma_start(W2T[:, i, :],
                                           w2b[0, :, i * 128:(i + 1) * 128],
                                           transpose=True).then_inc(s_w.h, 16)
                k.emit("sync", wp)

            LN1_MS = {}
            ST2_MS = {}
            SQ2_MS = {}
            AP2_MS = {}
            AFF_MS = {}
            AFF_CONS = {}
            RK_MS = {}
            r_tp = {}
            r_ev = {}
            r_sc = {}
            r_ch = {}
            rm_sq = {}

            def emit_front(c):
                def r_sync(sync, c=c):
                    if c >= 2:
                        sync.wait_ge(s_sq.h, LN1_MS[c - 2])
                    sync.dma_start(xc1[:, c % 2, :],
                                   x1s[128 * c:128 * (c + 1), :]).then_inc(s_x.h, 16)
                x_at = s_x.bump(16)
                k.emit("sync", r_sync)

                w1_, w2_ = s_sq.bump(), s_sq.bump()
                def r_ln1a(sc, c=c, x_at=x_at, w1_=w1_, w2_=w2_):
                    sc.wait_ge(s_x.h, x_at)
                    if c >= 2:
                        sc.wait_ge(s_tp.h, r_tp[c - 2])
                        sc.wait_ge(s_ln.h, AP2_MS[c - 2])
                    xi = xc1[:, c % 2, :]
                    sc.activation(junkS[:, :], xi, AF.Square,
                                  accum_out=scr[:, 0:1])
                    sc.activation(junkS[:, :], xi, AF.Copy,
                                  accum_out=scr[:, 1:2]).then_inc(s_sq.h, 1)
                    sc.wait_ge(s_sq.h, w1_)
                    sc.mul(scr[:, 2:3], scr[:, 1:2], 1.0 / DM)
                    sc.activation(scr[:, 3:4], scr[:, 2:3], AF.Square)
                    sc.mul(scr[:, 3:4], scr[:, 3:4], -1.0)
                    sc.mul(scr[:, 4:5], scr[:, 0:1], 1.0 / DM).then_inc(s_sq.h, 1)
                    sc.wait_ge(s_sq.h, w2_)
                    sc.add(scr[:, 4:5], scr[:, 4:5], scr[:, 3:4])
                    sc.activation(scr[:, 5:6], scr[:, 4:5], AF.Sqrt,
                                  bias=epsc[:, 0:1], scale=1.0)
                    sc.mul(scr[:, 0:1], scr[:, 2:3], -1.0).then_inc(s_sq.h, 1)
                LN1A_MS = s_sq.bump()
                k.emit("scalar", r_ln1a)

                def r_ln1v(v, c=c, LN1A_MS=LN1A_MS):
                    v.wait_ge(s_sq.h, LN1A_MS)
                    v.reciprocal(scr[:, 6:7], scr[:, 5:6]).then_inc(s_ln.h, 1)
                LN1V_MS = s_ln.bump()
                k.emit("vector", r_ln1v)

                def r_ln1b(sc, c=c, LN1V_MS=LN1V_MS):
                    sc.wait_ge(s_ln.h, LN1V_MS)
                    sc.add(junkS[:, :], xc1[:, c % 2, :], scr[:, 0:1])
                    sc.mul(zf[:, c % 2, :], junkS[:, :],
                           scr[:, 6:7]).then_inc(s_sq.h, 1)
                LN1_MS[c] = s_sq.bump()
                k.emit("scalar", r_ln1b)

            def emit_mid(c):
                e1, e2 = s_ln.bump(), s_ln.bump()
                def r_st2(v, c=c, e1=e1, e2=e2):
                    v.wait_ge(s_sq.h, LN1_MS[c])
                    if c >= 2:
                        v.wait_ge(s_sq.h, rm_sq[c - 2])
                    v.bn_stats(stat3[:, 0:6], zf[:, c % 2, 0:512]).then_inc(s_ln.h, 1)
                    v.wait_ge(s_ln.h, e1)
                    v.bn_stats(stat3[:, 6:12], zf[:, c % 2, 512:1024]).then_inc(s_ln.h, 1)
                    v.wait_ge(s_ln.h, e2)
                    v.bn_aggr(mvB[:, c % 2, :], stat3[:, 0:12]).then_inc(s_ln.h, 1)
                ST2_MS[c] = s_ln.bump()
                k.emit("vector", r_st2)

                def r_s2(sc, c=c):
                    sc.wait_ge(s_ln.h, ST2_MS[c])
                    sc.activation(sjk[:, 2 + (c % 2):3 + (c % 2)], mvB[:, c % 2, 1:2],
                                  AF.Sqrt, bias=epsc[:, 0:1],
                                  scale=1.0).then_inc(s_sq.h, 1)
                SQ2_MS[c] = s_sq.bump()
                k.emit("scalar", r_s2)

                er6 = s_ln.bump()
                def r_apl2(v, c=c, er6=er6):
                    v.wait_ge(s_sq.h, SQ2_MS[c])
                    if c >= 2:
                        v.wait_ge(s_sc.h, r_sc[c - 2])
                    v.reciprocal(rstdB[:, :],
                                 sjk[:, 2 + (c % 2):3 + (c % 2)]).then_inc(s_ln.h, 1)
                    v.wait_ge(s_ln.h, er6)
                    v.tensor_scalar(znbt[:, c % 2, :], zf[:, c % 2, :],
                                    mvB[:, c % 2, 0:1], rstdB[:, :],
                                    op0=ALU.subtract, op1=ALU.mult
                                    ).then_inc(s_ln.h, 1)
                AP2_MS[c] = s_ln.bump()
                k.emit("vector", r_apl2)

                def r_tr(t, c=c):
                    t.wait_ge(s_sq.h, LN1_MS[c])
                    if c >= 2:
                        t.wait_ge(s_ev.h, r_ev[c - 2])
                    bks = (pA, pB) if c % 2 == 0 else (pC, pD)
                    for d in range(8):
                        pX = bks[d // 4]
                        mm = t.transpose(pX[:, (d % 4) * 128:(d % 4 + 1) * 128],
                                         zf[:, c % 2, d * 128:(d + 1) * 128],
                                         id128f[:, :])
                    mm.then_inc(s_tp.h, 1)
                r_tp[c] = s_tp.bump()
                k.emit("tensor", r_tr)

                def r_tev(v, c=c):
                    v.wait_ge(s_tp.h, r_tp[c])
                    if c >= 2:
                        v.wait_ge(s_mm.h, AFF_MS[c - 2])
                    bks = (pA, pB) if c % 2 == 0 else (pC, pD)
                    for d in range(8):
                        pX = bks[d // 4]
                        mm = v.tensor_copy(zT32[:, c % 2, d, :],
                                           pX[:, (d % 4) * 128:(d % 4 + 1) * 128])
                    mm.then_inc(s_ev.h, 1)
                r_ev[c] = s_ev.bump()
                k.emit("vector", r_tev)

                def r_aff(t, c=c):
                    t.wait_ge(s_ev.h, r_ev[c])
                    if c >= 1:
                        t.wait_ge(s_ln.h, AFF_CONS[c - 1])
                    for d in range(8):
                        mm = t.matmul(pE_[:, 0:8], zT32[:, c % 2, d, :], CT[:, d, :],
                                      start=(d == 0), stop=(d == 7))
                    mm.then_inc(s_mm.h, 1)
                AFF_MS[c] = s_mm.bump()
                k.emit("tensor", r_aff)

            def emit_back(c):
                ea1, ea2, ea3, ea4 = (s_ln.bump(), s_ln.bump(), s_ln.bump(),
                                      s_ln.bump())
                def r_v4(v, c=c, ea1=ea1, ea2=ea2, ea3=ea3, ea4=ea4):
                    v.wait_ge(s_mm.h, AFF_MS[c])
                    if c >= 1:
                        v.wait_ge(s_sq.h, rm_sq[c - 1])
                    v.tensor_copy(affs[:, :], pE_[:, 0:8]).then_inc(s_ln.h, 1)
                    v.wait_ge(s_ln.h, ea1)
                    v.max(max8[:, :], affs[:, :]).then_inc(s_ln.h, 1)
                    v.wait_ge(s_ln.h, ea2)
                    v.max_index(idx8[:, :], max8[:, :], affs[:, :]).then_inc(s_ln.h, 1)
                    v.wait_ge(s_ln.h, ea3)
                    v.tensor_copy(esf[:, :], idx8[:, 0:1]).then_inc(s_ln.h, 1)
                    v.wait_ge(s_ln.h, ea4)
                    v.tensor_scalar(ohf[:, :], iota8[:, :], esf[:, :], None,
                                    op0=ALU.is_equal).then_inc(s_ln.h, 1)
                AFF_CONS[c] = s_ln.bump()
                k.emit("vector", r_v4)

                def r_rk(t, c=c):
                    t.wait_ge(s_ln.h, AFF_CONS[c])
                    if c >= 1:
                        t.wait_ge(s_rt.h, r_ch[c - 1])
                    t.matmul(pF_[:, 0:8], ltu[:, :], ohf[:, :], start=True, stop=False)
                    t.matmul(pF_[:, 0:8], ones1f[0:1, :], Rrow[0:1, :],
                             start=False, stop=True)
                    t.matmul(pF_[0:1, 8:16], ones128f[:, :], ohf[:, :],
                             start=True, stop=True).then_inc(s_mm.h, 1)
                RK_MS[c] = s_mm.bump()
                k.emit("tensor", r_rk)

                eb = [s_ln.bump() for _ in range(6)]
                def r_v5(v, c=c, eb=eb):
                    v.wait_ge(s_mm.h, RK_MS[c])
                    v.tensor_tensor(ohsel[:, :], pF_[:, 0:8], ohf[:, :],
                                    op=ALU.mult).then_inc(s_ln.h, 1)
                    v.wait_ge(s_ln.h, eb[0])
                    v.tensor_reduce(rkred[:, :], ohsel[:, :], axis=AX.X,
                                    op=ALU.add).then_inc(s_ln.h, 1)
                    v.wait_ge(s_ln.h, eb[1])
                    v.tensor_scalar(of32[:, :], esf[:, :], float(CAP), rkred[:, :],
                                    op0=ALU.mult, op1=ALU.add)
                    v.tensor_scalar(ovr[:, :], rkred[:, :], -(float(CAP) - 1.0), 0.0,
                                    op0=ALU.add, op1=ALU.max).then_inc(s_ln.h, 1)
                    v.wait_ge(s_ln.h, eb[2])
                    v.tensor_scalar(ovr[:, :], ovr[:, :], 1.0, None,
                                    op0=ALU.min).then_inc(s_ln.h, 1)
                    v.wait_ge(s_ln.h, eb[3])
                    v.scalar_tensor_tensor(of32[:, :], ovr[:, :], 1.0e9, of32[:, :],
                                           op0=ALU.mult, op1=ALU.add).then_inc(s_ln.h, 1)
                    v.wait_ge(s_ln.h, eb[4])
                    v.tensor_copy(o_all[:, c:c + 1], of32[:, :])
                    v.tensor_copy(cnt8[0:1, :], pF_[0:1, 8:16]).then_inc(s_ln.h, 1)
                    v.wait_ge(s_ln.h, eb[5])
                    if DBG:
                        v.tensor_copy(dbg_rk[:, 0, c:c + 1], rkred[:, :])
                        v.tensor_copy(dbg_rk[:, 1, c:c + 1], ovr[:, :])
                        v.tensor_copy(dbg_rk[:, 2, c:c + 1], of32[:, :])
                        v.tensor_copy(dbg_rk[:, 3, c:c + 1], esf[:, :])
                    v.tensor_tensor(Rrow[0:1, :], Rrow[0:1, :], cnt8[0:1, :],
                                    op=ALU.add).then_inc(s_rt.h, 1)
                r_ch[c] = s_rt.bump()
                k.emit("vector", r_v5)

                def r_rm(sc, c=c):
                    sc.wait_ge(s_ln.h, AFF_CONS[c])
                    sc.wait_ge(s_sq.h, SQ2_MS[c])
                    if c >= 2:
                        sc.wait_ge(s_sc.h, r_sc[c - 2])
                    sc.copy(rmch[:, c % 2, 0:1], riota[:, c:c + 1])
                    sc.copy(rmch[:, c % 2, 1:2], max8[:, 0:1])
                    sc.copy(rmch[:, c % 2, 2:3], mvB[:, c % 2, 0:1])
                    sc.copy(rmch[:, c % 2, 3:4],
                            sjk[:, 2 + (c % 2):3 + (c % 2)]).then_inc(s_sq.h, 1)
                rm_sq[c] = s_sq.bump()
                k.emit("scalar", r_rm)

                def r_scat(g, c=c):
                    if c == 0:
                        g.wait_ge(s_st.h, P2_ST_DONE)
                    g.wait_ge(s_ln.h, AP2_MS[c])
                    g.wait_ge(s_rt.h, r_ch[c])
                    g.indirect_dma_start(
                        out=znbkt, out_offset=IndirectOffsetOnAxis(o_all[:, c:c + 1], 0),
                        in_=znbt[:, c % 2, :], in_offset=None,
                        bounds_check=_breg["r"],
                        oob_is_err=False).then_inc(s_sc.h, 16)
                    g.wait_ge(s_sq.h, rm_sq[c])
                    g.indirect_dma_start(
                        out=rmtab, out_offset=IndirectOffsetOnAxis(o_all[:, c:c + 1], 0),
                        in_=rmch[:, c % 2, :], in_offset=None,
                        bounds_check=_breg["r"],
                        oob_is_err=False).then_inc(s_sc.h, 16)
                r_sc[c] = s_sc.bump(32)
                k.emit("gpsimd", r_scat)

            for c in range(NCH):
                if c % 4 == 0 and c // 4 < 8:
                    emit_wpre_piece(5 * (c // 4), 5 * (c // 4 + 1), c == 0)
                emit_front(c)
                if c >= 1:
                    emit_mid(c - 1)
                if c >= 2:
                    emit_back(c - 2)
            emit_mid(NCH - 1)
            emit_back(NCH - 2)
            emit_back(NCH - 1)

            en1, en2 = s_ln.bump(), s_ln.bump()
            def r_nse(v, en1=en1, en2=en2):
                v.wait_ge(s_rt.h, r_ch[NCH - 1])
                v.tensor_scalar(nse_f[0:1, :], Rrow[0:1, :], 255.0, 1.0 / 256.0,
                                op0=ALU.add, op1=ALU.mult).then_inc(s_ln.h, 1)
                v.wait_ge(s_ln.h, en1)
                v.tensor_scalar(nse_f[0:1, :], nse_f[0:1, :], 0.25, None,
                                op0=ALU.add).then_inc(s_ln.h, 1)
                v.wait_ge(s_ln.h, en2)
                v.tensor_copy(nse_i[0:1, :], nse_f[0:1, :]).then_inc(s_rt.h, 1)
            NSE_MS = s_rt.bump()
            k.emit("vector", r_nse)

            def r_gate(sync):
                sync.wait_ge(s_sc.h, r_sc[NCH - 1])
                sync.wait_ge(s_rt.h, NSE_MS)
            k.emit("sync", r_gate)
            k.run_block()

        with ExitStack() as fstk:
            fsb = lambda n, s, d: fstk.enter_context(nc.sbuf_tensor(k.uname(n), s, d))
            znl = fsb("znl", [128, 3, 2, DM], BF16)
            znTt = fsb("znTt", [128, 2, 8, 256], BF16)
            hTt = fsb("hTt", [128, 2, 256], BF16)
            t_sb = fsb("t_sb", [128, 2, 2, DM], BF16)
            b2row = fsb("b2row", [1, 2, DM], BF16)
            rmb = fsb("rmb", [128, 2, 2, 4], F32)
            idxu = fsb("idxu", [128, 4], U32)
            alph = fsb("alph", [128, 4], F32)
            xg = fsb("xg", [128, 2, DM], F32)
            fq4 = fsb("fq4", [128, 2, DM], F32)
            oq4 = fsb("oq4", [128, 2, DM], F32)
            stat5 = fsb("stat5", [128, 24], F32)
            mv5 = fsb("mv5", [128, 2, 2, 2], F32)
            sj5 = fsb("sj5", [128, 2, 2], F32)
            rstd5 = fsb("rstd5", [128, 2], F32)
            oobu = fsb("oobu", [128, 1], U32)
            PT4 = [(pA, pB), (pC, pD)]

            k.new_block()
            REGS = {}

            def mk_setup(eng_name):
                def setup(e):
                    REGS[eng_name] = (e.alloc_register(name="rns" + eng_name),
                                      e.alloc_register(name="rs" + eng_name),
                                      e.alloc_register(name="rnsE" + eng_name),
                                      e.alloc_register(name="rsE" + eng_name))
                return setup
            for en_ in ("sync", "tensor", "vector", "scalar", "gpsimd"):
                k.emit(en_, mk_setup(en_))

            def p4_oob(v):
                v.tensor_copy(oobu[:, :], rmsent[:, 0:1]).then_inc(s_ln.h, 1)
            OOB_MS = s_ln.bump()
            k.emit("vector", p4_oob)

            def p4_pro(g):
                g.wait_ge(s_st.h, P2_ST_DONE)
                g.wait_ge(s_ln.h, OOB_MS)
            k.emit("gpsimd", p4_pro)

            def cond(e_obj, eng_name, ex, s):
                rns, rs = REGS[eng_name][0], REGS[eng_name][1]
                if s == 0:
                    e_obj.reg_load(rns, nse_i[0:1, ex:ex + 1])
                e_obj.reg_mov(rs, s)
                return e_obj.If_lt(rs, rns)

            def condE(e_obj, eng_name, sgp):
                exP, sP = sgp // NSC, sgp % NSC
                rns, rs = REGS[eng_name][2], REGS[eng_name][3]
                e_obj.reg_load(rns, nse_i[0:1, exP:exP + 1])
                e_obj.reg_mov(rs, sP)
                return e_obj.If_lt(rs, rns)

            B2_AT = {}
            W_AT = {}
            ZN_AT = {}
            RM_AT = {}
            IDX_MS = {}
            ZREC_MS = {}
            FQ_MS = {}
            BN_MS = {}
            SQ5_MS = {}
            OQ_MS = {}
            ALPH_MS = {}
            GX_AT = {}
            SC_AT = {}
            KRELU = 11

            def emit_epi_a(sgp):
                def f_epi_a(v, sgp=sgp):
                    with condE(v, "vector", sgp):
                        v.wait_ge(s_x.h, RM_AT[sgp])
                        if sgp >= 1:
                            v.wait_ge(s_sc.h, SC_AT[sgp - 1])
                        for gi in range(2):
                            op = v.tensor_scalar(oq4[:, gi, :], znl[:, sgp % 3, gi, :],
                                                 rmb[:, sgp % 2, gi, 3:4],
                                                 rmb[:, sgp % 2, gi, 2:3],
                                                 op0=ALU.mult, op1=ALU.add)
                        op.then_inc(s_ln.h, 1)
                    with v.Else():
                        v.sem_inc(s_ln.h, 1)
                ZREC_MS[sgp] = s_ln.bump()
                k.emit("vector", f_epi_a)

            def emit_epi_b(sgp):
                def f_epi(v, sgp=sgp):
                    with condE(v, "vector", sgp):
                        v.wait_ge(s_te.h, sgp + 1)
                        v.wait_ge(s_sq.h, ALPH_MS[sgp])
                        for gi in range(2):
                            op = v.scalar_tensor_tensor(
                                fq4[:, gi, :], t_sb[:, sgp % 2, gi, :],
                                alph[:, 2 * (sgp % 2) + gi:2 * (sgp % 2) + gi + 1],
                                oq4[:, gi, :], op0=ALU.mult, op1=ALU.add)
                        op.then_inc(s_ln.h, 1)
                        v.wait_ge(s_ln.h, FQ_MS[sgp])
                        for gi in range(2):
                            v.bn_stats(stat5[:, 12 * gi:12 * gi + 6],
                                       fq4[:, gi, 0:512])
                            op = v.bn_stats(stat5[:, 12 * gi + 6:12 * gi + 12],
                                            fq4[:, gi, 512:1024])
                        op.then_inc(s_ln.h, 1)
                        v.wait_ge(s_ln.h, STAT_E[sgp])
                        v.bn_aggr(mv5[:, sgp % 2, 0, :], stat5[:, 0:12])
                        v.bn_aggr(mv5[:, sgp % 2, 1, :],
                                  stat5[:, 12:24]).then_inc(s_ln.h, 1)
                        v.wait_ge(s_sq.h, SQ5_MS[sgp])
                        v.wait_ge(s_ln.h, BN_MS[sgp])
                        v.reciprocal(rstd5[:, :],
                                     sj5[:, sgp % 2, :]).then_inc(s_ln.h, 1)
                        v.wait_ge(s_ln.h, RC_E[sgp])
                        v.wait_ge(s_sc.h, GX_AT[sgp])
                        for gi in range(2):
                            v.tensor_scalar(oq4[:, gi, :], fq4[:, gi, :],
                                            mv5[:, sgp % 2, gi, 0:1],
                                            rstd5[:, gi:gi + 1],
                                            op0=ALU.subtract, op1=ALU.mult)
                            op = v.tensor_tensor(oq4[:, gi, :], oq4[:, gi, :],
                                                 xg[:, gi, :], op=ALU.add)
                        op.then_inc(s_ln.h, 1)
                    with v.Else():
                        v.sem_inc(s_ln.h, 5)
                FQ_MS[sgp] = s_ln.bump()
                STAT_E[sgp] = s_ln.bump()
                BN_MS[sgp] = s_ln.bump()
                RC_E[sgp] = s_ln.bump()
                OQ_MS[sgp] = s_ln.bump()
                k.emit("vector", f_epi)

            def emit_sq5(sgp):
                def f_sq5(sc, sgp=sgp):
                    with condE(sc, "scalar", sgp):
                        sc.wait_ge(s_ln.h, BN_MS[sgp])
                        sc.activation(sj5[:, sgp % 2, :], mv5[:, sgp % 2, :, 1:2],
                                      AF.Sqrt, bias=epsc[:, 0:1],
                                      scale=1.0).then_inc(s_sq.h, 1)
                    with sc.Else():
                        sc.sem_inc(s_sq.h, 1)
                SQ5_MS[sgp] = s_sq.bump()
                k.emit("scalar", f_sq5)

            def emit_scat(sgp):
                def f_scat(g, sgp=sgp):
                    with condE(g, "gpsimd", sgp):
                        g.wait_ge(s_ln.h, OQ_MS[sgp])
                        for gi in range(2):
                            g.indirect_dma_start(
                                out=out_d, out_offset=IndirectOffsetOnAxis(
                                    idxu[:, 2 * (sgp % 2) + gi:
                                         2 * (sgp % 2) + gi + 1], 0),
                                in_=oq4[:, gi, :], in_offset=None,
                                bounds_check=_breg["rows"],
                                oob_is_err=False).then_inc(s_sc.h, 16)
                    with g.Else():
                        for gi in range(2):
                            g.indirect_dma_start(
                                out=out_d, out_offset=IndirectOffsetOnAxis(
                                    oobu[:, 0:1], 0),
                                in_=oq4[:, gi, :], in_offset=None,
                                bounds_check=_breg["rows"],
                                oob_is_err=False).then_inc(s_sc.h, 16)
                SC_AT[sgp] = s_sc.bump(32)
                k.emit("gpsimd", f_scat)

            STAT_E = {}
            RC_E = {}

            for ex in range(E):
                def f_b2(g, ex=ex):
                    if ex >= 2:
                        g.wait_ge(s_w2.h, NFC * NSC * (ex - 1))
                    g.dma_start(b2row[:, ex % 2, :],
                                b2_d[ex:ex + 1, :]).then_inc(s_gq.h, 16)
                B2_AT[ex] = s_gq.bump(16)
                k.emit("gpsimd", f_b2)

                def f_wload(sync, ex=ex):
                    if ex >= 1:
                        sync.wait_ge(s_w1.h, NFC * NSC * ex)
                        sync.wait_ge(s_w2.h, NFC * NSC * ex)
                    if ex >= 1:
                        sync.wait_ge(s_ln.h, ZREC_MS[NSC * ex - 1])
                    for s in range(2):
                        sg = NSC * ex + s
                        if sg >= 2:
                            sync.wait_ge(s_trg.h, 2 * (sg - 1))
                        for gi in range(2):
                            sync.dma_start(
                                znl[:, sg % 3, gi, :],
                                znbkt[ex * CAP + s * 256 + gi * 128:
                                      ex * CAP + s * 256 + gi * 128 + 128, :]
                            ).then_inc(s_zn.h, 16)
                    if ex >= 1:
                        for d in range(8):
                            sync.dma_start(W1T[:, d, :],
                                           w1b[ex, :, d * 128:(d + 1) * 128],
                                           transpose=True).then_inc(s_w.h, 16)
                        for f in range(NFC):
                            sync.dma_start(W2T[:, f, :],
                                           w2b[ex, :, f * 128:(f + 1) * 128],
                                           transpose=True).then_inc(s_w.h, 16)
                if ex == 0:
                    W_BASE = W0_BASE
                else:
                    s_w.bump(16 * 40)
                    W_BASE = s_w.v - 16 * 40
                ZN_AT[NSC * ex] = s_zn.bump(32)
                ZN_AT[NSC * ex + 1] = s_zn.bump(32)
                W_AT[ex] = W_BASE + 16 * 40
                if ex == 0:
                    W_BASES = {}
                W_BASES[ex] = W_BASE
                k.emit("sync", f_wload)

                def f_tpro(t, ex=ex):
                    t.wait_ge(s_w.h, W_BASES[ex] + 16 * 8)
                    for gi in range(2):
                        sg = NSC * ex
                        grp = 2 * sg + gi
                        pT = pTrb if grp % 2 == 0 else pTr2
                        t.wait_ge(s_zn.h, ZN_AT[sg])
                        if grp >= 2:
                            t.wait_ge(s_znT.h, grp - 1)
                        for d in range(8):
                            mm = t.transpose(pT[:, d, :],
                                             znl[:, sg % 3, gi, d * 128:(d + 1) * 128],
                                             id128[:, :])
                        mm.then_inc(s_trg.h, 1)
                s_trg.bump(2)
                k.emit("tensor", f_tpro)

                for s in range(NSC):
                    sg = NSC * ex + s

                    has_pre = s + 2 < NSC

                    def f_sync(sync, ex=ex, s=s, sg=sg, has_pre=has_pre):
                        with cond(sync, "sync", ex, s):
                            if sg >= 2:
                                sync.wait_ge(s_ln.h, ZREC_MS[sg - 2])
                            for gi in range(2):
                                sync.dma_start(
                                    rmb[:, sg % 2, gi, :],
                                    rmtab[ex * CAP + s * 256 + gi * 128:
                                          ex * CAP + s * 256 + gi * 128 + 128, :]
                                ).then_inc(s_x.h, 16)
                            if has_pre:
                                if sg >= 1:
                                    sync.wait_ge(s_trg.h, 2 * sg)
                                    sync.wait_ge(s_ln.h, ZREC_MS[sg - 1])
                                for gi in range(2):
                                    sync.dma_start(
                                        znl[:, (sg + 2) % 3, gi, :],
                                        znbkt[ex * CAP + (s + 2) * 256 + gi * 128:
                                              ex * CAP + (s + 2) * 256 + gi * 128 + 128, :]
                                    ).then_inc(s_zn.h, 16)
                        with sync.Else():
                            sync.sem_inc(s_x.h, 32)
                            if has_pre:
                                sync.sem_inc(s_zn.h, 32)
                    RM_AT[sg] = s_x.bump(32)
                    if has_pre:
                        ZN_AT[sg + 2] = s_zn.bump(32)
                    k.emit("sync", f_sync)

                    has_tr = s + 1 < NSC

                    def f_tensor(t, ex=ex, s=s, sg=sg, has_tr=has_tr):
                        def w2_group(kk):
                            ff = kk % NFC
                            t.wait_ge(s_he.h, kk + 1)
                            if s == 0:
                                t.wait_ge(s_w.h, W_BASES[ex] + 16 * (8 + ff + 1))
                            if ff == 0:
                                if sg >= 1:
                                    t.wait_ge(s_te.h, sg)
                                t.wait_ge(s_gq.h, B2_AT[ex])
                                for sg2 in range(2):
                                    for dh in range(2):
                                        t.matmul(PT4[sg2][dh][:, :], ones1r[:, :],
                                                 b2row[0:1, ex % 2, dh * 512:(dh + 1) * 512],
                                                 start=True, stop=False)
                            for sg2 in range(2):
                                for dh in range(2):
                                    mm = t.matmul(PT4[sg2][dh][:, :],
                                                  hTt[:, kk % 2, sg2 * 128:(sg2 + 1) * 128],
                                                  W2T[:, ff, dh * 512:(dh + 1) * 512],
                                                  start=False, stop=(ff == NFC - 1))
                            mm.then_inc(s_w2.h, 1)

                        def tr_group(sgp, gi):
                            grp = 2 * sgp + gi
                            pT = pTrb if grp % 2 == 0 else pTr2
                            t.wait_ge(s_zn.h, ZN_AT[sgp])
                            t.wait_ge(s_znT.h, grp - 1)
                            for d in range(8):
                                mm = t.transpose(pT[:, d, :],
                                                 znl[:, sgp % 3, gi, d * 128:(d + 1) * 128],
                                                 id128[:, :])
                            mm.then_inc(s_trg.h, 1)

                        with cond(t, "tensor", ex, s):
                            t.wait_ge(s_znT.h, 2 * (sg + 1))
                            for f in range(NFC):
                                kk = NFC * sg + f
                                pw = pE_ if kk % 2 == 0 else pF_
                                if kk >= 2:
                                    t.wait_ge(s_he.h, kk - 1)
                                for d in range(8):
                                    mm = t.matmul(pw[:, 0:256],
                                                  W1T[:, d, f * 128:(f + 1) * 128],
                                                  znTt[:, sg % 2, d, :],
                                                  start=(d == 0), stop=(d == 7))
                                mm.then_inc(s_w1.h, 1)
                                if has_tr and f == 26:
                                    tr_group(sg + 1, 0)
                                if has_tr and f == 28:
                                    tr_group(sg + 1, 1)
                                if f >= 1:
                                    w2_group(kk - 1)
                            w2_group(NFC * sg + NFC - 1)
                        with t.Else():
                            t.sem_inc(s_w1.h, NFC)
                            t.sem_inc(s_w2.h, NFC)
                            if has_tr:
                                t.sem_inc(s_trg.h, 2)
                    s_w1.bump(NFC)
                    s_w2.bump(NFC)
                    if has_tr:
                        s_trg.bump(2)
                    k.emit("tensor", f_tensor)

                    def f_vector(v, ex=ex, s=s, sg=sg):
                        with cond(v, "vector", ex, s):
                            if sg >= 2:
                                v.wait_ge(s_w1.h, NFC * (sg - 1))
                            for gi in range(2):
                                grp = 2 * sg + gi
                                pT = pTrb if grp % 2 == 0 else pTr2
                                v.wait_ge(s_trg.h, grp + 1)
                                for d in range(8):
                                    op = v.tensor_scalar(
                                        znTt[:, sg % 2, d, gi * 128:(gi + 1) * 128],
                                        pT[:, d, :], GT[:, d, ex:ex + 1],
                                        BT[:, d, ex:ex + 1],
                                        op0=ALU.mult, op1=ALU.add)
                                op.then_inc(s_znT.h, 1)
                            v.wait_ge(s_x.h, RM_AT[sg])
                            if sg >= 2:
                                v.wait_ge(s_sc.h, SC_AT[sg - 2])
                            v.tensor_copy(idxu[:, 2 * (sg % 2):2 * (sg % 2) + 1],
                                          rmb[:, sg % 2, 0, 0:1])
                            v.tensor_copy(idxu[:, 2 * (sg % 2) + 1:2 * (sg % 2) + 2],
                                          rmb[:, sg % 2, 1, 0:1]).then_inc(s_ln.h, 1)
                        with v.Else():
                            v.sem_inc(s_znT.h, 2)
                            v.sem_inc(s_ln.h, 1)
                    if sg >= 1:
                        emit_epi_a(sg - 1)
                    s_znT.bump(2)
                    IDX_MS[sg] = s_ln.bump()
                    k.emit("vector", f_vector)

                    if sg >= 1:
                        emit_epi_b(sg - 1)

                    def f_scalar_a(sc, ex=ex, s=s, sg=sg):
                        with cond(sc, "scalar", ex, s):
                            sc.wait_ge(s_x.h, RM_AT[sg])
                            sc.activation(alph[:, 2 * (sg % 2):2 * (sg % 2) + 2],
                                          rmb[:, sg % 2, :, 1:2],
                                          AF.Sigmoid).then_inc(s_sq.h, 1)
                            for f in range(KRELU):
                                kk = NFC * sg + f
                                pw = pE_ if kk % 2 == 0 else pF_
                                sc.wait_ge(s_w1.h, kk + 1)
                                if kk >= 2:
                                    sc.wait_ge(s_w2.h, kk - 1)
                                sc.activation(hTt[:, kk % 2, :], pw[:, 0:256], AF.Relu,
                                              bias=B1T[:, f, ex:ex + 1],
                                              scale=1.0).then_inc(s_he.h, 1)
                        with sc.Else():
                            sc.sem_inc(s_sq.h, 1)
                            sc.sem_inc(s_he.h, KRELU)
                    ALPH_MS[sg] = s_sq.bump()
                    s_he.bump(KRELU)
                    k.emit("scalar", f_scalar_a)

                    if sg >= 1:
                        emit_sq5(sg - 1)

                    def f_scalar_b(sc, ex=ex, s=s, sg=sg):
                        with cond(sc, "scalar", ex, s):
                            for f in range(KRELU, NFC):
                                kk = NFC * sg + f
                                pw = pE_ if kk % 2 == 0 else pF_
                                sc.wait_ge(s_w1.h, kk + 1)
                                sc.wait_ge(s_w2.h, kk - 1)
                                sc.activation(hTt[:, kk % 2, :], pw[:, 0:256], AF.Relu,
                                              bias=B1T[:, f, ex:ex + 1],
                                              scale=1.0).then_inc(s_he.h, 1)
                            sc.wait_ge(s_w2.h, NFC * (sg + 1))
                            if sg >= 2:
                                sc.wait_ge(s_ln.h, FQ_MS[sg - 2])
                            for sg2 in range(2):
                                for dh in range(2):
                                    op = sc.copy(t_sb[:, sg % 2, sg2, dh * 512:(dh + 1) * 512],
                                                 PT4[sg2][dh][:, :])
                            op.then_inc(s_te.h, 1)
                        with sc.Else():
                            sc.sem_inc(s_he.h, NFC - KRELU)
                            sc.sem_inc(s_te.h, 1)
                    s_he.bump(NFC - KRELU)
                    s_te.bump(1)
                    k.emit("scalar", f_scalar_b)

                    def f_gp(g, ex=ex, s=s, sg=sg):
                        with cond(g, "gpsimd", ex, s):
                            g.wait_ge(s_ln.h, IDX_MS[sg])
                            if sg >= 1:
                                g.wait_ge(s_ln.h, OQ_MS[sg - 1])
                            for gi in range(2):
                                g.indirect_dma_start(
                                    out=xg[:, gi, :], out_offset=None, in_=x1s,
                                    in_offset=IndirectOffsetOnAxis(
                                        idxu[:, 2 * (sg % 2) + gi:
                                             2 * (sg % 2) + gi + 1], 0),
                                    bounds_check=_breg["rows"],
                                    oob_is_err=False).then_inc(s_sc.h, 16)
                        with g.Else():
                            if sg >= 1:
                                g.wait_ge(s_ln.h, OQ_MS[sg - 1])
                            for gi in range(2):
                                g.indirect_dma_start(
                                    out=xg[:, gi, :], out_offset=None, in_=x1s,
                                    in_offset=IndirectOffsetOnAxis(oobu[:, 0:1], 0),
                                    bounds_check=_breg["rows"],
                                    oob_is_err=False).then_inc(s_sc.h, 16)
                    GX_AT[sg] = s_sc.bump(32)
                    k.emit("gpsimd", f_gp)

                    if sg >= 1:
                        emit_scat(sg - 1)

            SGL = NSC * E - 1
            emit_epi_a(SGL)
            emit_epi_b(SGL)
            emit_sq5(SGL)
            emit_scat(SGL)
            if debug_stage == "epi":
                fin_sc2 = s_sc.v
                def dmp_epi(sync, fin_sc2=fin_sc2):
                    sync.wait_ge(s_sc.h, fin_sc2)
                    sync.dma_start(d_oq, oq4[:, :, :]).then_inc(s_x.h, 16)
                    sync.dma_start(d_fq, fq4[:, :, :]).then_inc(s_x.h, 16)
                    sync.dma_start(d_xg, xg[:, :, :]).then_inc(s_x.h, 16)
                    sync.dma_start(d_ts, t_sb[:, :, :, :]).then_inc(s_x.h, 16)
                    sync.dma_start(d_rm2, rmb[:, :, :, :]).then_inc(s_x.h, 16)
                    sync.dma_start(d_al2, alph[:, :]).then_inc(s_x.h, 16)
                    sync.dma_start(d_ix, idxu[:, :]).then_inc(s_x.h, 16)
                    sync.dma_start(d_mv, mv5[:, :, :, :]).then_inc(s_x.h, 16)
                    sync.dma_start(d_zl, znl[:, :, :, :]).then_inc(s_x.h, 16)
                DMP_AT = s_x.bump(16 * 9)
                def dmp_fin(sync, DMP_AT=DMP_AT):
                    sync.wait_ge(s_x.h, DMP_AT)
                k.emit("sync", dmp_epi)
                k.emit("sync", dmp_fin)
            k.run_block()

        k.new_block()
        fin_sc = s_sc.v
        fin_st = s_st.v
        def q_fin(sync):
            sync.wait_ge(s_sc.h, fin_sc)
            sync.wait_ge(s_st.h, fin_st)
        k.emit("sync", q_fin)
        k.run_block()

        if debug_stage == "dump":
            with nc.sbuf_tensor("dmpbuf", [128, 2, DM], BF16) as dmpbuf:
                k.new_block()
                base = s_x.v

                def dmp(sync, base=base):
                    n = 0
                    for cc2 in range(E * CAP // 128):
                        if n >= 2:
                            sync.wait_ge(s_x.h, base + 16 * (2 * (n - 1)))
                        sync.dma_start(dmpbuf[:, n % 2, :],
                                       znbkt[128 * cc2:128 * (cc2 + 1), :]
                                       ).then_inc(s_x.h, 16)
                        sync.wait_ge(s_x.h, base + 32 * n + 16)
                        sync.dma_start(d_zn[128 * cc2:128 * (cc2 + 1), :],
                                       dmpbuf[:, n % 2, :]).then_inc(s_x.h, 16)
                        n += 1
                    sync.dma_start(d_o[:, 0, :], o_all[:, :]).then_inc(s_x.h, 16)
                    sync.dma_start(d_o[:, 1, :], o_all[:, :]).then_inc(s_x.h, 16)
                    sync.dma_start(d_rk[:, :, :], dbg_rk[:, :, :]).then_inc(s_x.h, 16)
                    sync.wait_ge(s_x.h, base + 32 * n + 48)
                s_x.bump(32 * (E * CAP // 128) + 48)
                k.emit("sync", dmp)
                k.run_block()

    return nc


def kernel(**inputs):
    x = np.ascontiguousarray(inputs["L0_attn_x"], np.float32)
    nc = build_program()
    in_maps = []
    for c in range(8):
        shard = {
            "x": np.ascontiguousarray(x[4 * c:4 * (c + 1)].reshape(ROWS, DM)),
            "wqkv": np.ascontiguousarray(inputs["L0_attn_w_qkv"], np.float32),
            "wproj": np.ascontiguousarray(inputs["L0_attn_w_proj"], np.float32),
            "cent": np.ascontiguousarray(inputs["L0_ffn_centroids"], np.float32),
            "lng": np.ascontiguousarray(inputs["L0_ffn_ln_g"], np.float32),
            "lnb": np.ascontiguousarray(inputs["L0_ffn_ln_b"], np.float32),
            "w1": np.ascontiguousarray(inputs["L0_ffn_w1"], np.float32),
            "b1": np.ascontiguousarray(inputs["L0_ffn_b1"], np.float32),
            "w2": np.ascontiguousarray(inputs["L0_ffn_w2"], np.float32),
            "b2": np.ascontiguousarray(inputs["L0_ffn_b2"], np.float32),
        }
        in_maps.append(shard)
    res = run_bass_kernel_spmd(nc, in_maps, core_ids=list(range(8)))
    out = np.concatenate([r["out"].reshape(4, N, DM) for r in res.results], axis=0)
    return out


# version 18 (final)
# speedup vs baseline: 1.0324x; 1.0324x over previous
import numpy as np
import ml_dtypes
from contextlib import ExitStack

import concourse.bass as bass
import concourse.mybir as mybir
from concourse.bass_utils import run_bass_kernel_spmd
from bass_rust import IndirectOffsetOnAxis

F32 = mybir.dt.float32
BF16 = mybir.dt.bfloat16
U32 = mybir.dt.uint32
AF = mybir.ActivationFunctionType
ALU = mybir.AluOpType
AX = mybir.AxisListType

B = 4
N = 2048
DM = 1024
H = 16
PAIRS = 8
HD = 64
MLM = 64
SEG = 32
ROWS = B * N
NCH = ROWS // 128
E = 8
FF = 4096
CAP = 4096
NSC = CAP // 256
NFC = FF // 128
EPS = 1e-5

ENGS = ("sync", "tensor", "vector", "scalar", "gpsimd")


class Sem:
    def __init__(self, stack, nc, name):
        self.h = stack.enter_context(nc.semaphore(name))
        self.v = 0

    def bump(self, k=1):
        self.v += k
        return self.v


class MK:
    def __init__(self, nc, stack, debug_stage=None):
        self.nc = nc
        self.stack = stack
        self.debug_stage = debug_stage
        self.q = None

    def new_block(self):
        self.q = {k: [] for k in ENGS}

    def emit(self, eng, fn):
        self.q[eng].append(fn)

    def run_block(self):
        q = self.q
        with self.nc.Block() as blk:
            for eng in ENGS:
                fns = q[eng]
                if not fns:
                    continue
                def body(e, fns=fns):
                    for fn in fns:
                        fn(e)
                getattr(blk, eng)(body)
        self.q = None

    _uid = 0

    def uname(self, name):
        MK._uid += 1
        return f"{name}_u{MK._uid}"

    def sb(self, name, shape, dt):
        return self.stack.enter_context(self.nc.sbuf_tensor(name, shape, dt))

    def ps(self, name, shape, dt):
        return self.stack.enter_context(self.nc.psum_tensor(name, shape, dt))

    def sem(self, name):
        return Sem(self.stack, self.nc, name)


def build_program(debug_stage=None):
    nc = bass.Bass("TRN2", target_bir_lowering=False, detect_race_conditions=False)

    x_d = nc.dram_tensor("x", [ROWS, DM], F32, kind="ExternalInput").ap()
    wqkv_d = nc.dram_tensor("wqkv", [DM, 3 * DM], F32, kind="ExternalInput").ap()
    wproj_d = nc.dram_tensor("wproj", [DM, DM], F32, kind="ExternalInput").ap()
    cent_d = nc.dram_tensor("cent", [E, DM], F32, kind="ExternalInput").ap()
    lng_d = nc.dram_tensor("lng", [E, DM], F32, kind="ExternalInput").ap()
    lnb_d = nc.dram_tensor("lnb", [E, DM], F32, kind="ExternalInput").ap()
    w1_d = nc.dram_tensor("w1", [E, FF, DM], F32, kind="ExternalInput").ap()
    b1_d = nc.dram_tensor("b1", [E, FF], F32, kind="ExternalInput").ap()
    w2_d = nc.dram_tensor("w2", [E, DM, FF], F32, kind="ExternalInput").ap()
    b2_d = nc.dram_tensor("b2", [E, DM], F32, kind="ExternalInput").ap()
    out_d = nc.dram_tensor("out", [ROWS, DM], F32, kind="ExternalOutput").ap()

    dbg_d = None
    if debug_stage in ("p2abd",):
        dbg_d = nc.dram_tensor("dbg", [128, 4, 2048], BF16, kind="ExternalOutput").ap()
    x1_kind = "ExternalOutput" if debug_stage == "attn" else "Internal"
    x1s = nc.dram_tensor("x1s", [ROWS, DM], F32, kind=x1_kind).ap()
    znbkt = nc.dram_tensor("znbkt", [E * CAP, DM], BF16, kind="Internal").ap()
    rmtab = nc.dram_tensor("rmtab", [E * CAP + 128, 4], F32, kind="Internal").ap()
    if debug_stage == "dump":
        d_zn = nc.dram_tensor("d_zn", [E * CAP, DM], BF16, kind="ExternalOutput").ap()
        d_o = nc.dram_tensor("d_o", [128, 2, NCH], U32, kind="ExternalOutput").ap()
        d_rk = nc.dram_tensor("d_rk", [128, 4, NCH], F32, kind="ExternalOutput").ap()
    if debug_stage == "epi":
        d_oq = nc.dram_tensor("d_oq", [128, 2, DM], F32, kind="ExternalOutput").ap()
        d_fq = nc.dram_tensor("d_fq", [128, 2, DM], F32, kind="ExternalOutput").ap()
        d_xg = nc.dram_tensor("d_xg", [128, 2, DM], F32, kind="ExternalOutput").ap()
        d_ts = nc.dram_tensor("d_ts", [128, 2, 2, DM], BF16, kind="ExternalOutput").ap()
        d_rm2 = nc.dram_tensor("d_rm2", [128, 2, 2, 4], F32, kind="ExternalOutput").ap()
        d_al2 = nc.dram_tensor("d_al2", [128, 4], F32, kind="ExternalOutput").ap()
        d_ix = nc.dram_tensor("d_ix", [128, 4], U32, kind="ExternalOutput").ap()
        d_mv = nc.dram_tensor("d_mv", [128, 2, 2, 2], F32, kind="ExternalOutput").ap()
        d_zl = nc.dram_tensor("d_zl", [128, 3, 2, DM], BF16, kind="ExternalOutput").ap()
    w1b = nc.dram_tensor("w1b", [E, FF, DM], BF16, kind="Internal").ap()
    w2b = nc.dram_tensor("w2b", [E, DM, FF], BF16, kind="Internal").ap()

    with ExitStack() as stack:
        k = MK(nc, stack, debug_stage)
        sb, ps, sem = k.sb, k.ps, k.sem

        id128 = sb("id128", [128, 128], BF16)
        id64s = sb("id64s", [128, 64], BF16)
        id128f = sb("id128f", [128, 128], F32)
        ltu = sb("ltu", [128, 128], F32)
        ones128f = sb("ones128f", [128, 1], F32)
        iota8 = sb("iota8", [128, 8], F32)
        ones128 = sb("ones128", [128, 1], BF16)
        ones1r = sb("ones1r", [1, 128], BF16)
        ones1f = sb("ones1f", [1, 128], F32)
        sel0 = sb("sel0", [1, 128], F32)
        sel1 = sb("sel1", [1, 128], F32)
        sel01 = sb("sel01", [2, 128], F32)
        selT = sb("selT", [128, 2], F32)
        cIs = sb("cIs", [128, 3, 64], F32)
        CT = sb("CT", [128, 8, E], F32)
        GT = sb("GT", [128, 8, E], F32)
        BT = sb("BT", [128, 8, E], F32)
        B1T = sb("B1T", [128, NFC, E], F32)
        dbg_rk = sb("dbg_rk", [128, 4, NCH], F32)
        o_all = sb("o_all", [128, NCH], U32)
        rmsent = sb("rmsent", [128, 4], F32)
        riota = sb("riota", [128, NCH], F32)
        Rrow = sb("Rrow", [1, E], F32)
        nse_f = sb("nse_f", [1, E], F32)
        nse_i = sb("nse_i", [1, E], mybir.dt.int32)
        epsc = sb("epsc", [128, 1], F32)
        zot = sb("zot", [128, DM], BF16)

        pA = ps("pA", [128, 512], F32)
        pB = ps("pB", [128, 512], F32)
        pC = ps("pC", [128, 512], F32)
        pD = ps("pD", [128, 512], F32)
        pE_ = ps("pE", [128, 512], F32)
        pF_ = ps("pF", [128, 512], F32)
        pTrb = ps("pTrb", [128, 8, 128], BF16)
        pTr2 = ps("pTr2", [128, 8, 128], BF16)

        s_x = sem("s_x")
        s_st = sem("s_st")
        s_gq = sem("s_gq")
        s_wp = sem("s_wp")
        s_ln = sem("s_ln")
        s_sq = sem("s_sq")
        s_tp = sem("s_tp")
        s_ev = sem("s_ev")
        s_ev2 = sem("s_ev2")
        s_mm = sem("s_mm")
        s_nt = sem("s_nt")
        s_nv = sem("s_nv")
        s_gp = sem("s_gp")
        s_rt = sem("s_rt")
        s_w = sem("s_w")
        s_zn = sem("s_zn")
        s_trg = sem("s_trg")
        s_znT = sem("s_znT")
        s_w1 = sem("s_w1")
        s_he = sem("s_he")
        s_w2 = sem("s_w2")
        s_te = sem("s_te")
        s_sc = sem("s_sc")

        wprep_units = [("w1", e) for e in range(E)] + [("w2", e) for e in range(E)]

        def emit_wprep(units):
            def body(g, units=units):
                for kind, e in units:
                    if kind == "w1":
                        g.dma_start(w1b[e, :, :], w1_d[e, :, :]).then_inc(s_wp.h, 16)
                    else:
                        g.dma_start(w2b[e, :, :], w2_d[e, :, :]).then_inc(s_wp.h, 16)
            s_wp.bump(16 * len(units))
            return body

        p0stk = ExitStack()
        rowbuf = p0stk.enter_context(nc.sbuf_tensor("rowbuf", [1, 4 * DM], F32))
        k.new_block()

        def p0_g(g):
            def ident_into(t, val=1.0, halfsize=None):
                g.memset(t, 0.0)
                if halfsize is None:
                    sq = t.shape[-1]
                    g.affine_select(out=t, in_=t, compare_op=ALU.not_equal, fill=val,
                                    base=0, pattern=[[-1, sq]], channel_multiplier=1)
            ident_into(id128[:, :])
            ident_into(id128f[:, :])
            g.memset(id64s[:, :], 0.0)
            for hh in range(2):
                g.affine_select(out=id64s[hh * 64:(hh + 1) * 64, :],
                                in_=id64s[hh * 64:(hh + 1) * 64, :],
                                compare_op=ALU.not_equal, fill=1.0, base=0,
                                pattern=[[-1, 64]], channel_multiplier=1)
            g.memset(ltu[:, :], 1.0)
            g.affine_select(out=ltu[:, :], in_=ltu[:, :], compare_op=ALU.is_ge,
                            fill=0.0, base=-1, pattern=[[1, 128]], channel_multiplier=-1)
            g.iota(iota8[:, :], [[1, 8]], base=0, channel_multiplier=0,
                   allow_small_or_imprecise_dtypes=True)
            g.memset(rmsent[:, :], 1.0e9)
            g.iota(riota[:, :], [[128, NCH]], base=0, channel_multiplier=1,
                   allow_small_or_imprecise_dtypes=True)
            g.memset(ones128[:, :], 1.0)
            g.memset(ones128f[:, :], 1.0)
            g.memset(ones1r[:, :], 1.0)
            g.memset(ones1f[:, :], 1.0)
            g.memset(sel0[:, :], 1.0)
            g.affine_select(out=sel0[:, :], in_=sel0[:, :], compare_op=ALU.is_ge,
                            fill=0.0, base=63, pattern=[[-1, 128]], channel_multiplier=0)
            g.memset(sel1[:, :], 1.0)
            g.affine_select(out=sel1[:, :], in_=sel1[:, :], compare_op=ALU.is_ge,
                            fill=0.0, base=-64, pattern=[[1, 128]], channel_multiplier=0)
            g.memset(selT[:, :], 1.0)
            g.affine_select(out=selT[:, 0:1], in_=selT[:, 0:1], compare_op=ALU.is_ge,
                            fill=0.0, base=63, pattern=[[1, 1]], channel_multiplier=-1)
            g.affine_select(out=selT[:, 1:2], in_=selT[:, 1:2], compare_op=ALU.is_ge,
                            fill=0.0, base=-64, pattern=[[1, 1]], channel_multiplier=1)
            g.memset(cIs[:, :, :], 0.0)
            for j, c in enumerate([7.0, 15.0, 13.0]):
                for hh in range(2):
                    g.affine_select(out=cIs[hh * 64:(hh + 1) * 64, j, :],
                                    in_=cIs[hh * 64:(hh + 1) * 64, j, :],
                                    compare_op=ALU.not_equal, fill=c, base=0,
                                    pattern=[[-1, 64]], channel_multiplier=1)
            g.memset(Rrow[:, :], 0.0)
            g.memset(epsc[:, :], EPS)
            g.memset(zot[:, :], 0.0)
            g.nop().then_inc(s_gp.h, 1)
        P0G = s_gp.bump()
        k.emit("gpsimd", p0_g)

        p0_plan = []
        ev_ms = []
        for e in range(E):
            p0_plan.append(("4rows", e))
        for seg in range(1, 4):
            for e in range(E):
                p0_plan.append(("b1seg", e, seg))

        def p0_sync(sync):
            for it, item in enumerate(p0_plan):
                if it >= 1:
                    sync.wait_ge(s_ev.h, it)
                if item[0] == "4rows":
                    e = item[1]
                    sync.dma_start(rowbuf[0:1, 0:DM], cent_d[e:e + 1, :]).then_inc(s_x.h, 16)
                    sync.dma_start(rowbuf[0:1, DM:2 * DM], lng_d[e:e + 1, :]).then_inc(s_x.h, 16)
                    sync.dma_start(rowbuf[0:1, 2 * DM:3 * DM], lnb_d[e:e + 1, :]).then_inc(s_x.h, 16)
                    sync.dma_start(rowbuf[0:1, 3 * DM:4 * DM], b1_d[e:e + 1, 0:1024]).then_inc(s_x.h, 16)
                else:
                    e, seg = item[1], item[2]
                    sync.dma_start(rowbuf[0:1, 0:DM],
                                   b1_d[e:e + 1, seg * 1024:(seg + 1) * 1024]).then_inc(s_x.h, 16)
        nload = 0
        p0_loadcnt = []
        for item in p0_plan:
            nload += 4 if item[0] == "4rows" else 1
            p0_loadcnt.append(nload)
        s_x.bump(16 * nload)

        def p0_selT_t(t):
            t.wait_ge(s_gp.h, P0G)
            t.transpose(pC[0:2, 0:128], selT[:, 0:2],
                        id128f[:, :]).then_inc(s_mm.h, 1)
        SELT_MM = s_mm.bump()
        k.emit("tensor", p0_selT_t)

        def p0_selT_v(v):
            v.wait_ge(s_mm.h, SELT_MM)
            v.tensor_copy(sel01[0:2, :], pC[0:2, 0:128]).then_inc(s_ln.h, 1)
        s_ln.bump()
        k.emit("vector", p0_selT_v)

        def p0_t(t):
            for it, item in enumerate(p0_plan):
                t.wait_ge(s_x.h, 16 * p0_loadcnt[it])
                if it >= 2:
                    t.wait_ge(s_ev.h, it - 1)
                pX = pA if it % 2 == 0 else pB
                nr = 4 if item[0] == "4rows" else 1
                for c in range(8):
                    for r in range(nr):
                        mm = t.matmul(pX[:, 4 * c + r:4 * c + r + 1],
                                      rowbuf[0:1, r * DM + c * 128:r * DM + (c + 1) * 128],
                                      ones1f[0:1, 0:1], start=True, stop=True)
                mm.then_inc(s_mm.h, 1)
        p0_mm0 = s_mm.v
        s_mm.bump(len(p0_plan))

        def p0_v(v):
            for it, item in enumerate(p0_plan):
                v.wait_ge(s_mm.h, p0_mm0 + it + 1)
                pX = pA if it % 2 == 0 else pB
                if item[0] == "4rows":
                    e = item[1]
                    for c in range(8):
                        v.tensor_copy(CT[:, c, e:e + 1], pX[:, 4 * c:4 * c + 1])
                        v.tensor_copy(GT[:, c, e:e + 1], pX[:, 4 * c + 1:4 * c + 2])
                        v.tensor_copy(BT[:, c, e:e + 1], pX[:, 4 * c + 2:4 * c + 3])
                        op = v.tensor_copy(B1T[:, c, e:e + 1], pX[:, 4 * c + 3:4 * c + 4])
                    op.then_inc(s_ev.h, 1)
                else:
                    e, seg = item[1], item[2]
                    for c in range(8):
                        op = v.tensor_copy(B1T[:, seg * 8 + c, e:e + 1],
                                           pX[:, 4 * c:4 * c + 1])
                    op.then_inc(s_ev.h, 1)
        s_ev.bump(len(p0_plan))

        k.emit("sync", p0_sync)
        k.emit("tensor", p0_t)
        k.emit("vector", p0_v)
        k.run_block()
        p0stk.close()

        if debug_stage == "p0":
            return nc

        wprep_pos = 0
        CTbf = None

        nbatch = 1 if debug_stage in ("p2ab", "p2abd", "p2c", "attn1") else B
        for b in range(nbatch):
            with ExitStack() as bstk:
                bsb = lambda n, s, d: bstk.enter_context(nc.sbuf_tensor(k.uname(n), s, d))
                QKT = bsb("QKT", [128, 16, N], BF16)
                Vrm = bsb("Vrm", [128, 16, DM], BF16)
                xnlT = bsb("xnlT", [128, 8, MLM], BF16)
                QlKlT = bsb("QlKlT", [128, 16, MLM], BF16)

                with ExitStack() as astk:
                    asb = lambda n, s, d: astk.enter_context(nc.sbuf_tensor(k.uname(n), s, d))
                    xnT = asb("xnT", [128, 8, N], BF16)
                    wqkvb = asb("wqkvb", [128, 8, 3 * DM], BF16)
                    xc = asb("xc", [128, 2, DM], F32)
                    xnrm = asb("xnrm", [128, 2, DM], BF16)
                    stat = asb("stat", [128, 16], F32)
                    mv2 = asb("mv2", [128, 2], F32)
                    rstd = asb("rstd", [128, 1], F32)
                    sjunk = asb("sjunk", [128, 4], F32)
                    lred = asb("lred", [128, MLM], F32)

                    k.new_block()
                    xrow0 = b * N

                    gq0 = s_gq.v
                    def a_gq(g, gq0=gq0):
                        for d in range(8):
                            g.dma_start(wqkvb[:, d, :],
                                        wqkv_d[d * 128:(d + 1) * 128, :]).then_inc(s_gq.h, 16)
                    s_gq.bump(16 * 8)
                    k.emit("gpsimd", a_gq)

                    ln_stats = {}
                    ln_apply = {}
                    tp_ms = {}
                    evV = {}
                    evS = {}
                    for i in range(16):
                        x_at = s_x.v + 16

                        def c_sync(sync, i=i, xrow0=xrow0):
                            if i >= 2:
                                sync.wait_ge(s_ln.h, ln_apply[i - 2])
                            sync.dma_start(
                                xc[:, i % 2, :],
                                x_d[xrow0 + 128 * i:xrow0 + 128 * (i + 1), :]).then_inc(s_x.h, 16)
                        s_x.bump(16)
                        k.emit("sync", c_sync)

                        e1_, e2_ = s_ln.bump(), s_ln.bump()
                        def c_stats(v, i=i, x_at=x_at, e1_=e1_, e2_=e2_):
                            v.wait_ge(s_x.h, x_at)
                            v.bn_stats(stat[:, 0:6], xc[:, i % 2, 0:512]).then_inc(s_ln.h, 1)
                            v.wait_ge(s_ln.h, e1_)
                            v.bn_stats(stat[:, 6:12], xc[:, i % 2, 512:1024]).then_inc(s_ln.h, 1)
                            v.wait_ge(s_ln.h, e2_)
                            v.bn_aggr(mv2[:, :], stat[:, 0:12]).then_inc(s_ln.h, 1)
                        ln_stats[i] = s_ln.bump()
                        k.emit("vector", c_stats)

                        def c_sqrt(sc, i=i):
                            sc.wait_ge(s_ln.h, ln_stats[i])
                            sc.activation(sjunk[:, 0:1], mv2[:, 1:2], AF.Sqrt,
                                          bias=epsc[:, 0:1], scale=1.0).then_inc(s_sq.h, 1)
                        sq_i = s_sq.bump()
                        k.emit("scalar", c_sqrt)

                        er_ = s_ln.bump()
                        def c_apply(v, i=i, sq_i=sq_i, er_=er_):
                            v.wait_ge(s_sq.h, sq_i)
                            if i >= 2:
                                v.wait_ge(s_tp.h, tp_ms[i - 2])
                            v.reciprocal(rstd[:, :], sjunk[:, 0:1]).then_inc(s_ln.h, 1)
                            v.wait_ge(s_ln.h, er_)
                            v.tensor_scalar(xnrm[:, i % 2, :], xc[:, i % 2, :],
                                            mv2[:, 0:1], rstd[:, :],
                                            op0=ALU.subtract, op1=ALU.mult
                                            ).then_inc(s_ln.h, 1)
                        ln_apply[i] = s_ln.bump()
                        k.emit("vector", c_apply)

                        def c_tr(t, i=i):
                            t.wait_ge(s_ln.h, ln_apply[i])
                            if i >= 2:
                                t.wait_ge(s_ev.h, evV[i - 2])
                            pT = pTrb if i % 2 == 0 else pTr2
                            for d in range(8):
                                mm = t.transpose(pT[:, d, :],
                                                 xnrm[:, i % 2, d * 128:(d + 1) * 128],
                                                 id128[:, :])
                            mm.then_inc(s_tp.h, 1)
                        tp_ms[i] = s_tp.bump()
                        k.emit("tensor", c_tr)

                        def c_evV(v, i=i):
                            v.wait_ge(s_tp.h, tp_ms[i])
                            pT = pTrb if i % 2 == 0 else pTr2
                            for d in range(8):
                                mm = v.tensor_copy(xnT[:, d, i * 128:(i + 1) * 128],
                                                   pT[:, d, :])
                            mm.then_inc(s_ev.h, 1)
                        evV[i] = s_ev.bump()
                        k.emit("vector", c_evV)
                        evS[i] = s_ev2.v

                    if b == 0:
                        def zfill(sync):
                            for cz in range(E * CAP // 128):
                                sync.dma_start(
                                    znbkt[128 * cz:128 * (cz + 1), :],
                                    zot[:, :]).then_inc(s_st.h, 16)

                        globals()
                        k.emit("sync", zfill)
                        ZFILL_AT = s_st.bump(16 * (E * CAP // 128))
                    else:
                        nrmf = (E * CAP + 128) // 128
                        lo_f = ((b - 1) * nrmf + 2) // 3
                        hi_f = (b * nrmf + 2) // 3
                        def rmfill(sync, lo_f=lo_f, hi_f=hi_f):
                            for cz in range(lo_f, hi_f):
                                sync.dma_start(
                                    rmtab[128 * cz:128 * (cz + 1), :],
                                    rmsent[:, :]).then_inc(s_st.h, 16)
                        k.emit("sync", rmfill)
                        s_st.bump(16 * (hi_f - lo_f))

                    lm_pieces = []
                    for d in range(8):
                        el_ = None
                        def c_lm(v, d=d, el_=None):
                            if d == 0:
                                v.wait_ge(s_ev.h, evV[15])
                            v.tensor_reduce(lred[:, :],
                                            xnT[:, d, :].rearrange("p (m s) -> p m s", m=MLM),
                                            axis=AX.X, op=ALU.add).then_inc(s_ln.h, 1)
                            v.wait_ge(s_ln.h, el_)
                            v.tensor_scalar(xnlT[:, d, :], lred[:, :], 1.0 / SEG, None,
                                            op0=ALU.mult).then_inc(s_ln.h, 1)
                        lm_pieces.append(c_lm)
                    lm_ms = []

                    qk_mm = {}
                    qk_ev = {}
                    wq_at = s_gq.v

                    for nh in range(4):
                        for cc in range(16):
                            gidx = nh * 16 + cc

                            def qk_t(t, nh=nh, cc=cc, gidx=gidx, wq_at=wq_at):
                                if gidx == 0:
                                    t.wait_ge(s_gq.h, wq_at)
                                if cc == 0:
                                    for j in range(4):
                                        t.wait_ge(s_ev.h, evV[4 * nh + j])
                                if gidx >= 2:
                                    t.wait_ge(s_ev.h, qk_ev[gidx - 2])
                                pQ = pA if gidx % 2 == 0 else pB
                                for d in range(8):
                                    mm = t.matmul(pQ[:, :],
                                                  wqkvb[:, d, cc * 128:(cc + 1) * 128],
                                                  xnT[:, d, nh * 512:(nh + 1) * 512],
                                                  start=(d == 0), stop=(d == 7))
                                mm.then_inc(s_mm.h, 1)
                            qk_mm[gidx] = s_mm.bump()
                            k.emit("tensor", qk_t)

                            def qk_e(v, nh=nh, cc=cc, gidx=gidx):
                                v.wait_ge(s_mm.h, qk_mm[gidx])
                                pQ = pA if gidx % 2 == 0 else pB
                                scale = 0.125 if cc < 8 else 1.0
                                v.tensor_scalar(QKT[:, cc, nh * 512:(nh + 1) * 512],
                                                pQ[:, :], scale, None,
                                                op0=ALU.mult).then_inc(s_ev.h, 1)
                            qk_ev[gidx] = s_ev.bump()
                            k.emit("vector", qk_e)
                            if 8 <= gidx < 16:
                                el_ = s_ln.bump()
                                fn = lm_pieces[gidx - 8]
                                k.emit("vector",
                                       lambda v, fn=fn, el_=el_: fn(v, el_=el_))
                                lm_ms.append(s_ln.bump())

                    v_mm = {}
                    v_ev = {}
                    for i in range(16):
                        for dh in range(2):
                            vc = i * 2 + dh

                            def v_t(t, i=i, dh=dh, vc=vc):
                                if vc >= 2:
                                    t.wait_ge(s_ev2.h, v_ev[vc - 2])
                                pV = pC if vc % 2 == 0 else pD
                                for d in range(8):
                                    mm = t.matmul(pV[:, :], xnT[:, d, i * 128:(i + 1) * 128],
                                                  wqkvb[:, d, 2048 + dh * 512:2048 + (dh + 1) * 512],
                                                  start=(d == 0), stop=(d == 7))
                                mm.then_inc(s_mm.h, 1)
                            v_mm[vc] = s_mm.bump()
                            k.emit("tensor", v_t)

                            def v_e(sc, i=i, dh=dh, vc=vc):
                                sc.wait_ge(s_mm.h, v_mm[vc])
                                pV = pC if vc % 2 == 0 else pD
                                sc.copy(Vrm[:, i, dh * 512:(dh + 1) * 512],
                                        pV[:, :]).then_inc(s_ev2.h, 1)
                            v_ev[vc] = s_ev2.bump()
                            k.emit("scalar", v_e)

                    ql_mm = {}
                    ql_ev = {}
                    for cc in range(16):
                        def ql_t(t, cc=cc):
                            if cc == 0:
                                t.wait_ge(s_ln.h, lm_ms[-1])
                            if cc >= 2:
                                t.wait_ge(s_ev.h, ql_ev[cc - 2])
                            pQ = pE_ if cc % 2 == 0 else pF_
                            for d in range(8):
                                mm = t.matmul(pQ[:, 0:MLM],
                                              wqkvb[:, d, cc * 128:(cc + 1) * 128],
                                              xnlT[:, d, :], start=(d == 0), stop=(d == 7))
                            mm.then_inc(s_mm.h, 1)
                        ql_mm[cc] = s_mm.bump()
                        k.emit("tensor", ql_t)

                        def ql_e(v, cc=cc):
                            v.wait_ge(s_mm.h, ql_mm[cc])
                            pQ = pE_ if cc % 2 == 0 else pF_
                            scale = 0.125 if cc < 8 else 1.0
                            v.tensor_scalar(QlKlT[:, cc, :], pQ[:, 0:MLM], scale, None,
                                            op0=ALU.mult).then_inc(s_ev.h, 1)
                        ql_ev[cc] = s_ev.bump()
                        k.emit("vector", ql_e)

                    nun = 4
                    k.emit("gpsimd", emit_wprep(wprep_units[wprep_pos:wprep_pos + nun]))
                    wprep_pos += nun

                    k.run_block()

                if debug_stage == "p2ab":
                    return nc
                if debug_stage == "p2abd":
                    k.new_block()
                    def dump(sync):
                        sync.dma_start(dbg_d[:, 0, :], xnT[:, 0, :]).then_inc(s_st.h, 16)
                        sync.dma_start(dbg_d[:, 1, :], QKT[:, 0, :]).then_inc(s_st.h, 16)
                        sync.dma_start(dbg_d[:, 2, :], QKT[:, 8, :]).then_inc(s_st.h, 16)
                        sync.dma_start(dbg_d[:, 3, 0:1024], Vrm[:, 0, :]).then_inc(s_st.h, 16)
                        sync.dma_start(dbg_d[:, 3, 1024:1088], QlKlT[:, 0, :]).then_inc(s_st.h, 16)
                        sync.dma_start(dbg_d[:, 3, 1088:1152], xnlT[:, 0, :]).then_inc(s_st.h, 16)
                        sync.wait_ge(s_st.h, s_st_target)
                    s_st_target = s_st.bump(16 * 6)
                    k.emit("sync", dump)
                    k.run_block()
                    return nc

                with ExitStack() as pstk:
                    psb = lambda n, s, d: pstk.enter_context(nc.sbuf_tensor(k.uname(n), s, d))
                    SVT = psb("SVT", [128, 8, N], BF16)
                    npstk = ExitStack()
                    psb = lambda n, s, d: npstk.enter_context(nc.sbuf_tensor(k.uname(n), s, d))
                    E1T = psb("E1T", [128, 2, N], BF16)
                    E3T = psb("E3T", [128, 2, 16, 128], BF16)
                    e1f = psb("e1f", [128, 2, 2, 512], F32)
                    e1n = psb("e1n", [128, 2, 2, 512], BF16)
                    s1r2 = psb("s1r2", [128, 2, 8], F32)
                    Kl_bd2 = psb("Kl_bd2", [128, 2, 128], BF16)
                    Ql_bd2 = psb("Ql_bd2", [128, 2, 128], BF16)
                    AVn2 = psb("AVn2", [128, 2, 64], BF16)
                    Zst2 = psb("Zst2", [128, 2, 64], BF16)
                    Z_bd2 = psb("Z_bd2", [128, 2, 128], BF16)
                    rs32 = psb("rs32", [128, 2], F32)
                    k2e_all = psb("k2e_all", [128, 8, 64], F32)
                    k2n_all = psb("k2n_all", [128, 8, 64], BF16)
                    nmax8 = psb("nmax8", [128, 8], F32)
                    k2s8 = psb("k2s8", [128, 8], F32)
                    rds8 = psb("rds8", [128, 8], F32)
                    scol8 = psb("scol8", [128, 8], F32)
                    rdr8 = psb("rdr8", [8, 2], F32)
                    rdrT_sb = psb("rdrT_sb", [2, 8], F32)
                    dsc_sb = psb("dsc_sb", [128, 8], F32)
                    k2T_bd = psb("k2T_bd", [128, 8, 128], BF16)
                    V_bd = psb("V_bd", [128, 8, 128], BF16)
                    KVT_bd = psb("KVT_bd", [128, 8, 128], BF16)
                    VT_bd = psb("VT_bd", [128, 8, 128], BF16)
                    T_all = psb("T_all", [128, 8, 128], BF16)
                    cIs128r = psb("cIs128r", [128, 12, 128], F32)

                    k.new_block()
                    QL_LAST = ql_mm[15]
                    QK_EV_LAST = qk_ev[63]
                    QL_EV_LAST = ql_ev[15]
                    V_EV_LAST = v_ev[31]

                    def g1(g):
                        g.wait_ge(s_mm.h, QL_LAST)
                        g.memset(k2T_bd[:, :, :], 0.0)
                        g.memset(Kl_bd2[:, :, :], 0.0)
                        g.memset(Ql_bd2[:, :, :], 0.0)
                        g.memset(Z_bd2[:, :, :], 0.0)
                        for j, cv in enumerate([7.0, 15.0, 13.0]):
                            for q in range(4):
                                g.tensor_scalar(cIs128r[:, 4 * j + q, :],
                                                id128f[:, :], cv, None,
                                                op0=ALU.mult)
                        g.nop().then_inc(s_gp.h, 1)
                    G1_MS = s_gp.bump()
                    k.emit("gpsimd", g1)

                    def s1(t):
                        t.wait_ge(s_ev.h, QL_EV_LAST)
                        for pr2 in range(PAIRS):
                            for hh in range(2):
                                mm = t.matmul(pF_[hh * 64:(hh + 1) * 64,
                                                  pr2 * 64:(pr2 + 1) * 64],
                                              QlKlT[hh * 64:(hh + 1) * 64, pr2, :],
                                              QlKlT[hh * 64:(hh + 1) * 64, 8 + pr2, :],
                                              start=True, stop=True)
                        mm.then_inc(s_nt.h, 1)
                    S1_MS = s_nt.bump()
                    k.emit("tensor", s1)

                    def s2(v):
                        v.wait_ge(s_nt.h, S1_MS)
                        v.reduce_max(nmax8[:, :],
                                     pF_[:, 0:512].rearrange("p (a m) -> p a m", a=8),
                                     axis=AX.X, negate=True).then_inc(s_nv.h, 1)
                    S2_MS = s_nv.bump()
                    k.emit("vector", s2)

                    def s3(sc):
                        sc.wait_ge(s_nv.h, S2_MS)
                        for pr2 in range(PAIRS):
                            op = sc.activation(k2e_all[:, pr2, :],
                                               pF_[:, pr2 * 64:(pr2 + 1) * 64],
                                               AF.Exp, bias=nmax8[:, pr2:pr2 + 1],
                                               scale=1.0,
                                               accum_out=k2s8[:, pr2:pr2 + 1])
                        op.then_inc(s_ev2.h, 1)
                    S3_MS = s_ev2.bump()
                    k.emit("scalar", s3)

                    es4 = s_nv.bump()
                    def s4(v, es4=es4):
                        v.wait_ge(s_ev2.h, S3_MS)
                        v.reciprocal(rds8[:, :], k2s8[:, :]).then_inc(s_nv.h, 1)
                        v.wait_ge(s_nv.h, es4)
                        for pr2 in range(PAIRS):
                            op = v.tensor_scalar(k2n_all[:, pr2, :], k2e_all[:, pr2, :],
                                                 rds8[:, pr2:pr2 + 1], None,
                                                 op0=ALU.mult)
                        op.then_inc(s_nv.h, 1)
                    S4_MS = s_nv.bump()
                    k.emit("vector", s4)

                    def s5(t):
                        t.wait_ge(s_nv.h, S4_MS)
                        for pr2 in range(PAIRS):
                            t.transpose(pTr2[0:64, pr2, 0:64], k2n_all[0:64, pr2, :],
                                        id64s[0:64, :])
                            mm = t.transpose(pTr2[64:128, pr2, 64:128],
                                             k2n_all[64:128, pr2, :],
                                             id64s[64:128, :])
                        mm.then_inc(s_nt.h, 1)
                    S5_MS = s_nt.bump()
                    k.emit("tensor", s5)

                    def s6(v):
                        v.wait_ge(s_nt.h, S5_MS)
                        v.wait_ge(s_gp.h, G1_MS)
                        for pr2 in range(PAIRS):
                            v.tensor_copy(k2T_bd[0:64, pr2, 0:64],
                                          pTr2[0:64, pr2, 0:64])
                            op = v.tensor_copy(k2T_bd[64:128, pr2, 64:128],
                                               pTr2[64:128, pr2, 64:128])
                        op.then_inc(s_nv.h, 1)
                    S6_MS = s_nv.bump()
                    k.emit("vector", s6)

                    def s7(v):
                        v.wait_ge(s_nv.h, S6_MS)
                        v.tensor_reduce(scol8[:, :],
                                        k2T_bd[:, :, :], axis=AX.X,
                                        op=ALU.add).then_inc(s_nv.h, 1)
                    S7_MS = s_nv.bump()
                    k.emit("vector", s7)

                    def s8(t):
                        t.wait_ge(s_nv.h, S7_MS)
                        t.wait_ge(s_ev2.h, V_EV_LAST)
                        t.transpose(pC[0:8, 0:128], scol8[:, :],
                                    id128f[:, :]).then_inc(s_nt.h, 1)
                    S8_MS = s_nt.bump()
                    k.emit("tensor", s8)

                    es9 = s_nv.bump()
                    def s9(v, es9=es9):
                        v.wait_ge(s_nt.h, S8_MS)
                        v.reduce_max(rdr8[0:8, 0:1], pC[0:8, 0:64], axis=AX.X)
                        v.reduce_max(rdr8[0:8, 1:2], pC[0:8, 64:128],
                                     axis=AX.X).then_inc(s_nv.h, 1)
                        v.wait_ge(s_nv.h, es9)
                        v.reciprocal(rdr8[0:8, :], rdr8[0:8, :]).then_inc(s_nv.h, 1)
                    S9_MS = s_nv.bump()
                    k.emit("vector", s9)

                    def s10(t):
                        t.wait_ge(s_nv.h, S9_MS)
                        t.transpose(pD[0:2, 0:8], rdr8[0:8, 0:2],
                                    id128f[0:8, 0:8]).then_inc(s_nt.h, 1)
                    S10_MS = s_nt.bump()
                    k.emit("tensor", s10)

                    def s11(v):
                        v.wait_ge(s_nt.h, S10_MS)
                        v.tensor_copy(rdrT_sb[0:2, 0:8],
                                      pD[0:2, 0:8]).then_inc(s_nv.h, 1)
                    S11_MS = s_nv.bump()
                    k.emit("vector", s11)

                    def s12(t):
                        t.wait_ge(s_nv.h, S11_MS)
                        t.matmul(pC[:, 0:8], sel01[0:2, :], rdrT_sb[0:2, 0:8],
                                 start=True, stop=True).then_inc(s_nt.h, 1)
                    S12_MS = s_nt.bump()
                    k.emit("tensor", s12)

                    es13 = s_nv.bump()
                    def s13(v, es13=es13):
                        v.wait_ge(s_nt.h, S12_MS)
                        v.tensor_copy(dsc_sb[:, :], pC[:, 0:8]).then_inc(s_nv.h, 1)
                        v.wait_ge(s_nv.h, es13)
                        for pr2 in range(PAIRS):
                            op = v.tensor_scalar(V_bd[:, pr2, :], k2T_bd[:, pr2, :],
                                                 dsc_sb[:, pr2:pr2 + 1], None,
                                                 op0=ALU.mult)
                        op.then_inc(s_nv.h, 1)
                    V0_MS = s_nv.bump()
                    k.emit("vector", s13)

                    if debug_stage == "p2cS":
                        k.run_block()
                        npstk.pop_all(); pstk.pop_all(); bstk.pop_all(); stack.pop_all()
                        return nc

                    VOK = {0: V0_MS}
                    for it in range(6):
                        def p1t(t, it=it):
                            t.wait_ge(s_nv.h, VOK[it])
                            if it == 0:
                                t.wait_ge(s_ev.h, QK_EV_LAST)
                            for pr2 in range(PAIRS):
                                pKV = pA if pr2 < 4 else pB
                                pKT = pC if pr2 < 4 else pD
                                t.matmul(pKV[:, (pr2 % 4) * 128:(pr2 % 4 + 1) * 128],
                                         k2T_bd[:, pr2, :], V_bd[:, pr2, :],
                                         start=True, stop=True)
                                t.matmul(pKT[:, (pr2 % 4) * 128:(pr2 % 4 + 1) * 128],
                                         V_bd[:, pr2, :], k2T_bd[:, pr2, :],
                                         start=True, stop=True)
                                mm = t.transpose(pTrb[:, pr2, :], V_bd[:, pr2, :],
                                                 id128[:, :])
                            mm.then_inc(s_nt.h, 1)
                        P1T = s_nt.bump()
                        k.emit("tensor", p1t)

                        def p1v(v, it=it, P1T=P1T):
                            v.wait_ge(s_nt.h, P1T)
                            v.tensor_tensor(T_all[:, 0:4, :],
                                            cIs128r[:, 0:4, :], pA[:, :],
                                            op=ALU.subtract)
                            v.tensor_tensor(T_all[:, 4:8, :],
                                            cIs128r[:, 0:4, :], pB[:, :],
                                            op=ALU.subtract)
                            v.tensor_copy(KVT_bd[:, 0:4, :], pC[:, :])
                            v.tensor_copy(KVT_bd[:, 4:8, :], pD[:, :])
                            v.tensor_copy(VT_bd[:, :, :],
                                          pTrb[:, :, :]).then_inc(s_nv.h, 1)
                        P1V = s_nv.bump()
                        k.emit("vector", p1v)

                        def p2t(t, it=it, P1V=P1V):
                            t.wait_ge(s_nv.h, P1V)
                            for pr2 in range(PAIRS):
                                pU = pC if pr2 < 4 else pD
                                mm = t.matmul(pU[:, (pr2 % 4) * 128:(pr2 % 4 + 1) * 128],
                                              KVT_bd[:, pr2, :], T_all[:, pr2, :],
                                              start=True, stop=True)
                            mm.then_inc(s_nt.h, 1)
                        P2T = s_nt.bump()
                        k.emit("tensor", p2t)

                        def p2v(v, it=it, P2T=P2T):
                            v.wait_ge(s_nt.h, P2T)
                            v.tensor_tensor(T_all[:, 0:4, :],
                                            cIs128r[:, 4:8, :], pC[:, :],
                                            op=ALU.subtract)
                            v.tensor_tensor(T_all[:, 4:8, :],
                                            cIs128r[:, 4:8, :], pD[:, :],
                                            op=ALU.subtract).then_inc(s_nv.h, 1)
                        P2V = s_nv.bump()
                        k.emit("vector", p2v)

                        def p3t(t, it=it, P2V=P2V):
                            t.wait_ge(s_nv.h, P2V)
                            for pr2 in range(PAIRS):
                                pU = pC if pr2 < 4 else pD
                                mm = t.matmul(pU[:, (pr2 % 4) * 128:(pr2 % 4 + 1) * 128],
                                              KVT_bd[:, pr2, :], T_all[:, pr2, :],
                                              start=True, stop=True)
                            mm.then_inc(s_nt.h, 1)
                        P3T = s_nt.bump()
                        k.emit("tensor", p3t)

                        def p3v(v, it=it, P3T=P3T):
                            v.wait_ge(s_nt.h, P3T)
                            v.tensor_tensor(T_all[:, 0:4, :],
                                            cIs128r[:, 8:12, :], pC[:, :],
                                            op=ALU.subtract)
                            v.tensor_tensor(T_all[:, 4:8, :],
                                            cIs128r[:, 8:12, :], pD[:, :],
                                            op=ALU.subtract).then_inc(s_nv.h, 1)
                        P3V = s_nv.bump()
                        k.emit("vector", p3v)

                        def p4t(t, it=it, P3V=P3V):
                            t.wait_ge(s_nv.h, P3V)
                            for pr2 in range(PAIRS):
                                pV2 = pA if pr2 < 4 else pB
                                mm = t.matmul(pV2[:, (pr2 % 4) * 128:(pr2 % 4 + 1) * 128],
                                              VT_bd[:, pr2, :], T_all[:, pr2, :],
                                              start=True, stop=True)
                            mm.then_inc(s_nt.h, 1)
                        P4T = s_nt.bump()
                        k.emit("tensor", p4t)

                        def p4v(v, it=it, P4T=P4T):
                            v.wait_ge(s_nt.h, P4T)
                            v.tensor_scalar(V_bd[:, 0:4, :], pA[:, :], 0.25, None,
                                            op0=ALU.mult)
                            v.tensor_scalar(V_bd[:, 4:8, :], pB[:, :], 0.25, None,
                                            op0=ALU.mult).then_inc(s_nv.h, 1)
                        VOK[it + 1] = s_nv.bump()
                        k.emit("vector", p4v)

                    def p5t(t):
                        t.wait_ge(s_nv.h, VOK[6])
                        for pr2 in range(PAIRS):
                            mm = t.transpose(pTrb[:, pr2, :], V_bd[:, pr2, :],
                                             id128[:, :])
                        mm.then_inc(s_nt.h, 1)
                    P5T = s_nt.bump()
                    k.emit("tensor", p5t)

                    def p5v(v, P5T=P5T):
                        v.wait_ge(s_nt.h, P5T)
                        v.tensor_copy(VT_bd[:, :, :],
                                      pTrb[:, :, :]).then_inc(s_nv.h, 1)
                    NEWT_DONE = s_nv.bump()
                    k.emit("vector", p5v)
                    VOK6_EVAC = NEWT_DONE

                    if debug_stage == "p2cN":
                        k.run_block()
                        npstk.pop_all(); pstk.pop_all(); bstk.pop_all(); stack.pop_all()
                        return nc

                    kg_last = {}
                    k1e_msP = {}
                    k3e_msP = {}
                    k1n_msP = {}
                    k1t_msP = {}
                    k1T_msP = {}
                    av_msP = {}
                    avn_msP = {}
                    zmm_msP = {}
                    zev_msP = {}
                    zbd_msP = {}
                    sv_msP = {}
                    svev_msP = {}

                    def build_pair(pr):
                        par = pr % 2
                        kgA, kgB = (pA, pB) if par == 0 else (pE_, pF_)
                        k1e_ms = {}; k3e_ms = {}; k1n_ms = {}
                        k1t_ms = {}; k1T_ms = {}
                        k1e_msP[pr] = k1e_ms; k3e_msP[pr] = k3e_ms
                        k1n_msP[pr] = k1n_ms; k1t_msP[pr] = k1t_ms
                        k1T_msP[pr] = k1T_ms

                        def c0(g, pr=pr, par=par):
                            g.wait_ge(s_gp.h, G1_MS)
                            if pr >= 2:
                                g.wait_ge(s_mm.h, kg_last[pr - 2])
                            g.tensor_copy(Kl_bd2[0:64, par, 0:64], QlKlT[0:64, 8 + pr, :])
                            g.tensor_copy(Kl_bd2[64:128, par, 64:128],
                                          QlKlT[64:128, 8 + pr, :])
                            g.tensor_copy(Ql_bd2[0:64, par, 0:64], QlKlT[0:64, pr, :])
                            g.tensor_copy(Ql_bd2[64:128, par, 64:128],
                                          QlKlT[64:128, pr, :]).then_inc(s_gp.h, 1)
                        c0_ms = s_gp.bump()
                        yield ("H", "gpsimd", c0)

                        for g8 in range(8):
                            isk1 = g8 < 4
                            gg = g8 % 4

                            def kg_t(t, pr=pr, par=par, g8=g8, isk1=isk1, gg=gg,
                                     c0_ms=c0_ms, kgA=kgA, kgB=kgB,
                                     k1e_ms=k1e_ms, k3e_ms=k3e_ms):
                                if g8 == 0:
                                    t.wait_ge(s_gp.h, c0_ms)
                                    if par == 0:
                                        t.wait_ge(s_nv.h, VOK6_EVAC)
                                    else:
                                        t.wait_ge(s_ev.h, QL_EV_LAST)
                                if g8 == 1 and par == 1:
                                    t.wait_ge(s_ev2.h, S3_MS)
                                if g8 >= 2:
                                    t.wait_ge(s_ev2.h,
                                              (k1e_ms if g8 - 2 < 4 else k3e_ms)[
                                                  (g8 - 2) % 4])
                                elif pr >= 2:
                                    t.wait_ge(s_ev2.h, k3e_msP[pr - 2][2 + g8])
                                pX = kgA if g8 % 2 == 0 else kgB
                                cc = (pr if isk1 else 8 + pr)
                                bd = Kl_bd2 if isk1 else Ql_bd2
                                for j in range(4):
                                    nch = 4 * gg + j
                                    mm = t.matmul(pX[:, j * 128:(j + 1) * 128],
                                                  QKT[:, cc, nch * 128:(nch + 1) * 128],
                                                  bd[:, par, :], start=True, stop=True)
                                mm.then_inc(s_mm.h, 1)
                            kg_ms = s_mm.bump()
                            if g8 == 7:
                                kg_last[pr] = kg_ms
                            yield ("H", "tensor", kg_t)

                            if isk1:
                                def k1_s(sc, pr=pr, par=par, gg=gg, kg_ms=kg_ms,
                                         kgA=kgA, kgB=kgB, k1n_ms=k1n_ms):
                                    sc.wait_ge(s_mm.h, kg_ms)
                                    if gg >= 2:
                                        sc.wait_ge(s_ev.h, k1n_ms[gg - 2])
                                    elif pr >= 2:
                                        sc.wait_ge(s_ev.h, k1n_msP[pr - 2][gg + 2])
                                    pX = kgA if gg % 2 == 0 else kgB
                                    sc.activation(e1f[:, par, gg % 2, :], pX[:, :],
                                                  AF.Exp).then_inc(s_ev2.h, 1)
                                k1e_ms[gg] = s_ev2.bump()
                                yield ("H", "scalar", k1_s)

                                ek1, ek2 = s_ev.bump(), s_ev.bump()
                                def k1_v(v, pr=pr, par=par, gg=gg,
                                         k1e_ms=k1e_ms, k1t_ms=k1t_ms,
                                         ek1=ek1, ek2=ek2):
                                    v.wait_ge(s_ev2.h, k1e_ms[gg])
                                    if gg >= 2:
                                        v.wait_ge(s_tp.h, k1t_ms[gg - 2])
                                    elif pr >= 2:
                                        v.wait_ge(s_tp.h, k1t_msP[pr - 2][gg + 2])
                                    v.tensor_reduce(s1r2[:, par, :],
                                                    e1f[:, par, gg % 2, :].rearrange(
                                                        "p (a m) -> p a m", a=8),
                                                    axis=AX.X, op=ALU.add).then_inc(s_ev.h, 1)
                                    v.wait_ge(s_ev.h, ek1)
                                    v.reciprocal(s1r2[:, par, :],
                                                 s1r2[:, par, :]).then_inc(s_ev.h, 1)
                                    v.wait_ge(s_ev.h, ek2)
                                    for j in range(8):
                                        op = v.tensor_scalar(
                                            e1n[:, par, gg % 2, j * 64:(j + 1) * 64],
                                            e1f[:, par, gg % 2, j * 64:(j + 1) * 64],
                                            s1r2[:, par, j:j + 1], None, op0=ALU.mult)
                                    op.then_inc(s_ev.h, 1)
                                k1n_ms[gg] = s_ev.bump()
                                yield ("H", "vector", k1_v)

                                def k1_tr(t, pr=pr, par=par, gg=gg,
                                          k1n_ms=k1n_ms, k1T_ms=k1T_ms):
                                    t.wait_ge(s_ev.h, k1n_ms[gg])
                                    if gg >= 1:
                                        t.wait_ge(s_ev2.h, k1T_ms[gg - 1])
                                    elif pr >= 2:
                                        t.wait_ge(s_ev2.h, k1T_msP[pr - 2][3])
                                    elif pr < 2:
                                        t.wait_ge(s_nv.h, NEWT_DONE)
                                    for j in range(4):
                                        mm = t.transpose(pTrb[:, 4 * par + j, :],
                                                         e1n[:, par, gg % 2,
                                                             j * 128:(j + 1) * 128],
                                                         id128[:, :])
                                    mm.then_inc(s_tp.h, 1)
                                k1t_ms[gg] = s_tp.bump()
                                yield ("H", "tensor", k1_tr)

                                def k1_e(sc, pr=pr, par=par, gg=gg, k1t_ms=k1t_ms):
                                    sc.wait_ge(s_tp.h, k1t_ms[gg])
                                    if gg == 0 and pr >= 2:
                                        sc.wait_ge(s_mm.h, sv_msP[pr - 2])
                                    for j in range(4):
                                        nch = 4 * gg + j
                                        mm = sc.copy(E1T[:, par, nch * 128:(nch + 1) * 128],
                                                     pTrb[:, 4 * par + j, :])
                                    mm.then_inc(s_ev2.h, 1)
                                k1T_ms[gg] = s_ev2.bump()
                                yield ("H", "scalar", k1_e)
                            else:
                                def k3_s(sc, pr=pr, par=par, gg=gg, kg_ms=kg_ms,
                                         kgA=kgA, kgB=kgB):
                                    sc.wait_ge(s_mm.h, kg_ms)
                                    if gg == 0 and pr >= 2:
                                        sc.wait_ge(s_nv.h, avn_msP[pr - 2])
                                    pX = kgA if gg % 2 == 0 else kgB
                                    sc.activation(
                                        E3T[:, par, 4 * gg:4 * gg + 4, :].rearrange(
                                            "p a m -> p (a m)"),
                                        pX[:, :], AF.Exp).then_inc(s_ev2.h, 1)
                                k3e_ms[gg] = s_ev2.bump()
                                yield ("H", "scalar", k3_s)

                        yield ("M", None, None)

                        def s3av_t(t, pr=pr, par=par, k3e_ms=k3e_ms):
                            t.wait_ge(s_ev2.h, k3e_ms[3])
                            if pr >= 1:
                                t.wait_ge(s_ev.h, svev_msP[pr - 1][3])
                            elif pr == 0:
                                t.wait_ge(s_nv.h, VOK6_EVAC)
                            for nch in range(16):
                                t.matmul(pC[:, 0:1], E3T[:, par, nch, :], ones128[:, :],
                                         start=(nch == 0), stop=(nch == 15))
                                mm = t.matmul(pD[:, 0:128], E3T[:, par, nch, :],
                                              Vrm[:, nch, pr * 128:(pr + 1) * 128],
                                              start=(nch == 0), stop=(nch == 15))
                            mm.then_inc(s_mm.h, 1)
                        av_msP[pr] = s_mm.bump()
                        yield ("T", "tensor", s3av_t)

                        eav = s_nv.bump()
                        def av_v(v, pr=pr, par=par, eav=eav):
                            v.wait_ge(s_mm.h, av_msP[pr])
                            if pr >= 2:
                                v.wait_ge(s_nt.h, zmm_msP[pr - 2])
                            v.reciprocal(rs32[:, par:par + 1],
                                         pC[:, 0:1]).then_inc(s_nv.h, 1)
                            v.wait_ge(s_nv.h, eav)
                            v.tensor_scalar(AVn2[0:64, par, :], pD[0:64, 0:64],
                                            rs32[0:64, par:par + 1], None, op0=ALU.mult)
                            v.tensor_scalar(AVn2[64:128, par, :], pD[64:128, 64:128],
                                            rs32[64:128, par:par + 1], None,
                                            op0=ALU.mult).then_inc(s_nv.h, 1)
                        avn_msP[pr] = s_nv.bump()
                        yield ("T", "vector", av_v)

                        def z_mm(t, pr=pr, par=par):
                            t.wait_ge(s_nv.h, NEWT_DONE)
                            t.wait_ge(s_nv.h, avn_msP[pr])
                            t.matmul(pC[:, 448:512], VT_bd[:, pr, :],
                                     AVn2[:, par, :], start=True,
                                     stop=True).then_inc(s_nt.h, 1)
                        zmm_msP[pr] = s_nt.bump()
                        yield ("T", "tensor", z_mm)

                        def z_ev(v, pr=pr, par=par):
                            v.wait_ge(s_nt.h, zmm_msP[pr])
                            if pr >= 2:
                                v.wait_ge(s_gp.h, zbd_msP[pr - 2])
                            v.tensor_copy(Zst2[:, par, :],
                                          pC[:, 448:512]).then_inc(s_nv.h, 1)
                        zev_msP[pr] = s_nv.bump()
                        yield ("T", "vector", z_ev)

                        def z_bd(g, pr=pr, par=par):
                            g.wait_ge(s_nv.h, zev_msP[pr])
                            if pr >= 2:
                                g.wait_ge(s_mm.h, sv_msP[pr - 2])
                            g.tensor_copy(Z_bd2[0:64, par, 0:64], Zst2[0:64, par, :])
                            g.tensor_copy(Z_bd2[64:128, par, 64:128],
                                          Zst2[64:128, par, :]).then_inc(s_gp.h, 1)
                        zbd_msP[pr] = s_gp.bump()
                        yield ("T", "gpsimd", z_bd)

                        sv_ev_ms = {}
                        svev_msP[pr] = sv_ev_ms
                        for nh2 in range(4):
                            def sv_t(t, pr=pr, par=par, nh2=nh2,
                                     k1T_ms=k1T_ms, sv_ev_ms=sv_ev_ms):
                                if nh2 == 0:
                                    t.wait_ge(s_gp.h, zbd_msP[pr])
                                    t.wait_ge(s_ev2.h, k1T_ms[3])
                                if nh2 >= 2:
                                    t.wait_ge(s_ev.h, sv_ev_ms[nh2 - 2])
                                pX = pD if nh2 % 2 == 0 else pC
                                mm = t.matmul(pX[:, :], Z_bd2[:, par, :],
                                              E1T[:, par, nh2 * 512:(nh2 + 1) * 512],
                                              start=True, stop=True)
                                mm.then_inc(s_mm.h, 1)
                            svmm = s_mm.bump()
                            sv_msP[pr] = svmm
                            yield ("T", "tensor", sv_t)

                            def sv_e(v, pr=pr, nh2=nh2, svmm=svmm):
                                v.wait_ge(s_mm.h, svmm)
                                pX = pD if nh2 % 2 == 0 else pC
                                v.tensor_copy(SVT[:, pr, nh2 * 512:(nh2 + 1) * 512],
                                              pX[:, :]).then_inc(s_ev.h, 1)
                            sv_ev_ms[nh2] = s_ev.bump()
                            yield ("T", "vector", sv_e)

                    for pg in range(4):
                        ga = build_pair(2 * pg)
                        gb = build_pair(2 * pg + 1)
                        ia = next(ga)
                        ib = next(gb)
                        while ia[0] == "H" or ib[0] == "H":
                            if ia[0] == "H":
                                k.emit(ia[1], ia[2])
                                ia = next(ga)
                            if ib[0] == "H":
                                k.emit(ib[1], ib[2])
                                ib = next(gb)
                        assert ia[0] == "M" and ib[0] == "M"
                        for tag, eng, fn in ga:
                            k.emit(eng, fn)
                        for tag, eng, fn in gb:
                            k.emit(eng, fn)

                    sv_done_v = s_ev.v
                    k.run_block()
                    npstk.close()

                    if isinstance(debug_stage, str) and debug_stage.startswith("p2c"):
                        pstk.pop_all(); bstk.pop_all(); stack.pop_all()
                        return nc

                    with ExitStack() as estk:
                        esb = lambda n, s, d: estk.enter_context(nc.sbuf_tensor(k.uname(n), s, d))
                        wprojb = esb("wprojb", [128, 8, DM], BF16)
                        xc2 = esb("xc2", [128, 2, DM], F32)
                        x1t = esb("x1t", [128, 2, DM], F32)
                        stat2 = esb("stat2", [128, 16], F32)
                        mv3 = esb("mv3", [128, 2], F32)
                        rstd2 = esb("rstd2", [128, 1], F32)
                        sjunk2 = esb("sjunk2", [128, 4], F32)

                        k.new_block()
                        gq0 = s_gq.v

                        def e_gq(g):
                            g.wait_ge(s_ev.h, sv_done_v)
                            for d in range(8):
                                g.dma_start(wprojb[:, d, :],
                                            wproj_d[d * 128:(d + 1) * 128, :]).then_inc(s_gq.h, 16)
                        s_gq.bump(16 * 8)
                        k.emit("gpsimd", e_gq)

                        pj_mm = {}
                        pj_ap = {}
                        ST_X1 = {}
                        for i in range(16):
                            def e_sync(sync, i=i, b=b):
                                if i == 0:
                                    sync.wait_ge(s_ev.h, sv_done_v)
                                if i >= 2:
                                    sync.wait_ge(s_ln.h, pj_ap[i - 2])
                                sync.dma_start(
                                    xc2[:, i % 2, :],
                                    x_d[b * N + 128 * i:b * N + 128 * (i + 1), :]
                                ).then_inc(s_x.h, 16)
                                if i >= 1:
                                    sync.wait_ge(s_ln.h, pj_ap[i - 1])
                                    sync.dma_start(
                                        x1s[b * N + 128 * (i - 1):b * N + 128 * i, :],
                                        x1t[:, (i - 1) % 2, :]).then_inc(s_st.h, 16)
                            x_at = s_x.bump(16)
                            if i >= 1:
                                ST_X1[i - 1] = s_st.bump(16)
                            k.emit("sync", e_sync)

                            def e_t(t, i=i, gq0=gq0):
                                if i == 0:
                                    t.wait_ge(s_gq.h, gq0 + 128)
                                    t.wait_ge(s_ev.h, sv_done_v)
                                for dh in range(2):
                                    gI = 2 * i + dh
                                    if gI >= 2:
                                        t.wait_ge(s_ln.h, pj_ap.get(i - 1, 0))
                                    pX = pA if gI % 2 == 0 else pB
                                    for pr2 in range(8):
                                        mm = t.matmul(pX[:, :],
                                                      SVT[:, pr2, i * 128:(i + 1) * 128],
                                                      wprojb[:, pr2, dh * 512:(dh + 1) * 512],
                                                      start=(pr2 == 0), stop=(pr2 == 7))
                                    mm.then_inc(s_mm.h, 1)
                            pj_mm[i] = (s_mm.bump(), s_mm.bump())
                            k.emit("tensor", e_t)

                            ee1, ee2 = s_ln.bump(), s_ln.bump()
                            def e_v1(v, i=i, x_at=x_at, ee1=ee1, ee2=ee2):
                                v.wait_ge(s_mm.h, pj_mm[i][1])
                                v.bn_stats(stat2[:, 0:6], pA[:, :]).then_inc(s_ln.h, 1)
                                v.wait_ge(s_ln.h, ee1)
                                v.bn_stats(stat2[:, 6:12], pB[:, :]).then_inc(s_ln.h, 1)
                                v.wait_ge(s_ln.h, ee2)
                                v.bn_aggr(mv3[:, :], stat2[:, 0:12]).then_inc(s_ln.h, 1)
                            pst = s_ln.bump()
                            k.emit("vector", e_v1)

                            def e_sq(sc, i=i, pst=pst):
                                sc.wait_ge(s_ln.h, pst)
                                sc.activation(sjunk2[:, 0:1], mv3[:, 1:2], AF.Sqrt,
                                              bias=epsc[:, 0:1], scale=1.0).then_inc(s_sq.h, 1)
                            sqm = s_sq.bump()
                            k.emit("scalar", e_sq)

                            ee3 = s_ln.bump()
                            def e_v2(v, i=i, sqm=sqm, x_at=x_at, ee3=ee3):
                                v.wait_ge(s_sq.h, sqm)
                                v.wait_ge(s_x.h, x_at)
                                if i >= 2:
                                    v.wait_ge(s_st.h, ST_X1[i - 2])
                                v.reciprocal(rstd2[:, :], sjunk2[:, 0:1]).then_inc(s_ln.h, 1)
                                v.wait_ge(s_ln.h, ee3)
                                for dh in range(2):
                                    pX = pA if dh == 0 else pB
                                    v.tensor_scalar(x1t[:, i % 2, dh * 512:(dh + 1) * 512],
                                                    pX[:, :], mv3[:, 0:1], rstd2[:, :],
                                                    op0=ALU.subtract, op1=ALU.mult)
                                op = v.tensor_tensor(x1t[:, i % 2, :], x1t[:, i % 2, :],
                                                     xc2[:, i % 2, :], op=ALU.add)
                                op.then_inc(s_ln.h, 1)
                            pj_ap[i] = s_ln.bump()
                            k.emit("vector", e_v2)

                        def e_fin(sync, b=b):
                            sync.wait_ge(s_ln.h, pj_ap[15])
                            sync.dma_start(x1s[b * N + 128 * 15:b * N + 128 * 16, :],
                                           x1t[:, 15 % 2, :]).then_inc(s_st.h, 16)
                        s_st.bump(16)
                        k.emit("sync", e_fin)
                        k.run_block()

        if debug_stage in ("attn", "attn1"):
            k.new_block()
            st_target = s_st.v
            def fin(sync):
                sync.wait_ge(s_st.h, st_target)
            k.emit("sync", fin)
            k.run_block()
            return nc

        W1T = sb("W1T", [128, 8, FF], BF16)
        W2T = sb("W2T", [128, NFC, DM], BF16)
        with ExitStack() as rstk:
            rsb = lambda n, s, d: rstk.enter_context(nc.sbuf_tensor(k.uname(n), s, d))
            xc1 = rsb("xc1", [128, 2, DM], F32)
            zf = rsb("zf", [128, 2, DM], F32)
            rmch = rsb("rmch", [128, 2, 4], F32)
            znbt = rsb("znbt", [128, 2, DM], BF16)
            zT32 = rsb("zT32", [128, 2, 8, 128], F32)
            stat3 = rsb("stat3", [128, 16], F32)
            mvB = rsb("mvB", [128, 2, 2], F32)
            rstdB = rsb("rstdB", [128, 1], F32)
            sjk = rsb("sjk", [128, 4], F32)
            junkS = rsb("junkS", [128, DM], F32)
            scr = rsb("scr", [128, 8], F32)
            affs = rsb("affs", [128, 8], F32)
            max8 = rsb("max8", [128, 8], F32)
            idx8 = rsb("idx8", [128, 8], U32)
            esf = rsb("esf", [128, 1], F32)
            ohf = rsb("ohf", [128, 8], F32)
            ohsel = rsb("ohsel", [128, 8], F32)
            rkred = rsb("rkred", [128, 1], F32)
            of32 = rsb("of32", [128, 1], F32)
            ovr = rsb("ovr", [128, 1], F32)
            cnt8 = rsb("cnt8", [1, 8], F32)

            DBG = debug_stage == "dump"
            k.new_block()
            P2_ST_DONE = s_st.v
            _breg = {}

            def r_breg(g):
                _breg["r"] = g.alloc_register(name="sc_bound")
                g.reg_mov(_breg["r"], E * CAP - 1)
                _breg["rows"] = g.alloc_register(name="sc_bound2")
                g.reg_mov(_breg["rows"], ROWS - 1)
            k.emit("gpsimd", r_breg)
            if wprep_pos < len(wprep_units):
                k.emit("gpsimd", emit_wprep(wprep_units[wprep_pos:]))
                wprep_pos = len(wprep_units)

            W0_BASE = s_w.v
            WP_ALL = s_wp.v
            s_w.bump(16 * 40)
            wpre_dmas = ([("w1", d) for d in range(8)] +
                         [("w2", f) for f in range(NFC)])

            def emit_wpre_piece(lo, hi, first):
                def wp(sync, lo=lo, hi=hi, first=first):
                    if first:
                        sync.wait_ge(s_st.h, P2_ST_DONE)
                        sync.wait_ge(s_wp.h, WP_ALL)
                    for kind, i in wpre_dmas[lo:hi]:
                        if kind == "w1":
                            sync.dma_start(W1T[:, i, :],
                                           w1b[0, :, i * 128:(i + 1) * 128],
                                           transpose=True).then_inc(s_w.h, 16)
                        else:
                            sync.dma_start(W2T[:, i, :],
                                           w2b[0, :, i * 128:(i + 1) * 128],
                                           transpose=True).then_inc(s_w.h, 16)
                k.emit("sync", wp)

            LN1_MS = {}
            ST2_MS = {}
            SQ2_MS = {}
            AP2_MS = {}
            AFF_MS = {}
            AFF_CONS = {}
            RK_MS = {}
            r_tp = {}
            r_ev = {}
            r_sc = {}
            r_ch = {}
            rm_sq = {}

            def emit_front(c):
                def r_sync(sync, c=c):
                    if c >= 2:
                        sync.wait_ge(s_sq.h, LN1_MS[c - 2])
                    sync.dma_start(xc1[:, c % 2, :],
                                   x1s[128 * c:128 * (c + 1), :]).then_inc(s_x.h, 16)
                x_at = s_x.bump(16)
                k.emit("sync", r_sync)

                w1_, w2_ = s_sq.bump(), s_sq.bump()
                def r_ln1a(sc, c=c, x_at=x_at, w1_=w1_, w2_=w2_):
                    sc.wait_ge(s_x.h, x_at)
                    if c >= 2:
                        sc.wait_ge(s_tp.h, r_tp[c - 2])
                        sc.wait_ge(s_ln.h, AP2_MS[c - 2])
                    xi = xc1[:, c % 2, :]
                    sc.activation(junkS[:, :], xi, AF.Square,
                                  accum_out=scr[:, 0:1])
                    sc.activation(junkS[:, :], xi, AF.Copy,
                                  accum_out=scr[:, 1:2]).then_inc(s_sq.h, 1)
                    sc.wait_ge(s_sq.h, w1_)
                    sc.mul(scr[:, 2:3], scr[:, 1:2], 1.0 / DM)
                    sc.activation(scr[:, 3:4], scr[:, 2:3], AF.Square)
                    sc.mul(scr[:, 3:4], scr[:, 3:4], -1.0)
                    sc.mul(scr[:, 4:5], scr[:, 0:1], 1.0 / DM).then_inc(s_sq.h, 1)
                    sc.wait_ge(s_sq.h, w2_)
                    sc.add(scr[:, 4:5], scr[:, 4:5], scr[:, 3:4])
                    sc.activation(scr[:, 5:6], scr[:, 4:5], AF.Sqrt,
                                  bias=epsc[:, 0:1], scale=1.0)
                    sc.mul(scr[:, 0:1], scr[:, 2:3], -1.0).then_inc(s_sq.h, 1)
                LN1A_MS = s_sq.bump()
                k.emit("scalar", r_ln1a)

                def r_ln1v(v, c=c, LN1A_MS=LN1A_MS):
                    v.wait_ge(s_sq.h, LN1A_MS)
                    v.reciprocal(scr[:, 6:7], scr[:, 5:6]).then_inc(s_ln.h, 1)
                LN1V_MS = s_ln.bump()
                k.emit("vector", r_ln1v)

                def r_ln1b(sc, c=c, LN1V_MS=LN1V_MS):
                    sc.wait_ge(s_ln.h, LN1V_MS)
                    sc.add(junkS[:, :], xc1[:, c % 2, :], scr[:, 0:1])
                    sc.mul(zf[:, c % 2, :], junkS[:, :],
                           scr[:, 6:7]).then_inc(s_sq.h, 1)
                LN1_MS[c] = s_sq.bump()
                k.emit("scalar", r_ln1b)

            def emit_mid(c):
                e1, e2 = s_ln.bump(), s_ln.bump()
                def r_st2(v, c=c, e1=e1, e2=e2):
                    v.wait_ge(s_sq.h, LN1_MS[c])
                    if c >= 2:
                        v.wait_ge(s_sq.h, rm_sq[c - 2])
                    v.bn_stats(stat3[:, 0:6], zf[:, c % 2, 0:512]).then_inc(s_ln.h, 1)
                    v.wait_ge(s_ln.h, e1)
                    v.bn_stats(stat3[:, 6:12], zf[:, c % 2, 512:1024]).then_inc(s_ln.h, 1)
                    v.wait_ge(s_ln.h, e2)
                    v.bn_aggr(mvB[:, c % 2, :], stat3[:, 0:12]).then_inc(s_ln.h, 1)
                ST2_MS[c] = s_ln.bump()
                k.emit("vector", r_st2)

                def r_s2(sc, c=c):
                    sc.wait_ge(s_ln.h, ST2_MS[c])
                    sc.activation(sjk[:, 2 + (c % 2):3 + (c % 2)], mvB[:, c % 2, 1:2],
                                  AF.Sqrt, bias=epsc[:, 0:1],
                                  scale=1.0).then_inc(s_sq.h, 1)
                SQ2_MS[c] = s_sq.bump()
                k.emit("scalar", r_s2)

                er6 = s_ln.bump()
                def r_apl2(v, c=c, er6=er6):
                    v.wait_ge(s_sq.h, SQ2_MS[c])
                    if c >= 2:
                        v.wait_ge(s_sc.h, r_sc[c - 2])
                    v.reciprocal(rstdB[:, :],
                                 sjk[:, 2 + (c % 2):3 + (c % 2)]).then_inc(s_ln.h, 1)
                    v.wait_ge(s_ln.h, er6)
                    v.tensor_scalar(znbt[:, c % 2, :], zf[:, c % 2, :],
                                    mvB[:, c % 2, 0:1], rstdB[:, :],
                                    op0=ALU.subtract, op1=ALU.mult
                                    ).then_inc(s_ln.h, 1)
                AP2_MS[c] = s_ln.bump()
                k.emit("vector", r_apl2)

                def r_tr(t, c=c):
                    t.wait_ge(s_sq.h, LN1_MS[c])
                    if c >= 2:
                        t.wait_ge(s_ev.h, r_ev[c - 2])
                    bks = (pA, pB) if c % 2 == 0 else (pC, pD)
                    for d in range(8):
                        pX = bks[d // 4]
                        mm = t.transpose(pX[:, (d % 4) * 128:(d % 4 + 1) * 128],
                                         zf[:, c % 2, d * 128:(d + 1) * 128],
                                         id128f[:, :])
                    mm.then_inc(s_tp.h, 1)
                r_tp[c] = s_tp.bump()
                k.emit("tensor", r_tr)

                def r_tev(v, c=c):
                    v.wait_ge(s_tp.h, r_tp[c])
                    if c >= 2:
                        v.wait_ge(s_mm.h, AFF_MS[c - 2])
                    bks = (pA, pB) if c % 2 == 0 else (pC, pD)
                    for d in range(8):
                        pX = bks[d // 4]
                        mm = v.tensor_copy(zT32[:, c % 2, d, :],
                                           pX[:, (d % 4) * 128:(d % 4 + 1) * 128])
                    mm.then_inc(s_ev.h, 1)
                r_ev[c] = s_ev.bump()
                k.emit("vector", r_tev)

                def r_aff(t, c=c):
                    t.wait_ge(s_ev.h, r_ev[c])
                    if c >= 1:
                        t.wait_ge(s_ln.h, AFF_CONS[c - 1])
                    for d in range(8):
                        mm = t.matmul(pE_[:, 0:8], zT32[:, c % 2, d, :], CT[:, d, :],
                                      start=(d == 0), stop=(d == 7))
                    mm.then_inc(s_mm.h, 1)
                AFF_MS[c] = s_mm.bump()
                k.emit("tensor", r_aff)

            def emit_back(c):
                ea1, ea2, ea3, ea4 = (s_ln.bump(), s_ln.bump(), s_ln.bump(),
                                      s_ln.bump())
                def r_v4(v, c=c, ea1=ea1, ea2=ea2, ea3=ea3, ea4=ea4):
                    v.wait_ge(s_mm.h, AFF_MS[c])
                    if c >= 1:
                        v.wait_ge(s_sq.h, rm_sq[c - 1])
                    v.tensor_copy(affs[:, :], pE_[:, 0:8]).then_inc(s_ln.h, 1)
                    v.wait_ge(s_ln.h, ea1)
                    v.max(max8[:, :], affs[:, :]).then_inc(s_ln.h, 1)
                    v.wait_ge(s_ln.h, ea2)
                    v.max_index(idx8[:, :], max8[:, :], affs[:, :]).then_inc(s_ln.h, 1)
                    v.wait_ge(s_ln.h, ea3)
                    v.tensor_copy(esf[:, :], idx8[:, 0:1]).then_inc(s_ln.h, 1)
                    v.wait_ge(s_ln.h, ea4)
                    v.tensor_scalar(ohf[:, :], iota8[:, :], esf[:, :], None,
                                    op0=ALU.is_equal).then_inc(s_ln.h, 1)
                AFF_CONS[c] = s_ln.bump()
                k.emit("vector", r_v4)

                def r_rk(t, c=c):
                    t.wait_ge(s_ln.h, AFF_CONS[c])
                    if c >= 1:
                        t.wait_ge(s_rt.h, r_ch[c - 1])
                    t.matmul(pF_[:, 0:8], ltu[:, :], ohf[:, :], start=True, stop=False)
                    t.matmul(pF_[:, 0:8], ones1f[0:1, :], Rrow[0:1, :],
                             start=False, stop=True)
                    t.matmul(pF_[0:1, 8:16], ones128f[:, :], ohf[:, :],
                             start=True, stop=True).then_inc(s_mm.h, 1)
                RK_MS[c] = s_mm.bump()
                k.emit("tensor", r_rk)

                eb = [s_ln.bump() for _ in range(6)]
                def r_v5(v, c=c, eb=eb):
                    v.wait_ge(s_mm.h, RK_MS[c])
                    v.tensor_tensor(ohsel[:, :], pF_[:, 0:8], ohf[:, :],
                                    op=ALU.mult).then_inc(s_ln.h, 1)
                    v.wait_ge(s_ln.h, eb[0])
                    v.tensor_reduce(rkred[:, :], ohsel[:, :], axis=AX.X,
                                    op=ALU.add).then_inc(s_ln.h, 1)
                    v.wait_ge(s_ln.h, eb[1])
                    v.tensor_scalar(of32[:, :], esf[:, :], float(CAP), rkred[:, :],
                                    op0=ALU.mult, op1=ALU.add)
                    v.tensor_scalar(ovr[:, :], rkred[:, :], -(float(CAP) - 1.0), 0.0,
                                    op0=ALU.add, op1=ALU.max).then_inc(s_ln.h, 1)
                    v.wait_ge(s_ln.h, eb[2])
                    v.tensor_scalar(ovr[:, :], ovr[:, :], 1.0, None,
                                    op0=ALU.min).then_inc(s_ln.h, 1)
                    v.wait_ge(s_ln.h, eb[3])
                    v.scalar_tensor_tensor(of32[:, :], ovr[:, :], 1.0e9, of32[:, :],
                                           op0=ALU.mult, op1=ALU.add).then_inc(s_ln.h, 1)
                    v.wait_ge(s_ln.h, eb[4])
                    v.tensor_copy(o_all[:, c:c + 1], of32[:, :])
                    v.tensor_copy(cnt8[0:1, :], pF_[0:1, 8:16]).then_inc(s_ln.h, 1)
                    v.wait_ge(s_ln.h, eb[5])
                    if DBG:
                        v.tensor_copy(dbg_rk[:, 0, c:c + 1], rkred[:, :])
                        v.tensor_copy(dbg_rk[:, 1, c:c + 1], ovr[:, :])
                        v.tensor_copy(dbg_rk[:, 2, c:c + 1], of32[:, :])
                        v.tensor_copy(dbg_rk[:, 3, c:c + 1], esf[:, :])
                    v.tensor_tensor(Rrow[0:1, :], Rrow[0:1, :], cnt8[0:1, :],
                                    op=ALU.add).then_inc(s_rt.h, 1)
                r_ch[c] = s_rt.bump()
                k.emit("vector", r_v5)

                def r_rm(sc, c=c):
                    sc.wait_ge(s_ln.h, AFF_CONS[c])
                    sc.wait_ge(s_sq.h, SQ2_MS[c])
                    if c >= 2:
                        sc.wait_ge(s_sc.h, r_sc[c - 2])
                    sc.copy(rmch[:, c % 2, 0:1], riota[:, c:c + 1])
                    sc.copy(rmch[:, c % 2, 1:2], max8[:, 0:1])
                    sc.copy(rmch[:, c % 2, 2:3], mvB[:, c % 2, 0:1])
                    sc.copy(rmch[:, c % 2, 3:4],
                            sjk[:, 2 + (c % 2):3 + (c % 2)]).then_inc(s_sq.h, 1)
                rm_sq[c] = s_sq.bump()
                k.emit("scalar", r_rm)

                def r_scat(g, c=c):
                    if c == 0:
                        g.wait_ge(s_st.h, P2_ST_DONE)
                    g.wait_ge(s_ln.h, AP2_MS[c])
                    g.wait_ge(s_rt.h, r_ch[c])
                    g.indirect_dma_start(
                        out=znbkt, out_offset=IndirectOffsetOnAxis(o_all[:, c:c + 1], 0),
                        in_=znbt[:, c % 2, :], in_offset=None,
                        bounds_check=_breg["r"],
                        oob_is_err=False).then_inc(s_sc.h, 16)
                    g.wait_ge(s_sq.h, rm_sq[c])
                    g.indirect_dma_start(
                        out=rmtab, out_offset=IndirectOffsetOnAxis(o_all[:, c:c + 1], 0),
                        in_=rmch[:, c % 2, :], in_offset=None,
                        bounds_check=_breg["r"],
                        oob_is_err=False).then_inc(s_sc.h, 16)
                r_sc[c] = s_sc.bump(32)
                k.emit("gpsimd", r_scat)

            for c in range(NCH):
                if c % 4 == 0 and c // 4 < 8:
                    emit_wpre_piece(5 * (c // 4), 5 * (c // 4 + 1), c == 0)
                emit_front(c)
                if c >= 1:
                    emit_mid(c - 1)
                if c >= 2:
                    emit_back(c - 2)
            emit_mid(NCH - 1)
            emit_back(NCH - 2)
            emit_back(NCH - 1)

            en1 = s_ln.bump()
            def r_nse(v, en1=en1):
                v.wait_ge(s_rt.h, r_ch[NCH - 1])
                v.tensor_scalar(nse_f[0:1, :], Rrow[0:1, :], 255.0, 1.0 / 256.0,
                                op0=ALU.add, op1=ALU.mult).then_inc(s_ln.h, 1)
                v.wait_ge(s_ln.h, en1)
                v.tensor_copy(nse_i[0:1, :], nse_f[0:1, :]).then_inc(s_rt.h, 1)
            NSE_MS = s_rt.bump()
            k.emit("vector", r_nse)

            def r_gate(sync):
                sync.wait_ge(s_sc.h, r_sc[NCH - 1])
                sync.wait_ge(s_rt.h, NSE_MS)
            k.emit("sync", r_gate)
            k.run_block()

        with ExitStack() as fstk:
            fsb = lambda n, s, d: fstk.enter_context(nc.sbuf_tensor(k.uname(n), s, d))
            znl = fsb("znl", [128, 3, 2, DM], BF16)
            znTt = fsb("znTt", [128, 2, 8, 256], BF16)
            hTt = fsb("hTt", [128, 2, 256], BF16)
            t_sb = fsb("t_sb", [128, 2, 2, DM], BF16)
            b2row = fsb("b2row", [1, 2, DM], BF16)
            rmb = fsb("rmb", [128, 2, 2, 4], F32)
            idxu = fsb("idxu", [128, 4], U32)
            alph = fsb("alph", [128, 4], F32)
            xg = fsb("xg", [128, 2, DM], F32)
            fq4 = fsb("fq4", [128, 2, DM], F32)
            oq4 = fsb("oq4", [128, 2, DM], F32)
            stat5 = fsb("stat5", [128, 24], F32)
            mv5 = fsb("mv5", [128, 2, 2, 2], F32)
            sj5 = fsb("sj5", [128, 2, 2], F32)
            rstd5 = fsb("rstd5", [128, 2], F32)
            oobu = fsb("oobu", [128, 1], U32)
            PT4 = [(pA, pB), (pC, pD)]

            k.new_block()
            REGS = {}

            def mk_setup(eng_name):
                def setup(e):
                    REGS[eng_name] = (e.alloc_register(name="rns" + eng_name),
                                      e.alloc_register(name="rs" + eng_name),
                                      e.alloc_register(name="rnsE" + eng_name),
                                      e.alloc_register(name="rsE" + eng_name))
                return setup
            for en_ in ("sync", "tensor", "vector", "scalar", "gpsimd"):
                k.emit(en_, mk_setup(en_))

            def p4_oob(v):
                v.tensor_copy(oobu[:, :], rmsent[:, 0:1]).then_inc(s_ln.h, 1)
            OOB_MS = s_ln.bump()
            k.emit("vector", p4_oob)

            def p4_pro(g):
                g.wait_ge(s_st.h, P2_ST_DONE)
                g.wait_ge(s_ln.h, OOB_MS)
            k.emit("gpsimd", p4_pro)

            def cond(e_obj, eng_name, ex, s):
                rns, rs = REGS[eng_name][0], REGS[eng_name][1]
                if s == 0:
                    e_obj.reg_load(rns, nse_i[0:1, ex:ex + 1])
                e_obj.reg_mov(rs, s)
                return e_obj.If_lt(rs, rns)

            def condE(e_obj, eng_name, sgp):
                exP, sP = sgp // NSC, sgp % NSC
                rns, rs = REGS[eng_name][2], REGS[eng_name][3]
                e_obj.reg_load(rns, nse_i[0:1, exP:exP + 1])
                e_obj.reg_mov(rs, sP)
                return e_obj.If_lt(rs, rns)

            B2_AT = {}
            W_AT = {}
            ZN_AT = {}
            RM_AT = {}
            IDX_MS = {}
            ZREC_MS = {}
            FQ_MS = {}
            BN_MS = {}
            SQ5_MS = {}
            OQ_MS = {}
            ALPH_MS = {}
            GX_AT = {}
            SC_AT = {}
            KRELU = 11

            def emit_epi_a(sgp):
                def f_epi_a(v, sgp=sgp):
                    with condE(v, "vector", sgp):
                        v.wait_ge(s_x.h, RM_AT[sgp])
                        if sgp >= 1:
                            v.wait_ge(s_sc.h, SC_AT[sgp - 1])
                        for gi in range(2):
                            op = v.tensor_scalar(oq4[:, gi, :], znl[:, sgp % 3, gi, :],
                                                 rmb[:, sgp % 2, gi, 3:4],
                                                 rmb[:, sgp % 2, gi, 2:3],
                                                 op0=ALU.mult, op1=ALU.add)
                        op.then_inc(s_ln.h, 1)
                    with v.Else():
                        v.sem_inc(s_ln.h, 1)
                ZREC_MS[sgp] = s_ln.bump()
                k.emit("vector", f_epi_a)

            def emit_epi_b(sgp):
                def f_epi(v, sgp=sgp):
                    with condE(v, "vector", sgp):
                        v.wait_ge(s_te.h, sgp + 1)
                        v.wait_ge(s_sq.h, ALPH_MS[sgp])
                        for gi in range(2):
                            op = v.scalar_tensor_tensor(
                                fq4[:, gi, :], t_sb[:, sgp % 2, gi, :],
                                alph[:, 2 * (sgp % 2) + gi:2 * (sgp % 2) + gi + 1],
                                oq4[:, gi, :], op0=ALU.mult, op1=ALU.add)
                        op.then_inc(s_ln.h, 1)
                        v.wait_ge(s_ln.h, FQ_MS[sgp])
                        for gi in range(2):
                            v.bn_stats(stat5[:, 12 * gi:12 * gi + 6],
                                       fq4[:, gi, 0:512])
                            op = v.bn_stats(stat5[:, 12 * gi + 6:12 * gi + 12],
                                            fq4[:, gi, 512:1024])
                        op.then_inc(s_ln.h, 1)
                        v.wait_ge(s_ln.h, STAT_E[sgp])
                        v.bn_aggr(mv5[:, sgp % 2, 0, :], stat5[:, 0:12])
                        v.bn_aggr(mv5[:, sgp % 2, 1, :],
                                  stat5[:, 12:24]).then_inc(s_ln.h, 1)
                        v.wait_ge(s_sq.h, SQ5_MS[sgp])
                        v.wait_ge(s_ln.h, BN_MS[sgp])
                        v.reciprocal(rstd5[:, :],
                                     sj5[:, sgp % 2, :]).then_inc(s_ln.h, 1)
                        v.wait_ge(s_ln.h, RC_E[sgp])
                        v.wait_ge(s_sc.h, GX_AT[sgp])
                        for gi in range(2):
                            v.tensor_scalar(oq4[:, gi, :], fq4[:, gi, :],
                                            mv5[:, sgp % 2, gi, 0:1],
                                            rstd5[:, gi:gi + 1],
                                            op0=ALU.subtract, op1=ALU.mult)
                            op = v.tensor_tensor(oq4[:, gi, :], oq4[:, gi, :],
                                                 xg[:, gi, :], op=ALU.add)
                        op.then_inc(s_ln.h, 1)
                    with v.Else():
                        v.sem_inc(s_ln.h, 5)
                FQ_MS[sgp] = s_ln.bump()
                STAT_E[sgp] = s_ln.bump()
                BN_MS[sgp] = s_ln.bump()
                RC_E[sgp] = s_ln.bump()
                OQ_MS[sgp] = s_ln.bump()
                k.emit("vector", f_epi)

            def emit_sq5(sgp):
                def f_sq5(sc, sgp=sgp):
                    with condE(sc, "scalar", sgp):
                        sc.wait_ge(s_ln.h, BN_MS[sgp])
                        sc.activation(sj5[:, sgp % 2, :], mv5[:, sgp % 2, :, 1:2],
                                      AF.Sqrt, bias=epsc[:, 0:1],
                                      scale=1.0).then_inc(s_sq.h, 1)
                    with sc.Else():
                        sc.sem_inc(s_sq.h, 1)
                SQ5_MS[sgp] = s_sq.bump()
                k.emit("scalar", f_sq5)

            def emit_scat(sgp):
                def f_scat(g, sgp=sgp):
                    with condE(g, "gpsimd", sgp):
                        g.wait_ge(s_ln.h, OQ_MS[sgp])
                        for gi in range(2):
                            g.indirect_dma_start(
                                out=out_d, out_offset=IndirectOffsetOnAxis(
                                    idxu[:, 2 * (sgp % 2) + gi:
                                         2 * (sgp % 2) + gi + 1], 0),
                                in_=oq4[:, gi, :], in_offset=None,
                                bounds_check=_breg["rows"],
                                oob_is_err=False).then_inc(s_sc.h, 16)
                    with g.Else():
                        for gi in range(2):
                            g.indirect_dma_start(
                                out=out_d, out_offset=IndirectOffsetOnAxis(
                                    oobu[:, 0:1], 0),
                                in_=oq4[:, gi, :], in_offset=None,
                                bounds_check=_breg["rows"],
                                oob_is_err=False).then_inc(s_sc.h, 16)
                SC_AT[sgp] = s_sc.bump(32)
                k.emit("gpsimd", f_scat)

            STAT_E = {}
            RC_E = {}

            for ex in range(E):
                def f_b2(g, ex=ex):
                    if ex >= 2:
                        g.wait_ge(s_w2.h, NFC * NSC * (ex - 1))
                    g.dma_start(b2row[:, ex % 2, :],
                                b2_d[ex:ex + 1, :]).then_inc(s_gq.h, 16)
                B2_AT[ex] = s_gq.bump(16)
                k.emit("gpsimd", f_b2)

                def f_wload(sync, ex=ex):
                    if ex >= 1:
                        sync.wait_ge(s_w1.h, NFC * NSC * ex)
                        sync.wait_ge(s_w2.h, NFC * NSC * ex)
                    if ex >= 1:
                        sync.wait_ge(s_ln.h, ZREC_MS[NSC * ex - 1])
                    for s in range(2):
                        sg = NSC * ex + s
                        if sg >= 2:
                            sync.wait_ge(s_trg.h, 2 * (sg - 1))
                        for gi in range(2):
                            sync.dma_start(
                                znl[:, sg % 3, gi, :],
                                znbkt[ex * CAP + s * 256 + gi * 128:
                                      ex * CAP + s * 256 + gi * 128 + 128, :]
                            ).then_inc(s_zn.h, 16)
                    if ex >= 1:
                        for d in range(8):
                            sync.dma_start(W1T[:, d, :],
                                           w1b[ex, :, d * 128:(d + 1) * 128],
                                           transpose=True).then_inc(s_w.h, 16)
                        for f in range(NFC):
                            sync.dma_start(W2T[:, f, :],
                                           w2b[ex, :, f * 128:(f + 1) * 128],
                                           transpose=True).then_inc(s_w.h, 16)
                if ex == 0:
                    W_BASE = W0_BASE
                else:
                    s_w.bump(16 * 40)
                    W_BASE = s_w.v - 16 * 40
                ZN_AT[NSC * ex] = s_zn.bump(32)
                ZN_AT[NSC * ex + 1] = s_zn.bump(32)
                W_AT[ex] = W_BASE + 16 * 40
                if ex == 0:
                    W_BASES = {}
                W_BASES[ex] = W_BASE
                k.emit("sync", f_wload)

                def f_tpro(t, ex=ex):
                    t.wait_ge(s_w.h, W_BASES[ex] + 16 * 8)
                    for gi in range(2):
                        sg = NSC * ex
                        grp = 2 * sg + gi
                        pT = pTrb if grp % 2 == 0 else pTr2
                        t.wait_ge(s_zn.h, ZN_AT[sg])
                        if grp >= 2:
                            t.wait_ge(s_znT.h, grp - 1)
                        for d in range(8):
                            mm = t.transpose(pT[:, d, :],
                                             znl[:, sg % 3, gi, d * 128:(d + 1) * 128],
                                             id128[:, :])
                        mm.then_inc(s_trg.h, 1)
                s_trg.bump(2)
                k.emit("tensor", f_tpro)

                for s in range(NSC):
                    sg = NSC * ex + s

                    has_pre = s + 2 < NSC

                    def f_sync(sync, ex=ex, s=s, sg=sg, has_pre=has_pre):
                        with cond(sync, "sync", ex, s):
                            if sg >= 2:
                                sync.wait_ge(s_ln.h, ZREC_MS[sg - 2])
                            for gi in range(2):
                                sync.dma_start(
                                    rmb[:, sg % 2, gi, :],
                                    rmtab[ex * CAP + s * 256 + gi * 128:
                                          ex * CAP + s * 256 + gi * 128 + 128, :]
                                ).then_inc(s_x.h, 16)
                            if has_pre:
                                if sg >= 1:
                                    sync.wait_ge(s_trg.h, 2 * sg)
                                    sync.wait_ge(s_ln.h, ZREC_MS[sg - 1])
                                for gi in range(2):
                                    sync.dma_start(
                                        znl[:, (sg + 2) % 3, gi, :],
                                        znbkt[ex * CAP + (s + 2) * 256 + gi * 128:
                                              ex * CAP + (s + 2) * 256 + gi * 128 + 128, :]
                                    ).then_inc(s_zn.h, 16)
                        with sync.Else():
                            sync.sem_inc(s_x.h, 32)
                            if has_pre:
                                sync.sem_inc(s_zn.h, 32)
                    RM_AT[sg] = s_x.bump(32)
                    if has_pre:
                        ZN_AT[sg + 2] = s_zn.bump(32)
                    k.emit("sync", f_sync)

                    has_tr = s + 1 < NSC

                    def f_tensor(t, ex=ex, s=s, sg=sg, has_tr=has_tr):
                        def w2_group(kk):
                            ff = kk % NFC
                            t.wait_ge(s_he.h, kk + 1)
                            if s == 0:
                                t.wait_ge(s_w.h, W_BASES[ex] + 16 * (8 + ff + 1))
                            if ff == 0:
                                if sg >= 1:
                                    t.wait_ge(s_te.h, sg)
                                t.wait_ge(s_gq.h, B2_AT[ex])
                                for sg2 in range(2):
                                    for dh in range(2):
                                        t.matmul(PT4[sg2][dh][:, :], ones1r[:, :],
                                                 b2row[0:1, ex % 2, dh * 512:(dh + 1) * 512],
                                                 start=True, stop=False)
                            for sg2 in range(2):
                                for dh in range(2):
                                    mm = t.matmul(PT4[sg2][dh][:, :],
                                                  hTt[:, kk % 2, sg2 * 128:(sg2 + 1) * 128],
                                                  W2T[:, ff, dh * 512:(dh + 1) * 512],
                                                  start=False, stop=(ff == NFC - 1))
                            mm.then_inc(s_w2.h, 1)

                        def tr_group(sgp, gi):
                            grp = 2 * sgp + gi
                            pT = pTrb if grp % 2 == 0 else pTr2
                            t.wait_ge(s_zn.h, ZN_AT[sgp])
                            t.wait_ge(s_znT.h, grp - 1)
                            for d in range(8):
                                mm = t.transpose(pT[:, d, :],
                                                 znl[:, sgp % 3, gi, d * 128:(d + 1) * 128],
                                                 id128[:, :])
                            mm.then_inc(s_trg.h, 1)

                        with cond(t, "tensor", ex, s):
                            t.wait_ge(s_znT.h, 2 * (sg + 1))
                            for f in range(NFC):
                                kk = NFC * sg + f
                                pw = pE_ if kk % 2 == 0 else pF_
                                if kk >= 2:
                                    t.wait_ge(s_he.h, kk - 1)
                                for d in range(8):
                                    mm = t.matmul(pw[:, 0:256],
                                                  W1T[:, d, f * 128:(f + 1) * 128],
                                                  znTt[:, sg % 2, d, :],
                                                  start=(d == 0), stop=(d == 7))
                                mm.then_inc(s_w1.h, 1)
                                if has_tr and f == 26:
                                    tr_group(sg + 1, 0)
                                if has_tr and f == 28:
                                    tr_group(sg + 1, 1)
                                if f >= 1:
                                    w2_group(kk - 1)
                            w2_group(NFC * sg + NFC - 1)
                        with t.Else():
                            t.sem_inc(s_w1.h, NFC)
                            t.sem_inc(s_w2.h, NFC)
                            if has_tr:
                                t.sem_inc(s_trg.h, 2)
                    s_w1.bump(NFC)
                    s_w2.bump(NFC)
                    if has_tr:
                        s_trg.bump(2)
                    k.emit("tensor", f_tensor)

                    def f_vector(v, ex=ex, s=s, sg=sg):
                        with cond(v, "vector", ex, s):
                            if sg >= 2:
                                v.wait_ge(s_w1.h, NFC * (sg - 1))
                            for gi in range(2):
                                grp = 2 * sg + gi
                                pT = pTrb if grp % 2 == 0 else pTr2
                                v.wait_ge(s_trg.h, grp + 1)
                                for d in range(8):
                                    op = v.tensor_scalar(
                                        znTt[:, sg % 2, d, gi * 128:(gi + 1) * 128],
                                        pT[:, d, :], GT[:, d, ex:ex + 1],
                                        BT[:, d, ex:ex + 1],
                                        op0=ALU.mult, op1=ALU.add)
                                op.then_inc(s_znT.h, 1)
                            v.wait_ge(s_x.h, RM_AT[sg])
                            if sg >= 2:
                                v.wait_ge(s_sc.h, SC_AT[sg - 2])
                            v.tensor_copy(idxu[:, 2 * (sg % 2):2 * (sg % 2) + 1],
                                          rmb[:, sg % 2, 0, 0:1])
                            v.tensor_copy(idxu[:, 2 * (sg % 2) + 1:2 * (sg % 2) + 2],
                                          rmb[:, sg % 2, 1, 0:1]).then_inc(s_ln.h, 1)
                        with v.Else():
                            v.sem_inc(s_znT.h, 2)
                            v.sem_inc(s_ln.h, 1)
                    if sg >= 1:
                        emit_epi_a(sg - 1)
                    s_znT.bump(2)
                    IDX_MS[sg] = s_ln.bump()
                    k.emit("vector", f_vector)

                    if sg >= 1:
                        emit_epi_b(sg - 1)

                    def f_scalar_a(sc, ex=ex, s=s, sg=sg):
                        with cond(sc, "scalar", ex, s):
                            sc.wait_ge(s_x.h, RM_AT[sg])
                            sc.activation(alph[:, 2 * (sg % 2):2 * (sg % 2) + 2],
                                          rmb[:, sg % 2, :, 1:2],
                                          AF.Sigmoid).then_inc(s_sq.h, 1)
                            for f in range(KRELU):
                                kk = NFC * sg + f
                                pw = pE_ if kk % 2 == 0 else pF_
                                sc.wait_ge(s_w1.h, kk + 1)
                                if kk >= 2:
                                    sc.wait_ge(s_w2.h, kk - 1)
                                sc.activation(hTt[:, kk % 2, :], pw[:, 0:256], AF.Relu,
                                              bias=B1T[:, f, ex:ex + 1],
                                              scale=1.0).then_inc(s_he.h, 1)
                        with sc.Else():
                            sc.sem_inc(s_sq.h, 1)
                            sc.sem_inc(s_he.h, KRELU)
                    ALPH_MS[sg] = s_sq.bump()
                    s_he.bump(KRELU)
                    k.emit("scalar", f_scalar_a)

                    if sg >= 1:
                        emit_sq5(sg - 1)

                    def f_scalar_b(sc, ex=ex, s=s, sg=sg):
                        with cond(sc, "scalar", ex, s):
                            for f in range(KRELU, NFC):
                                kk = NFC * sg + f
                                pw = pE_ if kk % 2 == 0 else pF_
                                sc.wait_ge(s_w1.h, kk + 1)
                                sc.wait_ge(s_w2.h, kk - 1)
                                sc.activation(hTt[:, kk % 2, :], pw[:, 0:256], AF.Relu,
                                              bias=B1T[:, f, ex:ex + 1],
                                              scale=1.0).then_inc(s_he.h, 1)
                            sc.wait_ge(s_w2.h, NFC * (sg + 1))
                            if sg >= 2:
                                sc.wait_ge(s_ln.h, FQ_MS[sg - 2])
                            for sg2 in range(2):
                                for dh in range(2):
                                    op = sc.copy(t_sb[:, sg % 2, sg2, dh * 512:(dh + 1) * 512],
                                                 PT4[sg2][dh][:, :])
                            op.then_inc(s_te.h, 1)
                        with sc.Else():
                            sc.sem_inc(s_he.h, NFC - KRELU)
                            sc.sem_inc(s_te.h, 1)
                    s_he.bump(NFC - KRELU)
                    s_te.bump(1)
                    k.emit("scalar", f_scalar_b)

                    def f_gp(g, ex=ex, s=s, sg=sg):
                        with cond(g, "gpsimd", ex, s):
                            g.wait_ge(s_ln.h, IDX_MS[sg])
                            if sg >= 1:
                                g.wait_ge(s_ln.h, OQ_MS[sg - 1])
                            for gi in range(2):
                                g.indirect_dma_start(
                                    out=xg[:, gi, :], out_offset=None, in_=x1s,
                                    in_offset=IndirectOffsetOnAxis(
                                        idxu[:, 2 * (sg % 2) + gi:
                                             2 * (sg % 2) + gi + 1], 0),
                                    bounds_check=_breg["rows"],
                                    oob_is_err=False).then_inc(s_sc.h, 16)
                        with g.Else():
                            if sg >= 1:
                                g.wait_ge(s_ln.h, OQ_MS[sg - 1])
                            for gi in range(2):
                                g.indirect_dma_start(
                                    out=xg[:, gi, :], out_offset=None, in_=x1s,
                                    in_offset=IndirectOffsetOnAxis(oobu[:, 0:1], 0),
                                    bounds_check=_breg["rows"],
                                    oob_is_err=False).then_inc(s_sc.h, 16)
                    GX_AT[sg] = s_sc.bump(32)
                    k.emit("gpsimd", f_gp)

                    if sg >= 1:
                        emit_scat(sg - 1)

            SGL = NSC * E - 1
            emit_epi_a(SGL)
            emit_epi_b(SGL)
            emit_sq5(SGL)
            emit_scat(SGL)
            if debug_stage == "epi":
                fin_sc2 = s_sc.v
                def dmp_epi(sync, fin_sc2=fin_sc2):
                    sync.wait_ge(s_sc.h, fin_sc2)
                    sync.dma_start(d_oq, oq4[:, :, :]).then_inc(s_x.h, 16)
                    sync.dma_start(d_fq, fq4[:, :, :]).then_inc(s_x.h, 16)
                    sync.dma_start(d_xg, xg[:, :, :]).then_inc(s_x.h, 16)
                    sync.dma_start(d_ts, t_sb[:, :, :, :]).then_inc(s_x.h, 16)
                    sync.dma_start(d_rm2, rmb[:, :, :, :]).then_inc(s_x.h, 16)
                    sync.dma_start(d_al2, alph[:, :]).then_inc(s_x.h, 16)
                    sync.dma_start(d_ix, idxu[:, :]).then_inc(s_x.h, 16)
                    sync.dma_start(d_mv, mv5[:, :, :, :]).then_inc(s_x.h, 16)
                    sync.dma_start(d_zl, znl[:, :, :, :]).then_inc(s_x.h, 16)
                DMP_AT = s_x.bump(16 * 9)
                def dmp_fin(sync, DMP_AT=DMP_AT):
                    sync.wait_ge(s_x.h, DMP_AT)
                k.emit("sync", dmp_epi)
                k.emit("sync", dmp_fin)
            k.run_block()

        k.new_block()
        fin_sc = s_sc.v
        fin_st = s_st.v
        def q_fin(sync):
            sync.wait_ge(s_sc.h, fin_sc)
            sync.wait_ge(s_st.h, fin_st)
        k.emit("sync", q_fin)
        k.run_block()

        if debug_stage == "dump":
            with nc.sbuf_tensor("dmpbuf", [128, 2, DM], BF16) as dmpbuf:
                k.new_block()
                base = s_x.v

                def dmp(sync, base=base):
                    n = 0
                    for cc2 in range(E * CAP // 128):
                        if n >= 2:
                            sync.wait_ge(s_x.h, base + 16 * (2 * (n - 1)))
                        sync.dma_start(dmpbuf[:, n % 2, :],
                                       znbkt[128 * cc2:128 * (cc2 + 1), :]
                                       ).then_inc(s_x.h, 16)
                        sync.wait_ge(s_x.h, base + 32 * n + 16)
                        sync.dma_start(d_zn[128 * cc2:128 * (cc2 + 1), :],
                                       dmpbuf[:, n % 2, :]).then_inc(s_x.h, 16)
                        n += 1
                    sync.dma_start(d_o[:, 0, :], o_all[:, :]).then_inc(s_x.h, 16)
                    sync.dma_start(d_o[:, 1, :], o_all[:, :]).then_inc(s_x.h, 16)
                    sync.dma_start(d_rk[:, :, :], dbg_rk[:, :, :]).then_inc(s_x.h, 16)
                    sync.wait_ge(s_x.h, base + 32 * n + 48)
                s_x.bump(32 * (E * CAP // 128) + 48)
                k.emit("sync", dmp)
                k.run_block()

    return nc


def kernel(**inputs):
    x = np.ascontiguousarray(inputs["L0_attn_x"], np.float32)
    nc = build_program()
    in_maps = []
    for c in range(8):
        shard = {
            "x": np.ascontiguousarray(x[4 * c:4 * (c + 1)].reshape(ROWS, DM)),
            "wqkv": np.ascontiguousarray(inputs["L0_attn_w_qkv"], np.float32),
            "wproj": np.ascontiguousarray(inputs["L0_attn_w_proj"], np.float32),
            "cent": np.ascontiguousarray(inputs["L0_ffn_centroids"], np.float32),
            "lng": np.ascontiguousarray(inputs["L0_ffn_ln_g"], np.float32),
            "lnb": np.ascontiguousarray(inputs["L0_ffn_ln_b"], np.float32),
            "w1": np.ascontiguousarray(inputs["L0_ffn_w1"], np.float32),
            "b1": np.ascontiguousarray(inputs["L0_ffn_b1"], np.float32),
            "w2": np.ascontiguousarray(inputs["L0_ffn_w2"], np.float32),
            "b2": np.ascontiguousarray(inputs["L0_ffn_b2"], np.float32),
        }
        in_maps.append(shard)
    res = run_bass_kernel_spmd(nc, in_maps, core_ids=list(range(8)))
    out = np.concatenate([r["out"].reshape(4, N, DM) for r in res.results], axis=0)
    return out
